# version 10
# baseline (speedup 1.0000x reference)
import contextlib
import numpy as np

import concourse.bass as bass
import concourse.mybir as mybir
from concourse.bass_utils import run_bass_kernel_spmd

F32 = mybir.dt.float32
BF16 = mybir.dt.bfloat16
I32 = mybir.dt.int32
AF = mybir.ActivationFunctionType
ALU = mybir.AluOpType
AX = mybir.AxisListType

FULL_CFG = dict(D=2048, S=64, TSEQ=4096, TOWN=2048, H=5632, NSW=2, EPS=1e-5)


def _cnt(K, j, NB):
    return (K - j + NB - 1) // NB


def build_nc(cfg, dbg=False):
    D = cfg["D"]; S = cfg["S"]; TSEQ = cfg["TSEQ"]; TOWN = cfg["TOWN"]
    H = cfg["H"]; NSW = cfg["NSW"]; EPS = cfg["EPS"]
    DC = D // 128
    TQ = TSEQ // 128
    TC = TOWN // 128
    HC = H // 128
    TSL = TOWN // NSW
    TCS = TSL // 128
    S2 = 2 * S
    QTB = min(512, TSEQ // 2)
    NQ = TSEQ // QTB
    RPQ = QTB // 128
    PJB = min(512, QTB)
    NPJQ = QTB // PJB
    NPJ = NQ * NPJQ
    HB = min(512, H)
    NHB = H // HB
    HCB = HB // 128
    DSL = min(512, D)
    NDS = D // DSL
    T5 = min(512, TSL)
    NJ5 = TSL // T5
    JW = min(512, D)
    NJW = D // JW
    FSTB = 4 if TQ % 4 == 0 else 1
    OSTB = 2 if TC % 2 == 0 else 1
    NXB = 5
    NOB = 3
    NWD = 4
    NPS = 4
    IVD = 1.0 / D

    nc = bass.Bass("TRN2", target_bir_lowering=False, detect_race_conditions=False)

    x_seq = nc.dram_tensor("x_seq", [TSEQ, D], F32, kind="ExternalInput").ap()
    x_own = nc.dram_tensor("x_own", [TOWN, D], F32, kind="ExternalInput").ap()
    sel0 = nc.dram_tensor("sel0", [128, 1], F32, kind="ExternalInput").ap()
    sel1 = nc.dram_tensor("sel1", [128, 1], F32, kind="ExternalInput").ap()
    wab_t = nc.dram_tensor("wab_t", [D, S2], F32, kind="ExternalInput").ap()
    ab_bias = nc.dram_tensor("ab_bias", [S, 2], F32, kind="ExternalInput").ap()
    wout_t = nc.dram_tensor("wout_t", [S, D], F32, kind="ExternalInput").ap()
    bout_row = nc.dram_tensor("bout_row", [1, D], F32, kind="ExternalInput").ap()
    wqg = nc.dram_tensor("wqg", [H, D], I32, kind="ExternalInput").ap()
    wqu = nc.dram_tensor("wqu", [H, D], I32, kind="ExternalInput").ap()
    wqd_t = nc.dram_tensor("wqd_t", [H, D], I32, kind="ExternalInput").ap()
    sg_t = nc.dram_tensor("sg_t", [128, HC], F32, kind="ExternalInput").ap()
    su_t = nc.dram_tensor("su_t", [128, HC], F32, kind="ExternalInput").ap()
    sd_row = nc.dram_tensor("sd_row", [1, D], F32, kind="ExternalInput").ap()
    ones_r = nc.dram_tensor("ones_r", [1, 128], F32, kind="ExternalInput").ap()

    y_out = nc.dram_tensor("y_out", [TOWN, D], F32, kind="ExternalOutput").ap()

    KI = "ExternalOutput" if dbg else "Internal"
    wg_deq = nc.dram_tensor("wg_deq", [H, D], BF16, kind=KI).ap()
    wu_deq = nc.dram_tensor("wu_deq", [H, D], BF16, kind=KI).ap()
    wdT_deq = nc.dram_tensor("wdT_deq", [H, D], BF16, kind=KI).ap()
    xn1b_h = nc.dram_tensor("xn1b_h", [TSEQ, D], BF16, kind=KI).ap()
    xn2b_h = nc.dram_tensor("xn2b_h", [TOWN, D], BF16, kind=KI).ap()
    x2_h = nc.dram_tensor("x2_h", [TOWN, D], F32, kind=KI).ap()
    prod_h = nc.dram_tensor("prod_h", [H, TSL], BF16, kind=KI).ap()
    dbg_st = nc.dram_tensor("dbg_st", [6 * 128, TQ], F32, kind=KI).ap() if dbg else None

    G = contextlib.ExitStack()
    g = G.enter_context
    with G:
        b_wabT = g(nc.sbuf_tensor("b_wabT", [128, DC * S2], BF16))
        b_woT = g(nc.sbuf_tensor("b_woT", [S, D], BF16))
        b_abbias = g(nc.sbuf_tensor("b_abbias", [S, 2], F32))
        b_bout_b = g(nc.sbuf_tensor("b_bout_b", [1, D], BF16))
        b_ones_b = g(nc.sbuf_tensor("b_ones_b", [1, 128], BF16))
        b_sel0 = g(nc.sbuf_tensor("b_sel0", [128, 1], F32))
        b_sel1 = g(nc.sbuf_tensor("b_sel1", [128, 1], F32))
        b_sg = g(nc.sbuf_tensor("b_sg", [128, HC], F32))
        b_su = g(nc.sbuf_tensor("b_su", [128, HC], F32))
        b_sgn = g(nc.sbuf_tensor("b_sgn", [128, HC], F32))
        b_sun = g(nc.sbuf_tensor("b_sun", [128, HC], F32))
        b_sdbc = g(nc.sbuf_tensor("b_sdbc", [128, D], BF16))
        b_onesF = g(nc.sbuf_tensor("b_onesF", [1, 128], F32))
        b_zero = g(nc.sbuf_tensor("b_zero", [S, 1], F32))
        b_sum1 = g(nc.sbuf_tensor("b_sum1", [128, TQ], F32))
        b_ssq1 = g(nc.sbuf_tensor("b_ssq1", [128, TQ], F32))
        b_m1 = g(nc.sbuf_tensor("b_m1", [128, TQ], F32))
        b_r1 = g(nc.sbuf_tensor("b_r1", [128, TQ], F32))
        b_mr1 = g(nc.sbuf_tensor("b_mr1", [128, TQ], F32))
        b_tq1 = g(nc.sbuf_tensor("b_tq1", [128, TQ], F32))
        b_tq2 = g(nc.sbuf_tensor("b_tq2", [128, TQ], F32))
        b_r1o = g(nc.sbuf_tensor("b_r1o", [128, TC], F32))
        b_mr1o = g(nc.sbuf_tensor("b_mr1o", [128, TC], F32))
        b_sum2 = g(nc.sbuf_tensor("b_sum2", [128, TC], F32))
        b_ssq2 = g(nc.sbuf_tensor("b_ssq2", [128, TC], F32))
        b_m2 = g(nc.sbuf_tensor("b_m2", [128, TC], F32))
        b_r2 = g(nc.sbuf_tensor("b_r2", [128, TC], F32))
        b_mr2 = g(nc.sbuf_tensor("b_mr2", [128, TC], F32))
        b_sum3 = g(nc.sbuf_tensor("b_sum3", [128, TC], F32))
        b_ssq3 = g(nc.sbuf_tensor("b_ssq3", [128, TC], F32))
        b_m3 = g(nc.sbuf_tensor("b_m3", [128, TC], F32))
        b_r3 = g(nc.sbuf_tensor("b_r3", [128, TC], F32))
        b_mr3 = g(nc.sbuf_tensor("b_mr3", [128, TC], F32))
        b_sum4 = g(nc.sbuf_tensor("b_sum4", [128, TC], F32))
        b_ssq4 = g(nc.sbuf_tensor("b_ssq4", [128, TC], F32))
        b_m4 = g(nc.sbuf_tensor("b_m4", [128, TC], F32))
        b_r4 = g(nc.sbuf_tensor("b_r4", [128, TC], F32))
        b_mr4 = g(nc.sbuf_tensor("b_mr4", [128, TC], F32))
        b_tc1 = g(nc.sbuf_tensor("b_tc1", [128, TC], F32))
        b_tc2 = g(nc.sbuf_tensor("b_tc2", [128, TC], F32))
        b_sqs = g(nc.sbuf_tensor("b_sqs", [128, D], BF16))
        b_one = g(nc.sbuf_tensor("b_one", [128, max(TQ, TC)], F32))
        b_nhalf = g(nc.sbuf_tensor("b_nhalf", [128, 1], F32))
        b_hsb = g(nc.sbuf_tensor("b_hsb", [S, TOWN], BF16))
        b_wst = [g(nc.sbuf_tensor(f"b_wst{i}", [128, D], BF16)) for i in range(2)]
        b_wdq = [g(nc.sbuf_tensor(f"b_wdq{i}", [128, D], BF16)) for i in range(2)]
        b_dstT = [g(nc.sbuf_tensor(f"b_dstT{i}", [128, D], BF16)) for i in range(2)]
        b_wdqd = [g(nc.sbuf_tensor(f"b_wdqd{i}", [128, D], BF16)) for i in range(1)]
        pall = g(nc.psum_tensor("pall", [128, 8 * 512], F32))

        def S1_(name):
            return G.enter_context(nc.semaphore(name))

        def SN_(name, n):
            return [G.enter_context(nc.semaphore(f"{name}{i}")) for i in range(n)]

        s_ppc = S1_("s_ppc"); s_red = S1_("s_red"); s_ssq = S1_("s_ssq")
        s_fa = S1_("s_fa"); s_fb = S1_("s_fb"); s_ap = S1_("s_ap")
        s_abp = S1_("s_abp"); s_th = S1_("s_th"); s_hs = S1_("s_hs")
        s_wps = S1_("s_wps"); s_y1 = S1_("s_y1")
        s_sq2 = S1_("s_sq2"); s_x2 = S1_("s_x2")
        s_sq3 = S1_("s_sq3"); s_xn2 = S1_("s_xn2")
        s_fa2 = S1_("s_fa2"); s_fb2 = S1_("s_fb2")
        s_wdqe = S1_("s_wdqe"); s_dq = S1_("s_dq")
        s_gmm = S1_("s_gmm"); s_gev = S1_("s_gev"); s_pr = S1_("s_pr")
        s_dmm = S1_("s_dmm"); s_dev = S1_("s_dev")
        s_sq4 = S1_("s_sq4"); s_fa4 = S1_("s_fa4"); s_fb4 = S1_("s_fb4")
        s_yo = S1_("s_yo"); s_sdp = S1_("s_sdp")
        s_f1v = S1_("s_f1v"); s_r3v = S1_("s_r3v")
        s_dvf = S1_("s_dvf"); s_acf = S1_("s_acf")
        dvf_n = [0]
        acf_n = [0]

        def dfence(vec, instr):
            instr.then_inc(s_dvf, 1)
            dvf_n[0] += 1
            vec.wait_ge(s_dvf, dvf_n[0])

        def afence(scal, instr):
            instr.then_inc(s_acf, 1)
            acf_n[0] += 1
            scal.wait_ge(s_acf, acf_n[0])
        s_pp = S1_("s_pp")
        s_xld = SN_("s_xld", NXB)
        s_x1s = SN_("s_x1s", 2)
        s_xt = S1_("s_xt")
        s_xo = SN_("s_xo", NOB)
        s_x2s = SN_("s_x2s", 3)
        s_n2s = SN_("s_n2s", 2)
        s_wcl = SN_("s_wcl", 2)
        s_wgs = S1_("s_wgs"); s_wus = S1_("s_wus")
        s_dcl = SN_("s_dcl", 2)
        s_wds = SN_("s_wds", 2)
        s_n2t = S1_("s_n2t")
        s_gwg = S1_("s_gwg"); s_gwu = S1_("s_gwu")
        s_prs = SN_("s_prs", 2)
        s_psl = SN_("s_psl", NPS)
        s_dw = SN_("s_dw", NWD)
        s_x2r = SN_("s_x2r", 2)
        s_ys = SN_("s_ys", 2)

        block = g(nc.Block())

        def fin_A(vec, sums, ssqs, m_t, tmpa, tmpb, sl):
            dfence(vec, vec.tensor_scalar(out=m_t[:, sl], in0=sums[:, sl], scalar1=IVD,
                                          scalar2=0.0, op0=ALU.mult, op1=ALU.subtract))
            dfence(vec, vec.tensor_tensor(out=tmpa[:, sl], in0=m_t[:, sl], in1=m_t[:, sl],
                                          op=ALU.mult))
            dfence(vec, vec.tensor_scalar(out=tmpa[:, sl], in0=tmpa[:, sl], scalar1=1.0,
                                          scalar2=EPS, op0=ALU.mult, op1=ALU.subtract))
            return vec.scalar_tensor_tensor(out=tmpb[:, sl], in0=ssqs[:, sl], scalar=IVD,
                                            in1=tmpa[:, sl], op0=ALU.mult, op1=ALU.subtract)

        def fin_act(scalar_, tmpb, r_t, sl):
            afence(scalar_, scalar_.activation(out=tmpb[:, sl], in_=tmpb[:, sl], func=AF.Ln))
            return scalar_.activation(out=r_t[:, sl], in_=tmpb[:, sl], func=AF.Exp,
                                      scale=b_nhalf[:, :])

        def fin_B(vec, tmpb, m_t, r_t, mr_t, sl):
            return vec.tensor_tensor(out=mr_t[:, sl], in0=m_t[:, sl], in1=r_t[:, sl], op=ALU.mult)

        EAGER = min(16, 2 * HC)
        cdq_i = [0]; cst_i = [0]; cwd_i = [0]

        def emit_cache_pair(scalar_, k=1):
            for _ in range(k):
                i = cdq_i[0]
                if i < 2 * HC:
                    hc = i // 2
                    is_g = (i % 2 == 0)
                    scalar_.wait_ge(s_wcl[i % 2], 16 * (i // 2 + 1))
                    scalar_.wait_ge(s_ppc, 10)
                    if i >= 2:
                        scalar_.wait_ge(s_wgs if is_g else s_wus, 16 * hc)
                    sc = b_sg if is_g else b_su
                    scn = b_sgn if is_g else b_sun
                    scalar_.activation(out=b_wdq[i % 2][:, :], in_=b_wst[i % 2][:, :],
                                       func=AF.Identity, bias=scn[:, hc:hc + 1],
                                       scale=sc[:, hc:hc + 1]).then_inc(s_wdqe, 1)
                    cdq_i[0] += 1
                j = cst_i[0]
                if j < cdq_i[0]:
                    hc = j // 2
                    mat = wg_deq if j % 2 == 0 else wu_deq
                    sm = s_wgs if j % 2 == 0 else s_wus
                    scalar_.wait_ge(s_wdqe, j + 1)
                    scalar_.dma_start(out=mat[hc * 128:(hc + 1) * 128, :],
                                      in_=b_wdq[j % 2][:, :]).then_inc(sm, 16)
                    cst_i[0] += 1

        cwdq_i = [0]
        cwds_i = [0]

        def emit_wd_dequant(vec, k=1):
            for _ in range(k):
                j = cwdq_i[0]
                if j >= HC:
                    return
                vec.wait_ge(s_dcl[j % 2], 16 * (j // 2 + 1))
                vec.wait_ge(s_ppc, 10)
                if j >= 1:
                    vec.wait_ge(s_wds[(j - 1) % 2], 16 * ((j - 1) // 2 + 1))
                vec.scalar_tensor_tensor(out=b_wdqd[0][:, :], in0=b_dstT[j % 2][:, :],
                                         scalar=128.0, in1=b_sdbc[:, :],
                                         op0=ALU.subtract, op1=ALU.mult).then_inc(s_dq, 1)
                cwdq_i[0] += 1

        def emit_wd_store(scalar_, k=1):
            for _ in range(k):
                j = cwds_i[0]
                if j >= HC:
                    return
                scalar_.wait_ge(s_dq, j + 1)
                scalar_.dma_start(out=wdT_deq[j * 128:(j + 1) * 128, :],
                                  in_=b_wdqd[0][:, :]).then_inc(s_wds[j % 2], 16)
                cwds_i[0] += 1

        A = contextlib.ExitStack()
        a = A.enter_context
        with A:
            P0 = contextlib.ExitStack()
            p0 = P0.enter_context
            b_wabF = p0(nc.sbuf_tensor("b_wabF", [128, DC * S2], F32))
            b_woF = p0(nc.sbuf_tensor("b_woF", [S, D], F32))
            b_boutF = p0(nc.sbuf_tensor("b_boutF", [1, D], F32))
            b_sdrF = p0(nc.sbuf_tensor("b_sdrF", [1, D], F32))
            P0.close()
            b_x = [a(nc.sbuf_tensor(f"b_x{i}", [128, D], F32)) for i in range(NXB)]
            b_xn1 = [a(nc.sbuf_tensor(f"b_xn1{i}", [128, D], BF16)) for i in range(2)]
            b_xt = [a(nc.sbuf_tensor(f"b_xt{i}", [128, QTB], BF16)) for i in range(2 * DC)]
            b_ab = a(nc.sbuf_tensor("b_ab", [S, 2 * TSEQ], F32))
            b_hs = a(nc.sbuf_tensor("b_hs", [S, TSEQ], F32))

            @block.sync
            def _(sync):
                for dst, srcap in [(b_wabF[:, :].rearrange("p (c s) -> p c s", s=S2),
                                    wab_t.rearrange("(c p) s -> p c s", p=128)),
                                   (b_abbias[:, :], ab_bias),
                                   (b_woF[:, :], wout_t),
                                   (b_boutF[:, :], bout_row), (b_onesF[:, :], ones_r),
                                   (b_sel0[:, :], sel0), (b_sel1[:, :], sel1),
                                   (b_sg[:, :], sg_t), (b_su[:, :], su_t),
                                   (b_sdrF[:, :], sd_row)]:
                    sync.dma_start(out=dst, in_=srcap).then_inc(s_pp, 16)
                LAG = FSTB + 1
                sync.wait_ge(s_ppc, 10)
                sync.wait_ge(s_sdp, 1)

                def xt_round(q):
                    K = RPQ * (q + 1)
                    sync.wait_ge(s_x1s[0], 16 * _cnt(K, 0, 2))
                    sync.wait_ge(s_x1s[1], 16 * _cnt(K, 1, 2))
                    if q >= 2:
                        sync.wait_ge(s_abp, NPJQ * (q - 1))
                    for dc in range(DC):
                        sync.dma_start(out=b_xt[(q % 2) * DC + dc][:, :],
                                       in_=xn1b_h[q * QTB:(q + 1) * QTB, dc * 128:(dc + 1) * 128],
                                       transpose=True).then_inc(s_xt, 16)

                nxt_q = [0]
                for tc in range(TQ):
                    if tc >= NXB:
                        sync.wait_ge(s_ap, tc - NXB + 1)
                        sync.wait_ge(s_red, tc - NXB + 1)
                        sync.wait_ge(s_ssq, tc - NXB + 1)
                    sync.dma_start(out=b_x[tc % NXB][:, :],
                                   in_=x_seq[tc * 128:(tc + 1) * 128, :]).then_inc(s_xld[tc % NXB], 16)
                    if tc >= LAG:
                        st_c = tc - LAG
                        sync.wait_ge(s_ap, st_c + 1)
                        sync.dma_start(out=xn1b_h[st_c * 128:(st_c + 1) * 128, :],
                                       in_=b_xn1[st_c % 2][:, :]).then_inc(s_x1s[st_c % 2], 16)
                        if st_c + 1 >= RPQ * (nxt_q[0] + 1) and nxt_q[0] < NQ:
                            xt_round(nxt_q[0])
                            nxt_q[0] += 1
                for st_c in range(max(0, TQ - LAG), TQ):
                    sync.wait_ge(s_ap, st_c + 1)
                    sync.dma_start(out=xn1b_h[st_c * 128:(st_c + 1) * 128, :],
                                   in_=b_xn1[st_c % 2][:, :]).then_inc(s_x1s[st_c % 2], 16)
                    if st_c + 1 >= RPQ * (nxt_q[0] + 1) and nxt_q[0] < NQ:
                        xt_round(nxt_q[0])
                        nxt_q[0] += 1
                while nxt_q[0] < NQ:
                    xt_round(nxt_q[0])
                    nxt_q[0] += 1

            @block.gpsimd
            def _(gpsimd):
                def gu_load(i):
                    mat = wqg if i % 2 == 0 else wqu
                    hc = i // 2
                    if i >= 2:
                        gpsimd.wait_ge(s_wdqe, i - 1)
                    if i >= EAGER:
                        gpsimd.wait_ge(s_gmm, min(max(0, 2 * (hc - 12)), 2 * HC))
                    gpsimd.dma_start(out=b_wst[i % 2][:, :],
                                     in_=mat[hc * 128:(hc + 1) * 128, :]).then_inc(s_wcl[i % 2], 16)

                def wd_load(j):
                    if j >= 2:
                        gpsimd.wait_ge(s_dq, j - 1)
                    gpsimd.wait_ge(s_gmm, min(max(0, 2 * j - 4), max(0, 2 * HC - 4)))
                    gpsimd.dma_start(out=b_dstT[j % 2][:, :],
                                     in_=wqd_t[j * 128:(j + 1) * 128, :]).then_inc(s_dcl[j % 2], 16)

                for i in range(EAGER):
                    gu_load(i)
                wj = [0]

                def wd_next():
                    if wj[0] < HC:
                        wd_load(wj[0])
                        wj[0] += 1

                wd_next()
                wd_next()
                for i in range(EAGER, 2 * HC):
                    gu_load(i)
                    if (i - EAGER) % 2 == 1:
                        wd_next()
                while wj[0] < HC:
                    wd_next()


            @block.vector
            def _(vector):
                vector.wait_ge(s_pp, 16 * 10)
                vector.tensor_copy(out=b_wabT[:, :], in_=b_wabF[:, :]).then_inc(s_ppc, 1)
                vector.tensor_copy(out=b_woT[:, :], in_=b_woF[:, :]).then_inc(s_ppc, 1)
                vector.tensor_copy(out=b_bout_b[:, :], in_=b_boutF[:, :]).then_inc(s_ppc, 1)
                vector.tensor_copy(out=b_ones_b[:, :], in_=b_onesF[:, :]).then_inc(s_ppc, 1)
                vector.memset(b_zero[:, :], 0.0).then_inc(s_ppc, 1)
                vector.memset(b_one[:, :], 1.0).then_inc(s_ppc, 1)
                vector.memset(b_nhalf[:, :], -0.5).then_inc(s_ppc, 1)
                vector.tensor_scalar(out=b_sgn[:, :], in0=b_sg[:, :], scalar1=-128.0,
                                     scalar2=0.0, op0=ALU.mult, op1=ALU.subtract).then_inc(s_ppc, 1)
                vector.tensor_scalar(out=b_sun[:, :], in0=b_su[:, :], scalar1=-128.0,
                                     scalar2=0.0, op0=ALU.mult, op1=ALU.subtract).then_inc(s_ppc, 1)
                vector.wait_ge(s_sdp, 1)
                vector.tensor_copy(out=b_sdbc[:, :], in_=pall[:, 0:D]).then_inc(s_ppc, 1)

                for bi in range(TQ // FSTB):
                    lo = bi * FSTB; hi = lo + FSTB; sl = slice(lo, hi)
                    for tc in range(lo, hi):
                        vector.wait_ge(s_xld[tc % NXB], 16 * (tc // NXB + 1))
                        vector.tensor_reduce(out=b_sum1[:, tc:tc + 1], in_=b_x[tc % NXB][:, :],
                                             axis=AX.X, op=ALU.add).then_inc(s_red, 1)
                        vector.scalar_tensor_tensor(out=b_sqs[:, :], in0=b_x[tc % NXB][:, :],
                                                    scalar=1.0, in1=b_x[tc % NXB][:, :],
                                                    op0=ALU.mult, op1=ALU.mult,
                                                    accum_out=b_ssq1[:, tc:tc + 1]).then_inc(s_ssq, 1)
                    vector.wait_ge(s_red, hi)
                    vector.wait_ge(s_ssq, hi)
                    fin_A(vector, b_sum1, b_ssq1, b_m1, b_tq1, b_tq2, sl).then_inc(s_fa, 1)
                    vector.wait_ge(s_fb, bi + 1)
                    fin_B(vector, b_tq2, b_m1, b_r1, b_mr1, sl).then_inc(s_f1v, 1)
                    vector.wait_ge(s_f1v, bi + 1)
                    for tc in range(lo, hi):
                        if tc >= 2:
                            vector.wait_ge(s_x1s[tc % 2], 16 * ((tc - 2) // 2 + 1))
                        vector.tensor_scalar(out=b_xn1[tc % 2][:, :], in0=b_x[tc % NXB][:, :],
                                             scalar1=b_r1[:, tc:tc + 1], scalar2=b_mr1[:, tc:tc + 1],
                                             op0=ALU.mult, op1=ALU.subtract).then_inc(s_ap, 1)

                vector.wait_ge(s_th, 2 * NPJ)
                vector.wait_ge(s_ssq, TQ)
                vector.tensor_tensor_scan(out=b_hs[:, :], data0=b_ab[:, 0:TSEQ],
                                          data1=b_ab[:, TSEQ:2 * TSEQ],
                                          initial=b_zero[:, :], op0=ALU.mult, op1=ALU.add).then_inc(s_hs, 1)
                vector.wait_ge(s_hs, 1)
                vector.tensor_scalar(out=b_ab[0:S, 0:TOWN], in0=b_hs[:, 0:TOWN],
                                     scalar1=b_sel0[0:S, :], scalar2=0.0,
                                     op0=ALU.mult, op1=ALU.subtract).then_inc(s_f1v, 1)
                vector.wait_ge(s_f1v, TQ // FSTB + 1)
                vector.scalar_tensor_tensor(out=b_hsb[:, :], in0=b_hs[:, TSEQ - TOWN:TSEQ],
                                            scalar=b_sel1[0:S, :], in1=b_ab[0:S, 0:TOWN],
                                            op0=ALU.mult, op1=ALU.add).then_inc(s_hs, 1)
                vector.tensor_scalar(out=b_tc1[:, :], in0=b_r1[:, 0:TC], scalar1=b_sel0[:, :],
                                     scalar2=0.0, op0=ALU.mult, op1=ALU.subtract).then_inc(s_f1v, 1)
                vector.wait_ge(s_f1v, TQ // FSTB + 2)
                vector.scalar_tensor_tensor(out=b_r1o[:, :], in0=b_r1[:, TQ - TC:TQ],
                                            scalar=b_sel1[:, :], in1=b_tc1[:, :],
                                            op0=ALU.mult, op1=ALU.add).then_inc(s_f1v, 1)
                vector.wait_ge(s_f1v, TQ // FSTB + 3)
                vector.tensor_scalar(out=b_tc1[:, :], in0=b_mr1[:, 0:TC], scalar1=b_sel0[:, :],
                                     scalar2=0.0, op0=ALU.mult, op1=ALU.subtract).then_inc(s_f1v, 1)
                vector.wait_ge(s_f1v, TQ // FSTB + 4)
                vector.scalar_tensor_tensor(out=b_mr1o[:, :], in0=b_mr1[:, TQ - TC:TQ],
                                            scalar=b_sel1[:, :], in1=b_tc1[:, :],
                                            op0=ALU.mult, op1=ALU.add).then_inc(s_hs, 1)

            @block.scalar
            def _(scalar):
                def tanh_blk(k):
                    scalar.wait_ge(s_abp, k + 1)
                    scalar.activation(out=b_ab[:, k * PJB:(k + 1) * PJB],
                                      in_=pall[0:S, (k % 2) * 512:(k % 2) * 512 + PJB],
                                      func=AF.Tanh, bias=b_abbias[:, 0:1]).then_inc(s_th, 1)
                    scalar.activation(out=b_ab[:, TSEQ + k * PJB:TSEQ + (k + 1) * PJB],
                                      in_=pall[0:S, 1024 + (k % 2) * 512:1024 + (k % 2) * 512 + PJB],
                                      func=AF.Tanh, bias=b_abbias[:, 1:2]).then_inc(s_th, 1)

                nxt_k = [0]
                for bi in range(TQ // FSTB):
                    scalar.wait_ge(s_fa, bi + 1)
                    fin_act(scalar, b_tq2, b_r1, slice(bi * FSTB, (bi + 1) * FSTB)).then_inc(s_fb, 1)
                    emit_cache_pair(scalar, 2)
                    while (nxt_k[0] < NPJ
                           and RPQ * (nxt_k[0] // NPJQ + 1) <= FSTB * bi):
                        tanh_blk(nxt_k[0])
                        nxt_k[0] += 1
                while nxt_k[0] < NPJ:
                    tanh_blk(nxt_k[0])
                    nxt_k[0] += 1
                while cdq_i[0] < EAGER or cst_i[0] < EAGER:
                    emit_cache_pair(scalar, 1)

            @block.tensor
            def _(tensor):
                tensor.wait_ge(s_pp, 16 * 10)
                for j in range(D // JW):
                    mm = tensor.matmul(pall[:, j * JW:(j + 1) * JW], b_onesF[:, :],
                                       b_sdrF[:, j * JW:(j + 1) * JW], start=True, stop=True)
                mm.then_inc(s_sdp, 1)
                tensor.wait_ge(s_ppc, 10)
                for q in range(NQ):
                    tensor.wait_ge(s_xt, 16 * DC * (q + 1))
                    for pb in range(NPJQ):
                        k = q * NPJQ + pb
                        if k >= 2:
                            tensor.wait_ge(s_th, 2 * (k - 1))
                        po_a = (k % 2) * 512
                        po_b = 1024 + (k % 2) * 512
                        xbuf = (q % 2) * DC
                        for dc in range(DC):
                            tensor.matmul(pall[0:S, po_a:po_a + PJB],
                                          b_wabT[:, dc * S2:dc * S2 + S],
                                          b_xt[xbuf + dc][:, pb * PJB:(pb + 1) * PJB],
                                          start=(dc == 0), stop=(dc == DC - 1))
                            mm = tensor.matmul(pall[0:S, po_b:po_b + PJB],
                                               b_wabT[:, dc * S2 + S:dc * S2 + S2],
                                               b_xt[xbuf + dc][:, pb * PJB:(pb + 1) * PJB],
                                               start=(dc == 0), stop=(dc == DC - 1))
                        mm.then_inc(s_abp, 1)

        B1 = contextlib.ExitStack()
        b1 = B1.enter_context
        with B1:
            b_xo = [b1(nc.sbuf_tensor(f"b_xo{i}", [128, D], F32)) for i in range(NOB)]
            b_txn = [b1(nc.sbuf_tensor(f"b_txn{i}", [128, D], F32)) for i in range(1)]
            b_y1 = [b1(nc.sbuf_tensor(f"b_y1{i}", [128, D], F32)) for i in range(3)]
            b_t2 = [b1(nc.sbuf_tensor(f"b_t2{i}", [128, D], F32)) for i in range(1)]
            b_x2 = [b1(nc.sbuf_tensor(f"b_x2{i}", [128, D], F32)) for i in range(3)]
            b_xn2 = [b1(nc.sbuf_tensor(f"b_xn2{i}", [128, D], BF16)) for i in range(2)]

            @block.sync
            def _(sync):
                sync.wait_ge(s_hs, 3)
                sync.wait_ge(s_ap, TQ)
                sync.wait_ge(s_ssq, TQ)
                sync.wait_ge(s_x1s[0], 16 * _cnt(TQ, 0, 2))
                sync.wait_ge(s_x1s[1], 16 * _cnt(TQ, 1, 2))
                sync.wait_ge(s_abp, NPJ)
                for oc in range(TC):
                    if oc >= NOB:
                        sync.wait_ge(s_x2, oc - NOB + 1)
                    sync.dma_start(out=b_xo[oc % NOB][:, :],
                                   in_=x_own[oc * 128:(oc + 1) * 128, :]).then_inc(s_xo[oc % NOB], 16)
                    if oc >= 2:
                        oc2 = oc - 2
                        sync.wait_ge(s_x2, oc2 + 1)
                        sync.dma_start(out=x2_h[oc2 * 128:(oc2 + 1) * 128, :],
                                       in_=b_x2[oc2 % 3][:, :]).then_inc(s_x2s[oc2 % 3], 16)
                        sync.wait_ge(s_xn2, oc2 + 1)
                        sync.dma_start(out=xn2b_h[oc2 * 128:(oc2 + 1) * 128, :],
                                       in_=b_xn2[oc2 % 2][:, :]).then_inc(s_n2s[oc2 % 2], 16)
                for oc2 in range(TC - 2, TC):
                    sync.wait_ge(s_x2, oc2 + 1)
                    sync.dma_start(out=x2_h[oc2 * 128:(oc2 + 1) * 128, :],
                                   in_=b_x2[oc2 % 3][:, :]).then_inc(s_x2s[oc2 % 3], 16)
                    sync.wait_ge(s_xn2, oc2 + 1)
                    sync.dma_start(out=xn2b_h[oc2 * 128:(oc2 + 1) * 128, :],
                                   in_=b_xn2[oc2 % 2][:, :]).then_inc(s_n2s[oc2 % 2], 16)

            @block.vector
            def _(vector):
                for bi in range(TC // OSTB):
                    lo = bi * OSTB; hi = lo + OSTB; sl = slice(lo, hi)
                    for oc in range(lo, hi):
                        vector.wait_ge(s_xo[oc % NOB], 16 * (oc // NOB + 1))
                        vector.wait_ge(s_wps, oc + 1)
                        if oc >= 3:
                            vector.wait_ge(s_sq2, oc - 2)
                        vector.tensor_scalar(out=b_txn[0][:, :], in0=b_xo[oc % NOB][:, :],
                                             scalar1=b_r1o[:, oc:oc + 1], scalar2=b_mr1o[:, oc:oc + 1],
                                             op0=ALU.mult, op1=ALU.subtract)
                        vector.scalar_tensor_tensor(out=b_y1[oc % 3][:, :], in0=b_txn[0][:, :],
                                                    scalar=1.0, in1=pall[:, 0:D],
                                                    op0=ALU.mult, op1=ALU.add,
                                                    accum_out=b_sum2[:, oc:oc + 1]).then_inc(s_y1, 1)
                    vector.wait_ge(s_sq2, hi)
                    vector.wait_ge(s_y1, hi)
                    fin_A(vector, b_sum2, b_ssq2, b_m2, b_tc1, b_tc2, sl).then_inc(s_fa2, 1)
                    vector.wait_ge(s_fb2, 2 * bi + 1)
                    dfence(vector, fin_B(vector, b_tc2, b_m2, b_r2, b_mr2, sl))
                    for oc in range(lo, hi):
                        if oc >= 3:
                            vector.wait_ge(s_x2s[(oc - 3) % 3], 16 * ((oc - 3) // 3 + 1))
                            vector.wait_ge(s_sq3, oc - 2)
                        vector.tensor_scalar(out=b_t2[0][:, :], in0=b_y1[oc % 3][:, :],
                                             scalar1=b_r2[:, oc:oc + 1], scalar2=b_mr2[:, oc:oc + 1],
                                             op0=ALU.mult, op1=ALU.subtract)
                        vector.scalar_tensor_tensor(out=b_x2[oc % 3][:, :], in0=b_t2[0][:, :],
                                                    scalar=1.0, in1=b_xo[oc % NOB][:, :],
                                                    op0=ALU.mult, op1=ALU.add,
                                                    accum_out=b_sum3[:, oc:oc + 1]).then_inc(s_x2, 1)
                    vector.wait_ge(s_sq3, hi)
                    vector.wait_ge(s_x2, hi)
                    fin_A(vector, b_sum3, b_ssq3, b_m3, b_tc1, b_tc2, sl).then_inc(s_fa2, 1)
                    vector.wait_ge(s_fb2, 2 * bi + 2)
                    dfence(vector, fin_B(vector, b_tc2, b_m3, b_r3, b_mr3, sl))
                    for oc in range(lo, hi):
                        if oc >= 2:
                            vector.wait_ge(s_n2s[oc % 2], 16 * ((oc - 2) // 2 + 1))
                        vector.tensor_scalar(out=b_xn2[oc % 2][:, :], in0=b_x2[oc % 3][:, :],
                                             scalar1=b_r3[:, oc:oc + 1], scalar2=b_mr3[:, oc:oc + 1],
                                             op0=ALU.mult, op1=ALU.subtract).then_inc(s_xn2, 1)

            @block.scalar
            def _(scalar):
                for bi in range(TC // OSTB):
                    lo = bi * OSTB; hi = lo + OSTB
                    for oc in range(lo, hi):
                        scalar.wait_ge(s_y1, oc + 1)
                        scalar.activation(out=b_sqs[:, :], in_=b_y1[oc % 3][:, :], func=AF.Square,
                                          accum_out=b_ssq2[:, oc:oc + 1]).then_inc(s_sq2, 1)
                    scalar.wait_ge(s_fa2, 2 * bi + 1)
                    fin_act(scalar, b_tc2, b_r2, slice(lo, hi)).then_inc(s_fb2, 1)
                    for oc in range(lo, hi):
                        scalar.wait_ge(s_x2, oc + 1)
                        scalar.activation(out=b_sqs[:, :], in_=b_x2[oc % 3][:, :], func=AF.Square,
                                          accum_out=b_ssq3[:, oc:oc + 1]).then_inc(s_sq3, 1)
                    scalar.wait_ge(s_fa2, 2 * bi + 2)
                    fin_act(scalar, b_tc2, b_r3, slice(lo, hi)).then_inc(s_fb2, 1)

            @block.tensor
            def _(tensor):
                for oc in range(TC):
                    tensor.wait_ge(s_hs, 2)
                    if oc >= 1:
                        tensor.wait_ge(s_y1, oc)
                    for j in range(NJW):
                        tensor.matmul(pall[:, j * JW:(j + 1) * JW],
                                      b_hsb[:, oc * 128:(oc + 1) * 128],
                                      b_woT[:, j * JW:(j + 1) * JW],
                                      start=True, stop=False)
                        mm = tensor.matmul(pall[:, j * JW:(j + 1) * JW],
                                           b_ones_b[:, :], b_bout_b[:, j * JW:(j + 1) * JW],
                                           start=False, stop=True)
                    mm.then_inc(s_wps, 1)

        B2 = contextlib.ExitStack()
        b2 = B2.enter_context
        with B2:
            b_xn2T = [b2(nc.sbuf_tensor(f"b_xn2T{i}", [128, TSL], BF16)) for i in range(DC)]
            b_wg = [b2(nc.sbuf_tensor(f"b_wg{i}", [128, HB], BF16)) for i in range(2 * DC)]
            b_wu = [b2(nc.sbuf_tensor(f"b_wu{i}", [128, HB], BF16)) for i in range(DC)]
            b_gt_ = [b2(nc.sbuf_tensor(f"b_gt{i}", [128, TSL], BF16)) for i in range(2)]
            b_ut = [b2(nc.sbuf_tensor(f"b_ut{i}", [128, TSL], BF16)) for i in range(2)]
            b_pw = [b2(nc.sbuf_tensor(f"b_pw{i}", [128, TSL], BF16)) for i in range(2)]
            b_pst = [b2(nc.sbuf_tensor(f"b_pst{i}", [128, TSL], BF16)) for i in range(NPS)]
            b_wd = [b2(nc.sbuf_tensor(f"b_wd{i}", [128, DSL], BF16)) for i in range(NWD)]
            b_f = [b2(nc.sbuf_tensor(f"b_f{i}", [128, D], BF16)) for i in range(TCS)]
            b_x2r = [b2(nc.sbuf_tensor(f"b_x2r{i}", [128, D], F32)) for i in range(2)]

            @block.sync
            def _(sync):
                for j in range(3):
                    sync.wait_ge(s_x2s[j], 16 * _cnt(TC, j, 3))
                for j in range(2):
                    sync.wait_ge(s_n2s[j], 16 * _cnt(TC, j, 2))
                sync.wait_ge(s_sq2, TC)
                sync.wait_ge(s_sq3, TC)
                sync.wait_ge(s_xn2, TC)
                for sw in range(NSW):
                    if sw > 0:
                        sync.wait_ge(s_gmm, 2 * HC * sw)
                    for dc in range(DC):
                        sync.dma_start(out=b_xn2T[dc][:, :],
                                       in_=xn2b_h[sw * TSL:(sw + 1) * TSL, dc * 128:(dc + 1) * 128],
                                       transpose=True).then_inc(s_n2t, 16)
                    def emit_prod_store(hc):
                        ghc = sw * HC + hc
                        sync.wait_ge(s_pr, ghc + 1)
                        sync.dma_start(out=prod_h[hc * 128:(hc + 1) * 128, :],
                                       in_=b_pw[ghc % 2][:, :]).then_inc(s_prs[ghc % 2], 16)

                    for hb in range(NHB):
                        g_blk = sw * NHB + hb
                        sync.wait_ge(s_wgs, 16 * min((hb + 1) * HCB, HC))
                        if g_blk >= 2:
                            sw2, hb2 = divmod(g_blk - 2, NHB)
                            sync.wait_ge(s_gmm, 2 * HC * sw2 + 2 * (HCB * (hb2 + 1) - 1) + 1)
                        for dc in range(DC):
                            sync.dma_start(out=b_wg[(g_blk % 2) * DC + dc][:, :],
                                           in_=wg_deq[hb * HB:(hb + 1) * HB, dc * 128:(dc + 1) * 128],
                                           transpose=True).then_inc(s_gwg, 16)
                        sync.wait_ge(s_wus, 16 * min((hb + 1) * HCB, HC))
                        if sw > 0 or hb > 0:
                            sync.wait_ge(s_gmm, 2 * HC * sw + 2 * HCB * hb)
                        for dc in range(DC):
                            sync.dma_start(out=b_wu[dc][:, :],
                                           in_=wu_deq[hb * HB:(hb + 1) * HB, dc * 128:(dc + 1) * 128],
                                           transpose=True).then_inc(s_gwu, 16)
                        if hb >= 1:
                            for hc in range((hb - 1) * HCB, hb * HCB):
                                emit_prod_store(hc)
                    for hc in range((NHB - 1) * HCB, HC):
                        emit_prod_store(hc)
                    for ds in range(NDS):
                        for hc in range(HC):
                            k = sw * NDS * HC + ds * HC + hc
                            ghc = sw * HC + hc
                            sync.wait_ge(s_prs[ghc % 2], 16 * (ghc // 2 + 1))
                            if k >= NPS:
                                sync.wait_ge(s_dmm, k - NPS + 1)
                            sync.dma_start(out=b_pst[k % NPS][:, :],
                                           in_=prod_h[hc * 128:(hc + 1) * 128, :]).then_inc(s_psl[k % NPS], 16)
                            sync.wait_ge(s_wds[hc % 2], 16 * (hc // 2 + 1))
                            if k >= NWD:
                                sync.wait_ge(s_dmm, k - NWD + 1)
                            sync.dma_start(out=b_wd[k % NWD][:, :],
                                           in_=wdT_deq[hc * 128:(hc + 1) * 128, ds * DSL:(ds + 1) * DSL]).then_inc(s_dw[k % NWD], 16)
                    for tc in range(TCS):
                        gtc = sw * TCS + tc
                        if gtc >= 2:
                            sync.wait_ge(s_ys[gtc % 2], 16 * ((gtc - 2) // 2 + 1))
                        sync.dma_start(out=b_x2r[gtc % 2][:, :],
                                       in_=x2_h[gtc * 128:(gtc + 1) * 128, :]).then_inc(s_x2r[gtc % 2], 16)
                        sync.wait_ge(s_yo, gtc + 1)
                        sync.dma_start(out=y_out[gtc * 128:(gtc + 1) * 128, :],
                                       in_=b_x2r[gtc % 2][:, :]).then_inc(s_ys[gtc % 2], 16)
                sync.wait_ge(s_ys[0], 16 * _cnt(TC, 0, 2))
                sync.wait_ge(s_ys[1], 16 * _cnt(TC, 1, 2))
                if dbg_st is not None:
                    for i, t in enumerate([b_sum1, b_ssq1, b_m1, b_r1, b_mr1, b_tq2]):
                        sync.dma_start(out=dbg_st[i * 128:(i + 1) * 128, :],
                                       in_=t[:, :]).then_inc(s_pp, 16)
                    sync.wait_ge(s_pp, 16 * 16)

            @block.tensor
            def _(tensor):
                for sw in range(NSW):
                    tensor.wait_ge(s_n2t, 16 * DC * (sw + 1))
                    for hc in range(HC):
                        hb, hcc = divmod(hc, HCB)
                        tensor.wait_ge(s_gwg, 16 * DC * (sw * NHB + hb + 1))
                        if hc >= 2:
                            tensor.wait_ge(s_gev, 2 * HC * sw + 2 * (hc - 2) + 1)
                        elif sw > 0:
                            tensor.wait_ge(s_dev, sw * NDS * TCS)
                        po = (4 + 2 * (hc % 2)) * 512
                        gbuf = ((sw * NHB + hb) % 2) * DC
                        for dc in range(DC):
                            for j in range(NJ5):
                                mm = tensor.matmul(pall[:, po + j * T5:po + (j + 1) * T5],
                                                   b_wg[gbuf + dc][:, hcc * 128:(hcc + 1) * 128],
                                                   b_xn2T[dc][:, j * T5:(j + 1) * T5],
                                                   start=(dc == 0), stop=(dc == DC - 1))
                        mm.then_inc(s_gmm, 1)
                        tensor.wait_ge(s_gwu, 16 * DC * (sw * NHB + hb + 1))
                        if hc >= 2:
                            tensor.wait_ge(s_gev, 2 * HC * sw + 2 * (hc - 2) + 2)
                        elif sw == 0:
                            tensor.wait_ge(s_y1, TC)
                        else:
                            tensor.wait_ge(s_dev, sw * NDS * TCS)
                        po = (2 * (hc % 2)) * 512
                        for dc in range(DC):
                            for j in range(NJ5):
                                mm = tensor.matmul(pall[:, po + j * T5:po + (j + 1) * T5],
                                                   b_wu[dc][:, hcc * 128:(hcc + 1) * 128],
                                                   b_xn2T[dc][:, j * T5:(j + 1) * T5],
                                                   start=(dc == 0), stop=(dc == DC - 1))
                        mm.then_inc(s_gmm, 1)
                    for ds in range(NDS):
                        if ds == 0:
                            tensor.wait_ge(s_gev, 2 * HC * (sw + 1))
                        else:
                            tensor.wait_ge(s_dev, (sw * NDS + ds) * TCS)
                        for hc in range(HC):
                            k = sw * NDS * HC + ds * HC + hc
                            tensor.wait_ge(s_psl[k % NPS], 16 * (k // NPS + 1))
                            tensor.wait_ge(s_dw[k % NWD], 16 * (k // NWD + 1))
                            for tc in range(TCS):
                                mm = tensor.matmul(pall[:, tc * 512:tc * 512 + DSL],
                                                   b_pst[k % NPS][:, tc * 128:(tc + 1) * 128],
                                                   b_wd[k % NWD][:, :],
                                                   start=(hc == 0), stop=(hc == HC - 1))
                            mm.then_inc(s_dmm, 1)

            @block.scalar
            def _(scalar):
                for sw in range(NSW):
                    for hc in range(HC):
                        ghc = sw * HC + hc
                        scalar.wait_ge(s_gmm, 2 * HC * sw + 2 * hc + 1)
                        if ghc >= 2:
                            scalar.wait_ge(s_pr, ghc - 1)
                        po = (4 + 2 * (hc % 2)) * 512
                        scalar.activation(out=b_gt_[hc % 2][:, :], in_=pall[:, po:po + TSL],
                                          func=AF.Silu).then_inc(s_gev, 1)
                        scalar.wait_ge(s_gmm, 2 * HC * sw + 2 * hc + 2)
                        po = (2 * (hc % 2)) * 512
                        scalar.activation(out=b_ut[hc % 2][:, :], in_=pall[:, po:po + TSL],
                                          func=AF.Copy).then_inc(s_gev, 1)
                        if sw == 0:
                            emit_cache_pair(scalar, 2)
                            emit_wd_store(scalar, 1)
                    if sw == 0:
                        emit_cache_pair(scalar, 2 * HC)
                        emit_wd_store(scalar, HC)
                    for ds in range(NDS):
                        for tc in range(TCS):
                            scalar.wait_ge(s_dmm, sw * NDS * HC + ds * HC + HC)
                            if sw > 0 and ds == 0:
                                scalar.wait_ge(s_yo, sw * TCS)
                            scalar.activation(out=b_f[tc][:, ds * DSL:(ds + 1) * DSL],
                                              in_=pall[:, tc * 512:tc * 512 + DSL],
                                              func=AF.Copy).then_inc(s_dev, 1)
                    for tc in range(TCS):
                        gtc = sw * TCS + tc
                        scalar.activation(out=b_sqs[:, :], in_=b_f[tc][:, :], func=AF.Copy,
                                          accum_out=b_sum4[:, gtc:gtc + 1]).then_inc(s_sq4, 1)
                        scalar.activation(out=b_sqs[:, :], in_=b_f[tc][:, :], func=AF.Square,
                                          accum_out=b_ssq4[:, gtc:gtc + 1]).then_inc(s_sq4, 1)
                    scalar.wait_ge(s_fa4, sw + 1)
                    fin_act(scalar, b_tc2, b_r4, slice(sw * TCS, (sw + 1) * TCS)).then_inc(s_fb4, 1)

            @block.vector
            def _(vector):
                for sw in range(NSW):
                    for hc in range(HC):
                        ghc = sw * HC + hc
                        vector.wait_ge(s_gev, 2 * HC * sw + 2 * (hc + 1))
                        if ghc >= 2:
                            vector.wait_ge(s_prs[ghc % 2], 16 * ((ghc - 2) // 2 + 1))
                        vector.tensor_tensor(out=b_pw[ghc % 2][:, :], in0=b_gt_[hc % 2][:, :],
                                             in1=b_ut[hc % 2][:, :], op=ALU.mult).then_inc(s_pr, 1)
                        if sw == 0:
                            emit_wd_dequant(vector, 1)

                    if sw == 0:
                        emit_wd_dequant(vector, HC)
                    sl = slice(sw * TCS, (sw + 1) * TCS)
                    vector.wait_ge(s_sq4, 2 * (sw + 1) * TCS)
                    fin_A(vector, b_sum4, b_ssq4, b_m4, b_tc1, b_tc2, sl).then_inc(s_fa4, 1)
                    vector.wait_ge(s_fb4, sw + 1)
                    dfence(vector, fin_B(vector, b_tc2, b_m4, b_r4, b_mr4, sl))
                    for tc in range(TCS):
                        gtc = sw * TCS + tc
                        vector.wait_ge(s_x2r[gtc % 2], 16 * (gtc // 2 + 1))
                        vector.tensor_scalar(out=b_x2r[gtc % 2][:, :], in0=b_x2r[gtc % 2][:, :],
                                             scalar1=1.0, scalar2=b_mr4[:, gtc:gtc + 1],
                                             op0=ALU.mult, op1=ALU.subtract)
                        vector.scalar_tensor_tensor(out=b_x2r[gtc % 2][:, :], in0=b_f[tc][:, :],
                                                    scalar=b_r4[:, gtc:gtc + 1],
                                                    in1=b_x2r[gtc % 2][:, :],
                                                    op0=ALU.mult, op1=ALU.add).then_inc(s_yo, 1)

    return nc


def kernel(L0_attn_x, L0_attn_Wa, L0_attn_ba, L0_attn_Wb, L0_attn_bb,
           L0_attn_Wout, L0_attn_bout, L0_ffn_wq_gate, L0_ffn_s_gate,
           L0_ffn_wq_up, L0_ffn_s_up, L0_ffn_wq_down, L0_ffn_s_down,
           _cfg=None, _n_cores=8):
    cfg = dict(FULL_CFG if _cfg is None else _cfg)
    D = cfg["D"]; TSEQ = cfg["TSEQ"]; TOWN = cfg["TOWN"]
    HC = cfg["H"] // 128; DC = D // 128
    x = np.ascontiguousarray(np.asarray(L0_attn_x, dtype=np.float32)).reshape(-1, TSEQ, D)
    B = x.shape[0]
    n_cores = _n_cores

    wab_t = np.ascontiguousarray(np.concatenate([np.asarray(L0_attn_Wa, np.float32),
                                                 np.asarray(L0_attn_Wb, np.float32)], axis=0).T)
    ab_bias = np.ascontiguousarray(np.stack([np.asarray(L0_attn_ba, np.float32),
                                             np.asarray(L0_attn_bb, np.float32)], axis=1))
    wout_t = np.ascontiguousarray(np.asarray(L0_attn_Wout, np.float32).T)
    bout_row = np.ascontiguousarray(np.asarray(L0_attn_bout, np.float32)[None, :])
    wqg = np.ascontiguousarray(np.asarray(L0_ffn_wq_gate, np.int32))
    wqu = np.ascontiguousarray(np.asarray(L0_ffn_wq_up, np.int32))
    wqd_t = np.ascontiguousarray(np.asarray(L0_ffn_wq_down, np.int32).T)
    sg_t = np.ascontiguousarray(np.asarray(L0_ffn_s_gate, np.float32).reshape(HC, 128).T)
    su_t = np.ascontiguousarray(np.asarray(L0_ffn_s_up, np.float32).reshape(HC, 128).T)
    sd_row = np.ascontiguousarray(np.asarray(L0_ffn_s_down, np.float32)[None, :])
    ones_r = np.ones((1, 128), np.float32)

    nc = build_nc(cfg)
    in_maps = []
    for c in range(n_cores):
        b, half = divmod(c, 2)
        in_maps.append(dict(
            x_seq=np.ascontiguousarray(x[b]),
            x_own=np.ascontiguousarray(x[b, half * TOWN:(half + 1) * TOWN]),
            sel0=np.full((128, 1), 1.0 - half, np.float32),
            sel1=np.full((128, 1), float(half), np.float32),
            wab_t=wab_t, ab_bias=ab_bias, wout_t=wout_t, bout_row=bout_row,
            wqg=wqg, wqu=wqu, wqd_t=wqd_t, sg_t=sg_t, su_t=su_t, sd_row=sd_row,
            ones_r=ones_r,
        ))
    res = run_bass_kernel_spmd(nc, in_maps, core_ids=list(range(n_cores)))
    out = np.concatenate([res.results[c]["y_out"] for c in range(n_cores)], axis=0)
    return out.reshape(B, TSEQ, D)


# version 11
# speedup vs baseline: 1.2496x; 1.2496x over previous
import contextlib
import numpy as np

import concourse.bass as bass
import concourse.mybir as mybir
from concourse.bass_utils import run_bass_kernel_spmd

F32 = mybir.dt.float32
BF16 = mybir.dt.bfloat16
I32 = mybir.dt.int32
AF = mybir.ActivationFunctionType
ALU = mybir.AluOpType
AX = mybir.AxisListType

FULL_CFG = dict(D=2048, S=64, TSEQ=4096, TOWN=2048, H=5632, NSW=2, EPS=1e-5)


def _cnt(K, j, NB):
    return (K - j + NB - 1) // NB


def build_nc(cfg, dbg=False):
    D = cfg["D"]; S = cfg["S"]; TSEQ = cfg["TSEQ"]; TOWN = cfg["TOWN"]
    H = cfg["H"]; NSW = cfg["NSW"]; EPS = cfg["EPS"]
    DC = D // 128
    TQ = TSEQ // 128
    TC = TOWN // 128
    HC = H // 128
    TSL = TOWN // NSW
    TCS = TSL // 128
    S2 = 2 * S
    QTB = min(512, TSEQ // 2)
    NQ = TSEQ // QTB
    RPQ = QTB // 128
    PJB = min(512, QTB)
    NPJQ = QTB // PJB
    NPJ = NQ * NPJQ
    HB = min(512, H)
    NHB = H // HB
    HCB = HB // 128
    DSL = min(512, D)
    NDS = D // DSL
    T5 = min(512, TSL)
    NJ5 = TSL // T5
    JW = min(512, D)
    NJW = D // JW
    FSTB = 4 if TQ % 4 == 0 else 1
    OSTB = 2 if TC % 2 == 0 else 1
    NXB = 5
    NOB = 3
    NWD = 4
    NPS = 4
    IVD = 1.0 / D

    nc = bass.Bass("TRN2", target_bir_lowering=False, detect_race_conditions=False)

    x_seq = nc.dram_tensor("x_seq", [TSEQ, D], F32, kind="ExternalInput").ap()
    x_own = nc.dram_tensor("x_own", [TOWN, D], F32, kind="ExternalInput").ap()
    sel0 = nc.dram_tensor("sel0", [128, 1], F32, kind="ExternalInput").ap()
    sel1 = nc.dram_tensor("sel1", [128, 1], F32, kind="ExternalInput").ap()
    wab_t = nc.dram_tensor("wab_t", [D, S2], F32, kind="ExternalInput").ap()
    ab_bias = nc.dram_tensor("ab_bias", [S, 2], F32, kind="ExternalInput").ap()
    wout_t = nc.dram_tensor("wout_t", [S, D], F32, kind="ExternalInput").ap()
    bout_row = nc.dram_tensor("bout_row", [1, D], F32, kind="ExternalInput").ap()
    wqg_t = nc.dram_tensor("wqg_t", [D, H], I32, kind="ExternalInput").ap()
    wqu_t = nc.dram_tensor("wqu_t", [D, H], I32, kind="ExternalInput").ap()
    wqd_t = nc.dram_tensor("wqd_t", [H, D], I32, kind="ExternalInput").ap()
    sg_t = nc.dram_tensor("sg_t", [128, HC], F32, kind="ExternalInput").ap()
    su_t = nc.dram_tensor("su_t", [128, HC], F32, kind="ExternalInput").ap()
    sd_row = nc.dram_tensor("sd_row", [1, D], F32, kind="ExternalInput").ap()
    ones_r = nc.dram_tensor("ones_r", [1, 128], F32, kind="ExternalInput").ap()

    y_out = nc.dram_tensor("y_out", [TOWN, D], F32, kind="ExternalOutput").ap()

    KI = "ExternalOutput" if dbg else "Internal"
    xn1b_h = nc.dram_tensor("xn1b_h", [TSEQ, D], BF16, kind=KI).ap()
    xn2b_h = nc.dram_tensor("xn2b_h", [TOWN, D], BF16, kind=KI).ap()
    x2_h = nc.dram_tensor("x2_h", [TOWN, D], F32, kind=KI).ap()
    prod_h = nc.dram_tensor("prod_h", [H, TSL], BF16, kind=KI).ap()
    dbg_st = nc.dram_tensor("dbg_st", [6 * 128, TQ], F32, kind=KI).ap() if dbg else None

    G = contextlib.ExitStack()
    g = G.enter_context
    with G:
        b_wabT = g(nc.sbuf_tensor("b_wabT", [128, DC * S2], BF16))
        b_woT = g(nc.sbuf_tensor("b_woT", [S, D], BF16))
        b_abbias = g(nc.sbuf_tensor("b_abbias", [S, 2], F32))
        b_bout_b = g(nc.sbuf_tensor("b_bout_b", [1, D], BF16))
        b_ones_b = g(nc.sbuf_tensor("b_ones_b", [1, 128], BF16))
        b_sel0 = g(nc.sbuf_tensor("b_sel0", [128, 1], F32))
        b_sel1 = g(nc.sbuf_tensor("b_sel1", [128, 1], F32))
        b_sg = g(nc.sbuf_tensor("b_sg", [128, HC], F32))
        b_su = g(nc.sbuf_tensor("b_su", [128, HC], F32))
        b_neg128 = g(nc.sbuf_tensor("b_neg128", [128, 1], F32))
        b_sdbc = g(nc.sbuf_tensor("b_sdbc", [128, D], BF16))
        b_onesF = g(nc.sbuf_tensor("b_onesF", [1, 128], F32))
        b_zero = g(nc.sbuf_tensor("b_zero", [S, 1], F32))
        b_sum1 = g(nc.sbuf_tensor("b_sum1", [128, TQ], F32))
        b_ssq1 = g(nc.sbuf_tensor("b_ssq1", [128, TQ], F32))
        b_m1 = g(nc.sbuf_tensor("b_m1", [128, TQ], F32))
        b_r1 = g(nc.sbuf_tensor("b_r1", [128, TQ], F32))
        b_mr1 = g(nc.sbuf_tensor("b_mr1", [128, TQ], F32))
        b_tq1 = g(nc.sbuf_tensor("b_tq1", [128, TQ], F32))
        b_tq2 = g(nc.sbuf_tensor("b_tq2", [128, TQ], F32))
        b_r1o = g(nc.sbuf_tensor("b_r1o", [128, TC], F32))
        b_mr1o = g(nc.sbuf_tensor("b_mr1o", [128, TC], F32))
        b_sum2 = g(nc.sbuf_tensor("b_sum2", [128, TC], F32))
        b_ssq2 = g(nc.sbuf_tensor("b_ssq2", [128, TC], F32))
        b_m2 = g(nc.sbuf_tensor("b_m2", [128, TC], F32))
        b_r2 = g(nc.sbuf_tensor("b_r2", [128, TC], F32))
        b_mr2 = g(nc.sbuf_tensor("b_mr2", [128, TC], F32))
        b_sum3 = g(nc.sbuf_tensor("b_sum3", [128, TC], F32))
        b_ssq3 = g(nc.sbuf_tensor("b_ssq3", [128, TC], F32))
        b_m3 = g(nc.sbuf_tensor("b_m3", [128, TC], F32))
        b_r3 = g(nc.sbuf_tensor("b_r3", [128, TC], F32))
        b_mr3 = g(nc.sbuf_tensor("b_mr3", [128, TC], F32))
        b_sum4 = g(nc.sbuf_tensor("b_sum4", [128, TC], F32))
        b_ssq4 = g(nc.sbuf_tensor("b_ssq4", [128, TC], F32))
        b_m4 = g(nc.sbuf_tensor("b_m4", [128, TC], F32))
        b_r4 = g(nc.sbuf_tensor("b_r4", [128, TC], F32))
        b_mr4 = g(nc.sbuf_tensor("b_mr4", [128, TC], F32))
        b_tc1 = g(nc.sbuf_tensor("b_tc1", [128, TC], F32))
        b_tc2 = g(nc.sbuf_tensor("b_tc2", [128, TC], F32))
        b_sqs = g(nc.sbuf_tensor("b_sqs", [128, D], BF16))
        b_one = g(nc.sbuf_tensor("b_one", [128, max(TQ, TC)], F32))
        b_nhalf = g(nc.sbuf_tensor("b_nhalf", [128, 1], F32))
        b_hsb = g(nc.sbuf_tensor("b_hsb", [S, TOWN], BF16))
        RG = 8; RU = 8; RD = 8
        b_wraw_g = [g(nc.sbuf_tensor(f"b_wraw_g{i}", [128, HB], BF16)) for i in range(RG)]
        b_wraw_u = [g(nc.sbuf_tensor(f"b_wraw_u{i}", [128, HB], BF16)) for i in range(RU)]
        b_wdraw = [g(nc.sbuf_tensor(f"b_wdraw{i}", [128, DSL], BF16)) for i in range(RD)]
        pall = g(nc.psum_tensor("pall", [128, 8 * 512], F32))

        def S1_(name):
            return G.enter_context(nc.semaphore(name))

        def SN_(name, n):
            return [G.enter_context(nc.semaphore(f"{name}{i}")) for i in range(n)]

        s_ppc = S1_("s_ppc"); s_red = S1_("s_red"); s_ssq = S1_("s_ssq")
        s_fa = S1_("s_fa"); s_fb = S1_("s_fb"); s_ap = S1_("s_ap")
        s_abp = S1_("s_abp"); s_th = S1_("s_th"); s_hs = S1_("s_hs")
        s_wps = S1_("s_wps"); s_y1 = S1_("s_y1")
        s_sq2 = S1_("s_sq2"); s_x2 = S1_("s_x2")
        s_sq3 = S1_("s_sq3"); s_xn2 = S1_("s_xn2")
        s_fa2 = S1_("s_fa2"); s_fb2 = S1_("s_fb2")
        s_gdq = S1_("s_gdq"); s_udq = S1_("s_udq"); s_dwq = S1_("s_dwq")
        s_gmm = S1_("s_gmm"); s_gev = S1_("s_gev"); s_pr = S1_("s_pr")
        s_dmm = S1_("s_dmm"); s_dev = S1_("s_dev")
        s_sq4 = S1_("s_sq4"); s_fa4 = S1_("s_fa4"); s_fb4 = S1_("s_fb4")
        s_yo = S1_("s_yo"); s_sdp = S1_("s_sdp")
        s_f1v = S1_("s_f1v"); s_r3v = S1_("s_r3v")
        s_dvf = S1_("s_dvf"); s_acf = S1_("s_acf")
        dvf_n = [0]
        acf_n = [0]

        def dfence(vec, instr):
            instr.then_inc(s_dvf, 1)
            dvf_n[0] += 1
            vec.wait_ge(s_dvf, dvf_n[0])

        def afence(scal, instr):
            instr.then_inc(s_acf, 1)
            acf_n[0] += 1
            scal.wait_ge(s_acf, acf_n[0])
        s_pp = S1_("s_pp")
        s_xld = SN_("s_xld", NXB)
        s_x1s = SN_("s_x1s", 2)
        s_xt = S1_("s_xt")
        s_xo = SN_("s_xo", NOB)
        s_x2s = SN_("s_x2s", 3)
        s_n2s = SN_("s_n2s", 2)
        s_rg = SN_("s_rg", RG)
        s_ru = SN_("s_ru", RU)
        s_rd = SN_("s_rd", RD)
        s_n2t = S1_("s_n2t")
        s_prs = SN_("s_prs", 2)
        s_psl = SN_("s_psl", NPS)
        s_x2r = SN_("s_x2r", 2)
        s_ys = SN_("s_ys", 2)

        block = g(nc.Block())

        def fin_A(vec, sums, ssqs, m_t, tmpa, tmpb, sl):
            dfence(vec, vec.tensor_scalar(out=m_t[:, sl], in0=sums[:, sl], scalar1=IVD,
                                          scalar2=0.0, op0=ALU.mult, op1=ALU.subtract))
            dfence(vec, vec.tensor_tensor(out=tmpa[:, sl], in0=m_t[:, sl], in1=m_t[:, sl],
                                          op=ALU.mult))
            dfence(vec, vec.tensor_scalar(out=tmpa[:, sl], in0=tmpa[:, sl], scalar1=1.0,
                                          scalar2=EPS, op0=ALU.mult, op1=ALU.subtract))
            return vec.scalar_tensor_tensor(out=tmpb[:, sl], in0=ssqs[:, sl], scalar=IVD,
                                            in1=tmpa[:, sl], op0=ALU.mult, op1=ALU.subtract)

        def fin_act(scalar_, tmpb, r_t, sl):
            afence(scalar_, scalar_.activation(out=tmpb[:, sl], in_=tmpb[:, sl], func=AF.Ln))
            return scalar_.activation(out=r_t[:, sl], in_=tmpb[:, sl], func=AF.Exp,
                                      scale=b_nhalf[:, :])

        def fin_B(vec, tmpb, m_t, r_t, mr_t, sl):
            return vec.tensor_tensor(out=mr_t[:, sl], in0=m_t[:, sl], in1=r_t[:, sl], op=ALU.mult)

        NGB = NSW * NHB
        cg_i = [0]; cu_i = [0]

        def emit_gdq(vec, lim):
            while cg_i[0] < min(lim, NGB * DC):
                ig = cg_i[0]
                gb, dc = divmod(ig, DC)
                vec.wait_ge(s_rg[ig % RG], 16 * (ig // RG + 1))
                if dc == 0 and gb >= 2:
                    vec.wait_ge(s_gmm, 2 * HCB * (gb - 1) - 1)
                vec.tensor_scalar(out=b_wg[(gb % 2) * DC + dc][:, :],
                                  in0=b_wraw_g[ig % RG][:, :],
                                  scalar1=128.0, scalar2=1.0,
                                  op0=ALU.subtract, op1=ALU.mult).then_inc(s_gdq, 1)
                cg_i[0] += 1

        def emit_udq(scalar_, lim):
            while cu_i[0] < min(lim, NGB * DC):
                iu = cu_i[0]
                gb, dc = divmod(iu, DC)
                scalar_.wait_ge(s_ru[iu % RU], 16 * (iu // RU + 1))
                if iu == 0:
                    scalar_.wait_ge(s_ppc, 9)
                if dc == 0 and gb >= 1:
                    scalar_.wait_ge(s_gmm, 2 * HCB * gb)
                scalar_.activation(out=b_wu[dc][:, :], in_=b_wraw_u[iu % RU][:, :],
                                   func=AF.Identity,
                                   bias=b_neg128[:, :]).then_inc(s_udq, 1)
                cu_i[0] += 1

        A = contextlib.ExitStack()
        a = A.enter_context
        with A:
            P0 = contextlib.ExitStack()
            p0 = P0.enter_context
            b_wabF = p0(nc.sbuf_tensor("b_wabF", [128, DC * S2], F32))
            b_woF = p0(nc.sbuf_tensor("b_woF", [S, D], F32))
            b_boutF = p0(nc.sbuf_tensor("b_boutF", [1, D], F32))
            b_sdrF = p0(nc.sbuf_tensor("b_sdrF", [1, D], F32))
            P0.close()
            b_x = [a(nc.sbuf_tensor(f"b_x{i}", [128, D], F32)) for i in range(NXB)]
            b_xn1 = [a(nc.sbuf_tensor(f"b_xn1{i}", [128, D], BF16)) for i in range(2)]
            b_xt = [a(nc.sbuf_tensor(f"b_xt{i}", [128, QTB], BF16)) for i in range(2 * DC)]
            b_ab = a(nc.sbuf_tensor("b_ab", [S, 2 * TSEQ], F32))
            b_hs = a(nc.sbuf_tensor("b_hs", [S, TSEQ], F32))

            @block.sync
            def _(sync):
                for dst, srcap in [(b_wabF[:, :].rearrange("p (c s) -> p c s", s=S2),
                                    wab_t.rearrange("(c p) s -> p c s", p=128)),
                                   (b_abbias[:, :], ab_bias),
                                   (b_woF[:, :], wout_t),
                                   (b_boutF[:, :], bout_row), (b_onesF[:, :], ones_r),
                                   (b_sel0[:, :], sel0), (b_sel1[:, :], sel1),
                                   (b_sg[:, :], sg_t), (b_su[:, :], su_t),
                                   (b_sdrF[:, :], sd_row)]:
                    sync.dma_start(out=dst, in_=srcap).then_inc(s_pp, 16)
                LAG = FSTB + 1
                sync.wait_ge(s_ppc, 9)
                sync.wait_ge(s_sdp, 1)

                def xt_round(q):
                    K = RPQ * (q + 1)
                    sync.wait_ge(s_x1s[0], 16 * _cnt(K, 0, 2))
                    sync.wait_ge(s_x1s[1], 16 * _cnt(K, 1, 2))
                    if q >= 2:
                        sync.wait_ge(s_abp, NPJQ * (q - 1))
                    for dc in range(DC):
                        sync.dma_start(out=b_xt[(q % 2) * DC + dc][:, :],
                                       in_=xn1b_h[q * QTB:(q + 1) * QTB, dc * 128:(dc + 1) * 128],
                                       transpose=True).then_inc(s_xt, 16)

                nxt_q = [0]
                for tc in range(TQ):
                    if tc >= NXB:
                        sync.wait_ge(s_ap, tc - NXB + 1)
                        sync.wait_ge(s_red, tc - NXB + 1)
                        sync.wait_ge(s_ssq, tc - NXB + 1)
                    sync.dma_start(out=b_x[tc % NXB][:, :],
                                   in_=x_seq[tc * 128:(tc + 1) * 128, :]).then_inc(s_xld[tc % NXB], 16)
                    if tc >= LAG:
                        st_c = tc - LAG
                        sync.wait_ge(s_ap, st_c + 1)
                        sync.dma_start(out=xn1b_h[st_c * 128:(st_c + 1) * 128, :],
                                       in_=b_xn1[st_c % 2][:, :]).then_inc(s_x1s[st_c % 2], 16)
                        if st_c + 1 >= RPQ * (nxt_q[0] + 1) and nxt_q[0] < NQ:
                            xt_round(nxt_q[0])
                            nxt_q[0] += 1
                for st_c in range(max(0, TQ - LAG), TQ):
                    sync.wait_ge(s_ap, st_c + 1)
                    sync.dma_start(out=xn1b_h[st_c * 128:(st_c + 1) * 128, :],
                                   in_=b_xn1[st_c % 2][:, :]).then_inc(s_x1s[st_c % 2], 16)
                    if st_c + 1 >= RPQ * (nxt_q[0] + 1) and nxt_q[0] < NQ:
                        xt_round(nxt_q[0])
                        nxt_q[0] += 1
                while nxt_q[0] < NQ:
                    xt_round(nxt_q[0])
                    nxt_q[0] += 1

            @block.gpsimd
            def _(gpsimd):
                lg = [0]; lu = [0]; lw = [0]

                def g_tile():
                    ig = lg[0]
                    gb, dc = divmod(ig, DC)
                    sw, hb = divmod(gb, NHB)
                    if ig >= RG:
                        gpsimd.wait_ge(s_gdq, ig - RG + 1)
                    gpsimd.dma_start(out=b_wraw_g[ig % RG][:, :],
                                     in_=wqg_t[dc * 128:(dc + 1) * 128,
                                               hb * HB:(hb + 1) * HB]).then_inc(s_rg[ig % RG], 16)
                    lg[0] += 1

                def u_tile():
                    iu = lu[0]
                    gb, dc = divmod(iu, DC)
                    sw, hb = divmod(gb, NHB)
                    if iu >= RU:
                        gpsimd.wait_ge(s_udq, iu - RU + 1)
                    gpsimd.dma_start(out=b_wraw_u[iu % RU][:, :],
                                     in_=wqu_t[dc * 128:(dc + 1) * 128,
                                               hb * HB:(hb + 1) * HB]).then_inc(s_ru[iu % RU], 16)
                    lu[0] += 1

                def w_tile():
                    k = lw[0]
                    sw, r = divmod(k, NDS * HC)
                    ds, hc = divmod(r, HC)
                    if k >= RD:
                        gpsimd.wait_ge(s_dwq, k - RD + 1)
                    gpsimd.dma_start(out=b_wdraw[k % RD][:, :],
                                     in_=wqd_t[hc * 128:(hc + 1) * 128,
                                               ds * DSL:(ds + 1) * DSL]).then_inc(s_rd[k % RD], 16)
                    lw[0] += 1

                NWDK = NSW * NDS * HC

                def g_blk():
                    if lg[0] < NGB * DC:
                        for _ in range(DC):
                            g_tile()

                def u_blk():
                    if lu[0] < NGB * DC:
                        for _ in range(DC):
                            u_tile()

                g_blk(); g_blk(); u_blk()
                for gb in range(NHB):
                    u_blk()
                    g_blk()
                while lw[0] < NDS * HC:
                    w_tile()
                for gb in range(NHB, 2 * NHB):
                    u_blk()
                    g_blk()
                while lw[0] < NWDK:
                    w_tile()


            @block.vector
            def _(vector):
                vector.wait_ge(s_pp, 16 * 10)
                vector.tensor_copy(out=b_wabT[:, :], in_=b_wabF[:, :]).then_inc(s_ppc, 1)
                vector.tensor_copy(out=b_woT[:, :], in_=b_woF[:, :]).then_inc(s_ppc, 1)
                vector.tensor_copy(out=b_bout_b[:, :], in_=b_boutF[:, :]).then_inc(s_ppc, 1)
                vector.tensor_copy(out=b_ones_b[:, :], in_=b_onesF[:, :]).then_inc(s_ppc, 1)
                vector.memset(b_zero[:, :], 0.0).then_inc(s_ppc, 1)
                vector.memset(b_one[:, :], 1.0).then_inc(s_ppc, 1)
                vector.memset(b_nhalf[:, :], -0.5).then_inc(s_ppc, 1)
                vector.memset(b_neg128[:, :], -128.0).then_inc(s_ppc, 1)
                vector.wait_ge(s_sdp, 1)
                vector.tensor_copy(out=b_sdbc[:, :], in_=pall[:, 0:D]).then_inc(s_ppc, 1)

                for bi in range(TQ // FSTB):
                    lo = bi * FSTB; hi = lo + FSTB; sl = slice(lo, hi)
                    for tc in range(lo, hi):
                        vector.wait_ge(s_xld[tc % NXB], 16 * (tc // NXB + 1))
                        vector.tensor_reduce(out=b_sum1[:, tc:tc + 1], in_=b_x[tc % NXB][:, :],
                                             axis=AX.X, op=ALU.add).then_inc(s_red, 1)
                        vector.scalar_tensor_tensor(out=b_sqs[:, :], in0=b_x[tc % NXB][:, :],
                                                    scalar=1.0, in1=b_x[tc % NXB][:, :],
                                                    op0=ALU.mult, op1=ALU.mult,
                                                    accum_out=b_ssq1[:, tc:tc + 1]).then_inc(s_ssq, 1)
                    vector.wait_ge(s_red, hi)
                    vector.wait_ge(s_ssq, hi)
                    fin_A(vector, b_sum1, b_ssq1, b_m1, b_tq1, b_tq2, sl).then_inc(s_fa, 1)
                    vector.wait_ge(s_fb, bi + 1)
                    fin_B(vector, b_tq2, b_m1, b_r1, b_mr1, sl).then_inc(s_f1v, 1)
                    vector.wait_ge(s_f1v, bi + 1)
                    for tc in range(lo, hi):
                        if tc >= 2:
                            vector.wait_ge(s_x1s[tc % 2], 16 * ((tc - 2) // 2 + 1))
                        vector.tensor_scalar(out=b_xn1[tc % 2][:, :], in0=b_x[tc % NXB][:, :],
                                             scalar1=b_r1[:, tc:tc + 1], scalar2=b_mr1[:, tc:tc + 1],
                                             op0=ALU.mult, op1=ALU.subtract).then_inc(s_ap, 1)

                vector.wait_ge(s_th, 2 * NPJ)
                vector.wait_ge(s_ssq, TQ)
                vector.tensor_tensor_scan(out=b_hs[:, :], data0=b_ab[:, 0:TSEQ],
                                          data1=b_ab[:, TSEQ:2 * TSEQ],
                                          initial=b_zero[:, :], op0=ALU.mult, op1=ALU.add).then_inc(s_hs, 1)
                vector.wait_ge(s_hs, 1)
                vector.tensor_scalar(out=b_ab[0:S, 0:TOWN], in0=b_hs[:, 0:TOWN],
                                     scalar1=b_sel0[0:S, :], scalar2=0.0,
                                     op0=ALU.mult, op1=ALU.subtract).then_inc(s_f1v, 1)
                vector.wait_ge(s_f1v, TQ // FSTB + 1)
                vector.scalar_tensor_tensor(out=b_hsb[:, :], in0=b_hs[:, TSEQ - TOWN:TSEQ],
                                            scalar=b_sel1[0:S, :], in1=b_ab[0:S, 0:TOWN],
                                            op0=ALU.mult, op1=ALU.add).then_inc(s_hs, 1)
                vector.tensor_scalar(out=b_tc1[:, :], in0=b_r1[:, 0:TC], scalar1=b_sel0[:, :],
                                     scalar2=0.0, op0=ALU.mult, op1=ALU.subtract).then_inc(s_f1v, 1)
                vector.wait_ge(s_f1v, TQ // FSTB + 2)
                vector.scalar_tensor_tensor(out=b_r1o[:, :], in0=b_r1[:, TQ - TC:TQ],
                                            scalar=b_sel1[:, :], in1=b_tc1[:, :],
                                            op0=ALU.mult, op1=ALU.add).then_inc(s_f1v, 1)
                vector.wait_ge(s_f1v, TQ // FSTB + 3)
                vector.tensor_scalar(out=b_tc1[:, :], in0=b_mr1[:, 0:TC], scalar1=b_sel0[:, :],
                                     scalar2=0.0, op0=ALU.mult, op1=ALU.subtract).then_inc(s_f1v, 1)
                vector.wait_ge(s_f1v, TQ // FSTB + 4)
                vector.scalar_tensor_tensor(out=b_mr1o[:, :], in0=b_mr1[:, TQ - TC:TQ],
                                            scalar=b_sel1[:, :], in1=b_tc1[:, :],
                                            op0=ALU.mult, op1=ALU.add).then_inc(s_hs, 1)

            @block.scalar
            def _(scalar):
                def tanh_blk(k):
                    scalar.wait_ge(s_abp, k + 1)
                    scalar.activation(out=b_ab[:, k * PJB:(k + 1) * PJB],
                                      in_=pall[0:S, (k % 2) * 512:(k % 2) * 512 + PJB],
                                      func=AF.Tanh, bias=b_abbias[:, 0:1]).then_inc(s_th, 1)
                    scalar.activation(out=b_ab[:, TSEQ + k * PJB:TSEQ + (k + 1) * PJB],
                                      in_=pall[0:S, 1024 + (k % 2) * 512:1024 + (k % 2) * 512 + PJB],
                                      func=AF.Tanh, bias=b_abbias[:, 1:2]).then_inc(s_th, 1)

                nxt_k = [0]
                for bi in range(TQ // FSTB):
                    scalar.wait_ge(s_fa, bi + 1)
                    fin_act(scalar, b_tq2, b_r1, slice(bi * FSTB, (bi + 1) * FSTB)).then_inc(s_fb, 1)
                    while (nxt_k[0] < NPJ
                           and RPQ * (nxt_k[0] // NPJQ + 1) <= FSTB * bi):
                        tanh_blk(nxt_k[0])
                        nxt_k[0] += 1
                while nxt_k[0] < NPJ:
                    tanh_blk(nxt_k[0])
                    nxt_k[0] += 1

            @block.tensor
            def _(tensor):
                tensor.wait_ge(s_pp, 16 * 10)
                for j in range(D // JW):
                    mm = tensor.matmul(pall[:, j * JW:(j + 1) * JW], b_onesF[:, :],
                                       b_sdrF[:, j * JW:(j + 1) * JW], start=True, stop=True)
                mm.then_inc(s_sdp, 1)
                tensor.wait_ge(s_ppc, 9)
                for q in range(NQ):
                    tensor.wait_ge(s_xt, 16 * DC * (q + 1))
                    for pb in range(NPJQ):
                        k = q * NPJQ + pb
                        if k >= 2:
                            tensor.wait_ge(s_th, 2 * (k - 1))
                        po_a = (k % 2) * 512
                        po_b = 1024 + (k % 2) * 512
                        xbuf = (q % 2) * DC
                        for dc in range(DC):
                            tensor.matmul(pall[0:S, po_a:po_a + PJB],
                                          b_wabT[:, dc * S2:dc * S2 + S],
                                          b_xt[xbuf + dc][:, pb * PJB:(pb + 1) * PJB],
                                          start=(dc == 0), stop=(dc == DC - 1))
                            mm = tensor.matmul(pall[0:S, po_b:po_b + PJB],
                                               b_wabT[:, dc * S2 + S:dc * S2 + S2],
                                               b_xt[xbuf + dc][:, pb * PJB:(pb + 1) * PJB],
                                               start=(dc == 0), stop=(dc == DC - 1))
                        mm.then_inc(s_abp, 1)

        B1 = contextlib.ExitStack()
        b1 = B1.enter_context
        with B1:
            b_xo = [b1(nc.sbuf_tensor(f"b_xo{i}", [128, D], F32)) for i in range(NOB)]
            b_txn = [b1(nc.sbuf_tensor(f"b_txn{i}", [128, D], F32)) for i in range(1)]
            b_y1 = [b1(nc.sbuf_tensor(f"b_y1{i}", [128, D], F32)) for i in range(3)]
            b_t2 = [b1(nc.sbuf_tensor(f"b_t2{i}", [128, D], F32)) for i in range(1)]
            b_x2 = [b1(nc.sbuf_tensor(f"b_x2{i}", [128, D], F32)) for i in range(3)]
            b_xn2 = [b1(nc.sbuf_tensor(f"b_xn2{i}", [128, D], BF16)) for i in range(2)]

            @block.sync
            def _(sync):
                sync.wait_ge(s_hs, 3)
                sync.wait_ge(s_ap, TQ)
                sync.wait_ge(s_ssq, TQ)
                sync.wait_ge(s_x1s[0], 16 * _cnt(TQ, 0, 2))
                sync.wait_ge(s_x1s[1], 16 * _cnt(TQ, 1, 2))
                sync.wait_ge(s_abp, NPJ)
                for oc in range(TC):
                    if oc >= NOB:
                        sync.wait_ge(s_x2, oc - NOB + 1)
                    sync.dma_start(out=b_xo[oc % NOB][:, :],
                                   in_=x_own[oc * 128:(oc + 1) * 128, :]).then_inc(s_xo[oc % NOB], 16)
                    if oc >= 2:
                        oc2 = oc - 2
                        sync.wait_ge(s_x2, oc2 + 1)
                        sync.dma_start(out=x2_h[oc2 * 128:(oc2 + 1) * 128, :],
                                       in_=b_x2[oc2 % 3][:, :]).then_inc(s_x2s[oc2 % 3], 16)
                        sync.wait_ge(s_xn2, oc2 + 1)
                        sync.dma_start(out=xn2b_h[oc2 * 128:(oc2 + 1) * 128, :],
                                       in_=b_xn2[oc2 % 2][:, :]).then_inc(s_n2s[oc2 % 2], 16)
                for oc2 in range(TC - 2, TC):
                    sync.wait_ge(s_x2, oc2 + 1)
                    sync.dma_start(out=x2_h[oc2 * 128:(oc2 + 1) * 128, :],
                                   in_=b_x2[oc2 % 3][:, :]).then_inc(s_x2s[oc2 % 3], 16)
                    sync.wait_ge(s_xn2, oc2 + 1)
                    sync.dma_start(out=xn2b_h[oc2 * 128:(oc2 + 1) * 128, :],
                                   in_=b_xn2[oc2 % 2][:, :]).then_inc(s_n2s[oc2 % 2], 16)

            @block.vector
            def _(vector):
                for bi in range(TC // OSTB):
                    lo = bi * OSTB; hi = lo + OSTB; sl = slice(lo, hi)
                    for oc in range(lo, hi):
                        vector.wait_ge(s_xo[oc % NOB], 16 * (oc // NOB + 1))
                        vector.wait_ge(s_wps, oc + 1)
                        if oc >= 3:
                            vector.wait_ge(s_sq2, oc - 2)
                        vector.tensor_scalar(out=b_txn[0][:, :], in0=b_xo[oc % NOB][:, :],
                                             scalar1=b_r1o[:, oc:oc + 1], scalar2=b_mr1o[:, oc:oc + 1],
                                             op0=ALU.mult, op1=ALU.subtract)
                        vector.scalar_tensor_tensor(out=b_y1[oc % 3][:, :], in0=b_txn[0][:, :],
                                                    scalar=1.0, in1=pall[:, 0:D],
                                                    op0=ALU.mult, op1=ALU.add,
                                                    accum_out=b_sum2[:, oc:oc + 1]).then_inc(s_y1, 1)
                    vector.wait_ge(s_sq2, hi)
                    vector.wait_ge(s_y1, hi)
                    fin_A(vector, b_sum2, b_ssq2, b_m2, b_tc1, b_tc2, sl).then_inc(s_fa2, 1)
                    vector.wait_ge(s_fb2, 2 * bi + 1)
                    dfence(vector, fin_B(vector, b_tc2, b_m2, b_r2, b_mr2, sl))
                    for oc in range(lo, hi):
                        if oc >= 3:
                            vector.wait_ge(s_x2s[(oc - 3) % 3], 16 * ((oc - 3) // 3 + 1))
                            vector.wait_ge(s_sq3, oc - 2)
                        vector.tensor_scalar(out=b_t2[0][:, :], in0=b_y1[oc % 3][:, :],
                                             scalar1=b_r2[:, oc:oc + 1], scalar2=b_mr2[:, oc:oc + 1],
                                             op0=ALU.mult, op1=ALU.subtract)
                        vector.scalar_tensor_tensor(out=b_x2[oc % 3][:, :], in0=b_t2[0][:, :],
                                                    scalar=1.0, in1=b_xo[oc % NOB][:, :],
                                                    op0=ALU.mult, op1=ALU.add,
                                                    accum_out=b_sum3[:, oc:oc + 1]).then_inc(s_x2, 1)
                    vector.wait_ge(s_sq3, hi)
                    vector.wait_ge(s_x2, hi)
                    fin_A(vector, b_sum3, b_ssq3, b_m3, b_tc1, b_tc2, sl).then_inc(s_fa2, 1)
                    vector.wait_ge(s_fb2, 2 * bi + 2)
                    dfence(vector, fin_B(vector, b_tc2, b_m3, b_r3, b_mr3, sl))
                    for oc in range(lo, hi):
                        if oc >= 2:
                            vector.wait_ge(s_n2s[oc % 2], 16 * ((oc - 2) // 2 + 1))
                        vector.tensor_scalar(out=b_xn2[oc % 2][:, :], in0=b_x2[oc % 3][:, :],
                                             scalar1=b_r3[:, oc:oc + 1], scalar2=b_mr3[:, oc:oc + 1],
                                             op0=ALU.mult, op1=ALU.subtract).then_inc(s_xn2, 1)

            @block.scalar
            def _(scalar):
                for bi in range(TC // OSTB):
                    lo = bi * OSTB; hi = lo + OSTB
                    for oc in range(lo, hi):
                        scalar.wait_ge(s_y1, oc + 1)
                        scalar.activation(out=b_sqs[:, :], in_=b_y1[oc % 3][:, :], func=AF.Square,
                                          accum_out=b_ssq2[:, oc:oc + 1]).then_inc(s_sq2, 1)
                    scalar.wait_ge(s_fa2, 2 * bi + 1)
                    fin_act(scalar, b_tc2, b_r2, slice(lo, hi)).then_inc(s_fb2, 1)
                    for oc in range(lo, hi):
                        scalar.wait_ge(s_x2, oc + 1)
                        scalar.activation(out=b_sqs[:, :], in_=b_x2[oc % 3][:, :], func=AF.Square,
                                          accum_out=b_ssq3[:, oc:oc + 1]).then_inc(s_sq3, 1)
                    scalar.wait_ge(s_fa2, 2 * bi + 2)
                    fin_act(scalar, b_tc2, b_r3, slice(lo, hi)).then_inc(s_fb2, 1)

            @block.tensor
            def _(tensor):
                for oc in range(TC):
                    tensor.wait_ge(s_hs, 2)
                    if oc >= 1:
                        tensor.wait_ge(s_y1, oc)
                    for j in range(NJW):
                        tensor.matmul(pall[:, j * JW:(j + 1) * JW],
                                      b_hsb[:, oc * 128:(oc + 1) * 128],
                                      b_woT[:, j * JW:(j + 1) * JW],
                                      start=True, stop=False)
                        mm = tensor.matmul(pall[:, j * JW:(j + 1) * JW],
                                           b_ones_b[:, :], b_bout_b[:, j * JW:(j + 1) * JW],
                                           start=False, stop=True)
                    mm.then_inc(s_wps, 1)

        B2 = contextlib.ExitStack()
        b2 = B2.enter_context
        with B2:
            b_xn2T = [b2(nc.sbuf_tensor(f"b_xn2T{i}", [128, TSL], BF16)) for i in range(DC)]
            b_wg = [b2(nc.sbuf_tensor(f"b_wg{i}", [128, HB], BF16)) for i in range(2 * DC)]
            b_wu = [b2(nc.sbuf_tensor(f"b_wu{i}", [128, HB], BF16)) for i in range(DC)]
            b_gt_ = [b2(nc.sbuf_tensor(f"b_gt{i}", [128, TSL], BF16)) for i in range(2)]
            b_ut = [b2(nc.sbuf_tensor(f"b_ut{i}", [128, TSL], BF16)) for i in range(2)]
            b_pw = [b2(nc.sbuf_tensor(f"b_pw{i}", [128, TSL], BF16)) for i in range(2)]
            b_pst = [b2(nc.sbuf_tensor(f"b_pst{i}", [128, TSL], BF16)) for i in range(NPS)]
            b_wd = [b2(nc.sbuf_tensor(f"b_wd{i}", [128, DSL], BF16)) for i in range(NWD)]
            b_f = [b2(nc.sbuf_tensor(f"b_f{i}", [128, D], BF16)) for i in range(TCS)]
            b_x2r = [b2(nc.sbuf_tensor(f"b_x2r{i}", [128, D], F32)) for i in range(2)]

            @block.sync
            def _(sync):
                for j in range(3):
                    sync.wait_ge(s_x2s[j], 16 * _cnt(TC, j, 3))
                for j in range(2):
                    sync.wait_ge(s_n2s[j], 16 * _cnt(TC, j, 2))
                sync.wait_ge(s_sq2, TC)
                sync.wait_ge(s_sq3, TC)
                sync.wait_ge(s_xn2, TC)
                for sw in range(NSW):
                    if sw > 0:
                        sync.wait_ge(s_gmm, 2 * HC * sw)
                    for dc in range(DC):
                        sync.dma_start(out=b_xn2T[dc][:, :],
                                       in_=xn2b_h[sw * TSL:(sw + 1) * TSL, dc * 128:(dc + 1) * 128],
                                       transpose=True).then_inc(s_n2t, 16)
                    def emit_prod_store(hc):
                        ghc = sw * HC + hc
                        sync.wait_ge(s_pr, ghc + 1)
                        sync.dma_start(out=prod_h[hc * 128:(hc + 1) * 128, :],
                                       in_=b_pw[ghc % 2][:, :]).then_inc(s_prs[ghc % 2], 16)

                    for hc in range(HC):
                        emit_prod_store(hc)
                    for ds in range(NDS):
                        for hc in range(HC):
                            k = sw * NDS * HC + ds * HC + hc
                            ghc = sw * HC + hc
                            sync.wait_ge(s_prs[ghc % 2], 16 * (ghc // 2 + 1))
                            if k >= NPS:
                                sync.wait_ge(s_dmm, k - NPS + 1)
                            sync.dma_start(out=b_pst[k % NPS][:, :],
                                           in_=prod_h[hc * 128:(hc + 1) * 128, :]).then_inc(s_psl[k % NPS], 16)
                    for tc in range(TCS):
                        gtc = sw * TCS + tc
                        if gtc >= 2:
                            sync.wait_ge(s_ys[gtc % 2], 16 * ((gtc - 2) // 2 + 1))
                        sync.dma_start(out=b_x2r[gtc % 2][:, :],
                                       in_=x2_h[gtc * 128:(gtc + 1) * 128, :]).then_inc(s_x2r[gtc % 2], 16)
                        sync.wait_ge(s_yo, gtc + 1)
                        sync.dma_start(out=y_out[gtc * 128:(gtc + 1) * 128, :],
                                       in_=b_x2r[gtc % 2][:, :]).then_inc(s_ys[gtc % 2], 16)
                sync.wait_ge(s_ys[0], 16 * _cnt(TC, 0, 2))
                sync.wait_ge(s_ys[1], 16 * _cnt(TC, 1, 2))
                if dbg_st is not None:
                    for i, t in enumerate([b_sum1, b_ssq1, b_m1, b_r1, b_mr1, b_tq2]):
                        sync.dma_start(out=dbg_st[i * 128:(i + 1) * 128, :],
                                       in_=t[:, :]).then_inc(s_pp, 16)
                    sync.wait_ge(s_pp, 16 * 16)

            @block.tensor
            def _(tensor):
                for sw in range(NSW):
                    tensor.wait_ge(s_n2t, 16 * DC * (sw + 1))
                    for hc in range(HC):
                        hb, hcc = divmod(hc, HCB)
                        tensor.wait_ge(s_gdq, DC * (sw * NHB + hb + 1))
                        if hc >= 2:
                            tensor.wait_ge(s_gev, 2 * HC * sw + 2 * (hc - 2) + 1)
                        elif sw > 0:
                            tensor.wait_ge(s_dev, sw * NDS * TCS)
                        po = (4 + 2 * (hc % 2)) * 512
                        gbuf = ((sw * NHB + hb) % 2) * DC
                        for dc in range(DC):
                            for j in range(NJ5):
                                mm = tensor.matmul(pall[:, po + j * T5:po + (j + 1) * T5],
                                                   b_wg[gbuf + dc][:, hcc * 128:(hcc + 1) * 128],
                                                   b_xn2T[dc][:, j * T5:(j + 1) * T5],
                                                   start=(dc == 0), stop=(dc == DC - 1))
                        mm.then_inc(s_gmm, 1)
                        tensor.wait_ge(s_udq, DC * (sw * NHB + hb + 1))
                        if hc >= 2:
                            tensor.wait_ge(s_gev, 2 * HC * sw + 2 * (hc - 2) + 2)
                        elif sw == 0:
                            tensor.wait_ge(s_y1, TC)
                        else:
                            tensor.wait_ge(s_dev, sw * NDS * TCS)
                        po = (2 * (hc % 2)) * 512
                        for dc in range(DC):
                            for j in range(NJ5):
                                mm = tensor.matmul(pall[:, po + j * T5:po + (j + 1) * T5],
                                                   b_wu[dc][:, hcc * 128:(hcc + 1) * 128],
                                                   b_xn2T[dc][:, j * T5:(j + 1) * T5],
                                                   start=(dc == 0), stop=(dc == DC - 1))
                        mm.then_inc(s_gmm, 1)
                    for ds in range(NDS):
                        if ds == 0:
                            tensor.wait_ge(s_gev, 2 * HC * (sw + 1))
                        else:
                            tensor.wait_ge(s_dev, (sw * NDS + ds) * TCS)
                        for hc in range(HC):
                            k = sw * NDS * HC + ds * HC + hc
                            tensor.wait_ge(s_psl[k % NPS], 16 * (k // NPS + 1))
                            tensor.wait_ge(s_dwq, k + 1)
                            for tc in range(TCS):
                                mm = tensor.matmul(pall[:, tc * 512:tc * 512 + DSL],
                                                   b_pst[k % NPS][:, tc * 128:(tc + 1) * 128],
                                                   b_wd[k % NWD][:, :],
                                                   start=(hc == 0), stop=(hc == HC - 1))
                            mm.then_inc(s_dmm, 1)

            @block.scalar
            def _(scalar):
                emit_udq(scalar, DC)
                for sw in range(NSW):
                    for hc in range(HC):
                        ghc = sw * HC + hc
                        gb = sw * NHB + hc // HCB
                        hcc = hc % HCB
                        scalar.wait_ge(s_gmm, 2 * HC * sw + 2 * hc + 1)
                        if ghc >= 2:
                            scalar.wait_ge(s_pr, ghc - 1)
                        po = (4 + 2 * (hc % 2)) * 512
                        scalar.activation(out=b_gt_[hc % 2][:, :], in_=pall[:, po:po + TSL],
                                          func=AF.Silu,
                                          scale=b_sg[:, hc:hc + 1]).then_inc(s_gev, 1)
                        scalar.wait_ge(s_gmm, 2 * HC * sw + 2 * hc + 2)
                        po = (2 * (hc % 2)) * 512
                        scalar.activation(out=b_ut[hc % 2][:, :], in_=pall[:, po:po + TSL],
                                          func=AF.Identity,
                                          scale=b_su[:, hc:hc + 1]).then_inc(s_gev, 1)
                        if hcc == HCB - 1:
                            emit_udq(scalar, DC * (gb + 2))
                    for ds in range(NDS):
                        for tc in range(TCS):
                            scalar.wait_ge(s_dmm, sw * NDS * HC + ds * HC + HC)
                            if sw > 0 and ds == 0:
                                scalar.wait_ge(s_yo, sw * TCS)
                            scalar.activation(out=b_f[tc][:, ds * DSL:(ds + 1) * DSL],
                                              in_=pall[:, tc * 512:tc * 512 + DSL],
                                              func=AF.Copy).then_inc(s_dev, 1)
                    for tc in range(TCS):
                        gtc = sw * TCS + tc
                        scalar.activation(out=b_sqs[:, :], in_=b_f[tc][:, :], func=AF.Copy,
                                          accum_out=b_sum4[:, gtc:gtc + 1]).then_inc(s_sq4, 1)
                        scalar.activation(out=b_sqs[:, :], in_=b_f[tc][:, :], func=AF.Square,
                                          accum_out=b_ssq4[:, gtc:gtc + 1]).then_inc(s_sq4, 1)
                    scalar.wait_ge(s_fa4, sw + 1)
                    fin_act(scalar, b_tc2, b_r4, slice(sw * TCS, (sw + 1) * TCS)).then_inc(s_fb4, 1)

            @block.vector
            def _(vector):
                cw_i = [0]

                def emit_wdq(lim):
                    while cw_i[0] < min(lim, NSW * NDS * HC):
                        k = cw_i[0]
                        swk, r = divmod(k, NDS * HC)
                        ds = r // HC
                        vector.wait_ge(s_rd[k % RD], 16 * (k // RD + 1))
                        if k >= NWD:
                            vector.wait_ge(s_dmm, k - NWD + 1)
                        vector.scalar_tensor_tensor(out=b_wd[k % NWD][:, :],
                                                    in0=b_wdraw[k % RD][:, :],
                                                    scalar=128.0,
                                                    in1=b_sdbc[:, ds * DSL:(ds + 1) * DSL],
                                                    op0=ALU.subtract,
                                                    op1=ALU.mult).then_inc(s_dwq, 1)
                        cw_i[0] += 1

                emit_gdq(vector, 2 * DC)
                for sw in range(NSW):
                    for hc in range(HC):
                        ghc = sw * HC + hc
                        gb = sw * NHB + hc // HCB
                        hcc = hc % HCB
                        vector.wait_ge(s_gev, 2 * HC * sw + 2 * (hc + 1))
                        if ghc >= 2:
                            vector.wait_ge(s_prs[ghc % 2], 16 * ((ghc - 2) // 2 + 1))
                        vector.tensor_tensor(out=b_pw[ghc % 2][:, :], in0=b_gt_[hc % 2][:, :],
                                             in1=b_ut[hc % 2][:, :], op=ALU.mult).then_inc(s_pr, 1)
                        if hcc == HCB - 1:
                            emit_gdq(vector, DC * (gb + 3))
                    emit_wdq((sw + 1) * NDS * HC)
                    sl = slice(sw * TCS, (sw + 1) * TCS)
                    vector.wait_ge(s_sq4, 2 * (sw + 1) * TCS)
                    fin_A(vector, b_sum4, b_ssq4, b_m4, b_tc1, b_tc2, sl).then_inc(s_fa4, 1)
                    vector.wait_ge(s_fb4, sw + 1)
                    dfence(vector, fin_B(vector, b_tc2, b_m4, b_r4, b_mr4, sl))
                    for tc in range(TCS):
                        gtc = sw * TCS + tc
                        vector.wait_ge(s_x2r[gtc % 2], 16 * (gtc // 2 + 1))
                        vector.tensor_scalar(out=b_x2r[gtc % 2][:, :], in0=b_x2r[gtc % 2][:, :],
                                             scalar1=1.0, scalar2=b_mr4[:, gtc:gtc + 1],
                                             op0=ALU.mult, op1=ALU.subtract)
                        vector.scalar_tensor_tensor(out=b_x2r[gtc % 2][:, :], in0=b_f[tc][:, :],
                                                    scalar=b_r4[:, gtc:gtc + 1],
                                                    in1=b_x2r[gtc % 2][:, :],
                                                    op0=ALU.mult, op1=ALU.add).then_inc(s_yo, 1)

    return nc


def kernel(L0_attn_x, L0_attn_Wa, L0_attn_ba, L0_attn_Wb, L0_attn_bb,
           L0_attn_Wout, L0_attn_bout, L0_ffn_wq_gate, L0_ffn_s_gate,
           L0_ffn_wq_up, L0_ffn_s_up, L0_ffn_wq_down, L0_ffn_s_down,
           _cfg=None, _n_cores=8):
    cfg = dict(FULL_CFG if _cfg is None else _cfg)
    D = cfg["D"]; TSEQ = cfg["TSEQ"]; TOWN = cfg["TOWN"]
    HC = cfg["H"] // 128; DC = D // 128
    x = np.ascontiguousarray(np.asarray(L0_attn_x, dtype=np.float32)).reshape(-1, TSEQ, D)
    B = x.shape[0]
    n_cores = _n_cores

    wab_t = np.ascontiguousarray(np.concatenate([np.asarray(L0_attn_Wa, np.float32),
                                                 np.asarray(L0_attn_Wb, np.float32)], axis=0).T)
    ab_bias = np.ascontiguousarray(np.stack([np.asarray(L0_attn_ba, np.float32),
                                             np.asarray(L0_attn_bb, np.float32)], axis=1))
    wout_t = np.ascontiguousarray(np.asarray(L0_attn_Wout, np.float32).T)
    bout_row = np.ascontiguousarray(np.asarray(L0_attn_bout, np.float32)[None, :])
    wqg_t = np.ascontiguousarray(np.asarray(L0_ffn_wq_gate, np.int32).T)
    wqu_t = np.ascontiguousarray(np.asarray(L0_ffn_wq_up, np.int32).T)
    wqd_t = np.ascontiguousarray(np.asarray(L0_ffn_wq_down, np.int32).T)
    sg_t = np.ascontiguousarray(np.asarray(L0_ffn_s_gate, np.float32).reshape(HC, 128).T)
    su_t = np.ascontiguousarray(np.asarray(L0_ffn_s_up, np.float32).reshape(HC, 128).T)
    sd_row = np.ascontiguousarray(np.asarray(L0_ffn_s_down, np.float32)[None, :])
    ones_r = np.ones((1, 128), np.float32)

    nc = build_nc(cfg)
    in_maps = []
    for c in range(n_cores):
        b, half = divmod(c, 2)
        in_maps.append(dict(
            x_seq=np.ascontiguousarray(x[b]),
            x_own=np.ascontiguousarray(x[b, half * TOWN:(half + 1) * TOWN]),
            sel0=np.full((128, 1), 1.0 - half, np.float32),
            sel1=np.full((128, 1), float(half), np.float32),
            wab_t=wab_t, ab_bias=ab_bias, wout_t=wout_t, bout_row=bout_row,
            wqg_t=wqg_t, wqu_t=wqu_t, wqd_t=wqd_t, sg_t=sg_t, su_t=su_t, sd_row=sd_row,
            ones_r=ones_r,
        ))
    res = run_bass_kernel_spmd(nc, in_maps, core_ids=list(range(n_cores)))
    out = np.concatenate([res.results[c]["y_out"] for c in range(n_cores)], axis=0)
    return out.reshape(B, TSEQ, D)


# version 14
# speedup vs baseline: 1.2642x; 1.0117x over previous
import contextlib
import numpy as np

import concourse.bass as bass
import concourse.mybir as mybir
from concourse.bass_utils import run_bass_kernel_spmd

F32 = mybir.dt.float32
BF16 = mybir.dt.bfloat16
I32 = mybir.dt.int32
AF = mybir.ActivationFunctionType
ALU = mybir.AluOpType
AX = mybir.AxisListType

FULL_CFG = dict(D=2048, S=64, TSEQ=4096, TOWN=2048, H=5632, NSW=2, EPS=1e-5)


def _cnt(K, j, NB):
    return (K - j + NB - 1) // NB


def build_nc(cfg, dbg=False):
    D = cfg["D"]; S = cfg["S"]; TSEQ = cfg["TSEQ"]; TOWN = cfg["TOWN"]
    H = cfg["H"]; NSW = cfg["NSW"]; EPS = cfg["EPS"]
    DC = D // 128
    TQ = TSEQ // 128
    TC = TOWN // 128
    HC = H // 128
    TSL = TOWN // NSW
    TCS = TSL // 128
    S2 = 2 * S
    QTB = min(512, TSEQ // 2)
    NQ = TSEQ // QTB
    RPQ = QTB // 128
    PJB = min(512, QTB)
    NPJQ = QTB // PJB
    NPJ = NQ * NPJQ
    HB = min(512, H)
    NHB = H // HB
    HCB = HB // 128
    DSL = min(512, D)
    NDS = D // DSL
    T5 = min(512, TSL)
    NJ5 = TSL // T5
    JW = min(512, D)
    NJW = D // JW
    FSTB = 4 if TQ % 4 == 0 else 1
    OSTB = 2 if TC % 2 == 0 else 1
    NXB = 5
    NOB = 3
    NWD = 4
    NPS = 4
    IVD = 1.0 / D

    nc = bass.Bass("TRN2", target_bir_lowering=False, detect_race_conditions=False)

    x_seq = nc.dram_tensor("x_seq", [TSEQ, D], F32, kind="ExternalInput").ap()
    x_own = nc.dram_tensor("x_own", [TOWN, D], F32, kind="ExternalInput").ap()
    sel0 = nc.dram_tensor("sel0", [128, 1], F32, kind="ExternalInput").ap()
    sel1 = nc.dram_tensor("sel1", [128, 1], F32, kind="ExternalInput").ap()
    wab_t = nc.dram_tensor("wab_t", [D, S2], F32, kind="ExternalInput").ap()
    ab_bias = nc.dram_tensor("ab_bias", [S, 2], F32, kind="ExternalInput").ap()
    wout_t = nc.dram_tensor("wout_t", [S, D], F32, kind="ExternalInput").ap()
    bout_row = nc.dram_tensor("bout_row", [1, D], F32, kind="ExternalInput").ap()
    wqg_t = nc.dram_tensor("wqg_t", [D, H], I32, kind="ExternalInput").ap()
    wqu_t = nc.dram_tensor("wqu_t", [D, H], I32, kind="ExternalInput").ap()
    wqd_t = nc.dram_tensor("wqd_t", [H, D], I32, kind="ExternalInput").ap()
    sg_t = nc.dram_tensor("sg_t", [128, HC], F32, kind="ExternalInput").ap()
    su_t = nc.dram_tensor("su_t", [128, HC], F32, kind="ExternalInput").ap()
    sd_row = nc.dram_tensor("sd_row", [1, D], F32, kind="ExternalInput").ap()
    ones_r = nc.dram_tensor("ones_r", [1, 128], F32, kind="ExternalInput").ap()

    y_out = nc.dram_tensor("y_out", [TOWN, D], F32, kind="ExternalOutput").ap()

    KI = "ExternalOutput" if dbg else "Internal"
    xn1b_h = nc.dram_tensor("xn1b_h", [TSEQ, D], BF16, kind=KI).ap()
    xn2b_h = nc.dram_tensor("xn2b_h", [TOWN, D], BF16, kind=KI).ap()
    x2_h = nc.dram_tensor("x2_h", [TOWN, D], F32, kind=KI).ap()
    prod_h = nc.dram_tensor("prod_h", [H, TSL], BF16, kind=KI).ap()
    dbg_st = nc.dram_tensor("dbg_st", [6 * 128, TQ], F32, kind=KI).ap() if dbg else None

    G = contextlib.ExitStack()
    g = G.enter_context
    with G:
        b_wabT = g(nc.sbuf_tensor("b_wabT", [128, DC * S2], BF16))
        b_woT = g(nc.sbuf_tensor("b_woT", [S, D], BF16))
        b_abbias = g(nc.sbuf_tensor("b_abbias", [S, 2], F32))
        b_bout_b = g(nc.sbuf_tensor("b_bout_b", [1, D], BF16))
        b_ones_b = g(nc.sbuf_tensor("b_ones_b", [1, 128], BF16))
        b_sel0 = g(nc.sbuf_tensor("b_sel0", [128, 1], F32))
        b_sel1 = g(nc.sbuf_tensor("b_sel1", [128, 1], F32))
        b_sg = g(nc.sbuf_tensor("b_sg", [128, HC], F32))
        b_su = g(nc.sbuf_tensor("b_su", [128, HC], F32))
        b_neg128 = g(nc.sbuf_tensor("b_neg128", [128, 1], F32))
        b_sdbc = g(nc.sbuf_tensor("b_sdbc", [128, D], BF16))
        b_onesF = g(nc.sbuf_tensor("b_onesF", [1, 128], F32))
        b_zero = g(nc.sbuf_tensor("b_zero", [S, 1], F32))
        b_sum1 = g(nc.sbuf_tensor("b_sum1", [128, TQ], F32))
        b_ssq1 = g(nc.sbuf_tensor("b_ssq1", [128, TQ], F32))
        b_m1 = g(nc.sbuf_tensor("b_m1", [128, TQ], F32))
        b_r1 = g(nc.sbuf_tensor("b_r1", [128, TQ], F32))
        b_mr1 = g(nc.sbuf_tensor("b_mr1", [128, TQ], F32))
        b_tq1 = g(nc.sbuf_tensor("b_tq1", [128, TQ], F32))
        b_tq2 = g(nc.sbuf_tensor("b_tq2", [128, TQ], F32))
        b_r1o = g(nc.sbuf_tensor("b_r1o", [128, TC], F32))
        b_mr1o = g(nc.sbuf_tensor("b_mr1o", [128, TC], F32))
        b_sum2 = g(nc.sbuf_tensor("b_sum2", [128, TC], F32))
        b_ssq2 = g(nc.sbuf_tensor("b_ssq2", [128, TC], F32))
        b_m2 = g(nc.sbuf_tensor("b_m2", [128, TC], F32))
        b_r2 = g(nc.sbuf_tensor("b_r2", [128, TC], F32))
        b_mr2 = g(nc.sbuf_tensor("b_mr2", [128, TC], F32))
        b_sum3 = g(nc.sbuf_tensor("b_sum3", [128, TC], F32))
        b_ssq3 = g(nc.sbuf_tensor("b_ssq3", [128, TC], F32))
        b_m3 = g(nc.sbuf_tensor("b_m3", [128, TC], F32))
        b_r3 = g(nc.sbuf_tensor("b_r3", [128, TC], F32))
        b_mr3 = g(nc.sbuf_tensor("b_mr3", [128, TC], F32))
        b_sum4 = g(nc.sbuf_tensor("b_sum4", [128, TC], F32))
        b_ssq4 = g(nc.sbuf_tensor("b_ssq4", [128, TC], F32))
        b_m4 = g(nc.sbuf_tensor("b_m4", [128, TC], F32))
        b_r4 = g(nc.sbuf_tensor("b_r4", [128, TC], F32))
        b_mr4 = g(nc.sbuf_tensor("b_mr4", [128, TC], F32))
        b_tc1 = g(nc.sbuf_tensor("b_tc1", [128, TC], F32))
        b_tc2 = g(nc.sbuf_tensor("b_tc2", [128, TC], F32))
        b_sqs = g(nc.sbuf_tensor("b_sqs", [128, D], BF16))
        b_one = g(nc.sbuf_tensor("b_one", [128, max(TQ, TC)], F32))
        b_nhalf = g(nc.sbuf_tensor("b_nhalf", [128, 1], F32))
        b_hsb = g(nc.sbuf_tensor("b_hsb", [S, TOWN], BF16))
        RG = 8; RU = 8; RD = 8
        b_wraw_g = [g(nc.sbuf_tensor(f"b_wraw_g{i}", [128, HB], BF16)) for i in range(RG)]
        b_wraw_u = [g(nc.sbuf_tensor(f"b_wraw_u{i}", [128, HB], BF16)) for i in range(RU)]
        b_wdraw = [g(nc.sbuf_tensor(f"b_wdraw{i}", [128, DSL], BF16)) for i in range(RD)]
        pall = g(nc.psum_tensor("pall", [128, 8 * 512], F32))

        def S1_(name):
            return G.enter_context(nc.semaphore(name))

        def SN_(name, n):
            return [G.enter_context(nc.semaphore(f"{name}{i}")) for i in range(n)]

        s_ppc = S1_("s_ppc"); s_red = S1_("s_red"); s_ssq = S1_("s_ssq")
        s_fa = S1_("s_fa"); s_fb = S1_("s_fb"); s_ap = S1_("s_ap")
        s_abp = S1_("s_abp"); s_th = S1_("s_th"); s_hs = S1_("s_hs")
        s_wps = S1_("s_wps"); s_y1 = S1_("s_y1")
        s_sq2 = S1_("s_sq2"); s_x2 = S1_("s_x2")
        s_sq3 = S1_("s_sq3"); s_xn2 = S1_("s_xn2")
        s_fa2 = S1_("s_fa2"); s_fb2 = S1_("s_fb2")
        s_gdq = S1_("s_gdq"); s_udq = S1_("s_udq"); s_dwq = S1_("s_dwq")
        s_gmm = S1_("s_gmm"); s_gev = S1_("s_gev"); s_pr = S1_("s_pr")
        s_dmm = S1_("s_dmm"); s_dev = S1_("s_dev")
        s_sq4 = S1_("s_sq4"); s_fa4 = S1_("s_fa4"); s_fb4 = S1_("s_fb4")
        s_yo = S1_("s_yo"); s_sdp = S1_("s_sdp")
        s_f1v = S1_("s_f1v"); s_r3v = S1_("s_r3v")
        s_dvf = S1_("s_dvf"); s_acf = S1_("s_acf")
        dvf_n = [0]
        acf_n = [0]

        def dfence(vec, instr):
            instr.then_inc(s_dvf, 1)
            dvf_n[0] += 1
            vec.wait_ge(s_dvf, dvf_n[0])

        def afence(scal, instr):
            instr.then_inc(s_acf, 1)
            acf_n[0] += 1
            scal.wait_ge(s_acf, acf_n[0])
        s_pp = S1_("s_pp")
        s_xld = SN_("s_xld", NXB)
        s_x1s = SN_("s_x1s", 2)
        s_xt = S1_("s_xt")
        s_xo = SN_("s_xo", NOB)
        s_x2s = SN_("s_x2s", 3)
        s_n2s = SN_("s_n2s", 2)
        s_rg = SN_("s_rg", RG)
        s_ru = SN_("s_ru", RU)
        s_rd = SN_("s_rd", RD)
        s_n2t = S1_("s_n2t")
        s_prs = SN_("s_prs", 2)
        s_psl = SN_("s_psl", NPS)
        s_x2r = SN_("s_x2r", 2)
        s_ys = SN_("s_ys", 2)

        block = g(nc.Block())

        def fin_A(vec, sums, ssqs, m_t, tmpa, tmpb, sl):
            dfence(vec, vec.tensor_scalar(out=m_t[:, sl], in0=sums[:, sl], scalar1=IVD,
                                          scalar2=0.0, op0=ALU.mult, op1=ALU.subtract))
            dfence(vec, vec.tensor_tensor(out=tmpa[:, sl], in0=m_t[:, sl], in1=m_t[:, sl],
                                          op=ALU.mult))
            dfence(vec, vec.tensor_scalar(out=tmpa[:, sl], in0=tmpa[:, sl], scalar1=1.0,
                                          scalar2=EPS, op0=ALU.mult, op1=ALU.subtract))
            return vec.scalar_tensor_tensor(out=tmpb[:, sl], in0=ssqs[:, sl], scalar=IVD,
                                            in1=tmpa[:, sl], op0=ALU.mult, op1=ALU.subtract)

        def fin_act(scalar_, tmpb, r_t, sl):
            afence(scalar_, scalar_.activation(out=tmpb[:, sl], in_=tmpb[:, sl], func=AF.Ln))
            return scalar_.activation(out=r_t[:, sl], in_=tmpb[:, sl], func=AF.Exp,
                                      scale=b_nhalf[:, :])

        def fin_B(vec, tmpb, m_t, r_t, mr_t, sl):
            return vec.tensor_tensor(out=mr_t[:, sl], in0=m_t[:, sl], in1=r_t[:, sl], op=ALU.mult)

        NGB = NSW * NHB
        cg_i = [0]; cu_i = [0]

        def emit_gdq(vec, lim):
            while cg_i[0] < min(lim, NGB * DC):
                ig = cg_i[0]
                gb, dc = divmod(ig, DC)
                vec.wait_ge(s_rg[ig % RG], 16 * (ig // RG + 1))
                if dc == 0 and gb >= 2:
                    vec.wait_ge(s_gmm, 2 * HCB * (gb - 1) - 1)
                vec.tensor_scalar(out=b_wg[(gb % 2) * DC + dc][:, :],
                                  in0=b_wraw_g[ig % RG][:, :],
                                  scalar1=128.0, scalar2=1.0,
                                  op0=ALU.subtract, op1=ALU.mult).then_inc(s_gdq, 1)
                cg_i[0] += 1

        def emit_udq(scalar_, lim):
            while cu_i[0] < min(lim, NGB * DC):
                iu = cu_i[0]
                gb, dc = divmod(iu, DC)
                scalar_.wait_ge(s_ru[iu % RU], 16 * (iu // RU + 1))
                if iu == 0:
                    scalar_.wait_ge(s_ppc, 9)
                if dc == 0 and gb >= 1:
                    scalar_.wait_ge(s_gmm, 2 * HCB * gb)
                scalar_.activation(out=b_wu[dc][:, :], in_=b_wraw_u[iu % RU][:, :],
                                   func=AF.Identity,
                                   bias=b_neg128[:, :]).then_inc(s_udq, 1)
                cu_i[0] += 1

        A = contextlib.ExitStack()
        a = A.enter_context
        with A:
            P0 = contextlib.ExitStack()
            p0 = P0.enter_context
            b_wabF = p0(nc.sbuf_tensor("b_wabF", [128, DC * S2], F32))
            b_woF = p0(nc.sbuf_tensor("b_woF", [S, D], F32))
            b_boutF = p0(nc.sbuf_tensor("b_boutF", [1, D], F32))
            b_sdrF = p0(nc.sbuf_tensor("b_sdrF", [1, D], F32))
            P0.close()
            b_x = [a(nc.sbuf_tensor(f"b_x{i}", [128, D], F32)) for i in range(NXB)]
            b_xn1 = [a(nc.sbuf_tensor(f"b_xn1{i}", [128, D], BF16)) for i in range(2)]
            b_xt = [a(nc.sbuf_tensor(f"b_xt{i}", [128, QTB], BF16)) for i in range(2 * DC)]
            b_ab = a(nc.sbuf_tensor("b_ab", [S, 2 * TSEQ], F32))
            b_hs = a(nc.sbuf_tensor("b_hs", [S, TSEQ], F32))

            @block.sync
            def _(sync):
                for dst, srcap in [(b_wabF[:, :].rearrange("p (c s) -> p c s", s=S2),
                                    wab_t.rearrange("(c p) s -> p c s", p=128)),
                                   (b_abbias[:, :], ab_bias),
                                   (b_woF[:, :], wout_t),
                                   (b_boutF[:, :], bout_row), (b_onesF[:, :], ones_r),
                                   (b_sel0[:, :], sel0), (b_sel1[:, :], sel1),
                                   (b_sg[:, :], sg_t), (b_su[:, :], su_t),
                                   (b_sdrF[:, :], sd_row)]:
                    sync.dma_start(out=dst, in_=srcap).then_inc(s_pp, 16)
                LAG = FSTB + 1
                sync.wait_ge(s_ppc, 9)
                sync.wait_ge(s_sdp, 1)

                def xt_round(q):
                    K = RPQ * (q + 1)
                    sync.wait_ge(s_x1s[0], 16 * _cnt(K, 0, 2))
                    sync.wait_ge(s_x1s[1], 16 * _cnt(K, 1, 2))
                    if q >= 2:
                        sync.wait_ge(s_abp, NPJQ * (q - 1))
                    for dc in range(DC):
                        sync.dma_start(out=b_xt[(q % 2) * DC + dc][:, :],
                                       in_=xn1b_h[q * QTB:(q + 1) * QTB, dc * 128:(dc + 1) * 128],
                                       transpose=True).then_inc(s_xt, 16)

                nxt_q = [0]
                for tc in range(TQ):
                    if tc >= NXB:
                        sync.wait_ge(s_ap, tc - NXB + 1)
                        sync.wait_ge(s_red, tc - NXB + 1)
                        sync.wait_ge(s_ssq, tc - NXB + 1)
                    sync.dma_start(out=b_x[tc % NXB][:, :],
                                   in_=x_seq[tc * 128:(tc + 1) * 128, :]).then_inc(s_xld[tc % NXB], 16)
                    if tc >= LAG:
                        st_c = tc - LAG
                        sync.wait_ge(s_ap, st_c + 1)
                        sync.dma_start(out=xn1b_h[st_c * 128:(st_c + 1) * 128, :],
                                       in_=b_xn1[st_c % 2][:, :]).then_inc(s_x1s[st_c % 2], 16)
                        if st_c + 1 >= RPQ * (nxt_q[0] + 1) and nxt_q[0] < NQ:
                            xt_round(nxt_q[0])
                            nxt_q[0] += 1
                for st_c in range(max(0, TQ - LAG), TQ):
                    sync.wait_ge(s_ap, st_c + 1)
                    sync.dma_start(out=xn1b_h[st_c * 128:(st_c + 1) * 128, :],
                                   in_=b_xn1[st_c % 2][:, :]).then_inc(s_x1s[st_c % 2], 16)
                    if st_c + 1 >= RPQ * (nxt_q[0] + 1) and nxt_q[0] < NQ:
                        xt_round(nxt_q[0])
                        nxt_q[0] += 1
                while nxt_q[0] < NQ:
                    xt_round(nxt_q[0])
                    nxt_q[0] += 1

            @block.gpsimd
            def _(gpsimd):
                lg = [0]; lu = [0]; lw = [0]

                def g_tile():
                    ig = lg[0]
                    gb, dc = divmod(ig, DC)
                    sw, hb = divmod(gb, NHB)
                    if ig >= RG:
                        gpsimd.wait_ge(s_gdq, ig - RG + 1)
                    gpsimd.dma_start(out=b_wraw_g[ig % RG][:, :],
                                     in_=wqg_t[dc * 128:(dc + 1) * 128,
                                               hb * HB:(hb + 1) * HB]).then_inc(s_rg[ig % RG], 16)
                    lg[0] += 1

                def u_tile():
                    iu = lu[0]
                    gb, dc = divmod(iu, DC)
                    sw, hb = divmod(gb, NHB)
                    if iu >= RU:
                        gpsimd.wait_ge(s_udq, iu - RU + 1)
                    gpsimd.dma_start(out=b_wraw_u[iu % RU][:, :],
                                     in_=wqu_t[dc * 128:(dc + 1) * 128,
                                               hb * HB:(hb + 1) * HB]).then_inc(s_ru[iu % RU], 16)
                    lu[0] += 1

                def w_tile():
                    k = lw[0]
                    sw, r = divmod(k, NDS * HC)
                    ds, hc = divmod(r, HC)
                    if k >= RD:
                        gpsimd.wait_ge(s_dwq, k - RD + 1)
                    gpsimd.dma_start(out=b_wdraw[k % RD][:, :],
                                     in_=wqd_t[hc * 128:(hc + 1) * 128,
                                               ds * DSL:(ds + 1) * DSL]).then_inc(s_rd[k % RD], 16)
                    lw[0] += 1

                NWDK = NSW * NDS * HC

                def g_blk():
                    if lg[0] < NGB * DC:
                        for _ in range(DC):
                            g_tile()

                def u_blk():
                    if lu[0] < NGB * DC:
                        for _ in range(DC):
                            u_tile()

                g_blk(); g_blk(); u_blk()
                for gb in range(NHB):
                    u_blk()
                    g_blk()
                while lw[0] < NDS * HC:
                    w_tile()
                for gb in range(NHB, 2 * NHB):
                    u_blk()
                    g_blk()
                while lw[0] < NWDK:
                    w_tile()


            @block.vector
            def _(vector):
                vector.wait_ge(s_pp, 16 * 10)
                vector.tensor_copy(out=b_wabT[:, :], in_=b_wabF[:, :]).then_inc(s_ppc, 1)
                vector.tensor_copy(out=b_woT[:, :], in_=b_woF[:, :]).then_inc(s_ppc, 1)
                vector.tensor_copy(out=b_bout_b[:, :], in_=b_boutF[:, :]).then_inc(s_ppc, 1)
                vector.tensor_copy(out=b_ones_b[:, :], in_=b_onesF[:, :]).then_inc(s_ppc, 1)
                vector.memset(b_zero[:, :], 0.0).then_inc(s_ppc, 1)
                vector.memset(b_one[:, :], 1.0).then_inc(s_ppc, 1)
                vector.memset(b_nhalf[:, :], -0.5).then_inc(s_ppc, 1)
                vector.memset(b_neg128[:, :], -128.0).then_inc(s_ppc, 1)
                vector.wait_ge(s_sdp, 1)
                vector.tensor_copy(out=b_sdbc[:, :], in_=pall[:, 0:D]).then_inc(s_ppc, 1)

                for bi in range(TQ // FSTB):
                    lo = bi * FSTB; hi = lo + FSTB; sl = slice(lo, hi)
                    for tc in range(lo, hi):
                        vector.wait_ge(s_xld[tc % NXB], 16 * (tc // NXB + 1))
                        vector.tensor_reduce(out=b_sum1[:, tc:tc + 1], in_=b_x[tc % NXB][:, :],
                                             axis=AX.X, op=ALU.add).then_inc(s_red, 1)
                        vector.scalar_tensor_tensor(out=b_sqs[:, :], in0=b_x[tc % NXB][:, :],
                                                    scalar=1.0, in1=b_x[tc % NXB][:, :],
                                                    op0=ALU.mult, op1=ALU.mult,
                                                    accum_out=b_ssq1[:, tc:tc + 1]).then_inc(s_ssq, 1)
                    vector.wait_ge(s_red, hi)
                    vector.wait_ge(s_ssq, hi)
                    fin_A(vector, b_sum1, b_ssq1, b_m1, b_tq1, b_tq2, sl).then_inc(s_fa, 1)
                    vector.wait_ge(s_fb, bi + 1)
                    fin_B(vector, b_tq2, b_m1, b_r1, b_mr1, sl).then_inc(s_f1v, 1)
                    vector.wait_ge(s_f1v, bi + 1)
                    for tc in range(lo, hi):
                        if tc >= 2:
                            vector.wait_ge(s_x1s[tc % 2], 16 * ((tc - 2) // 2 + 1))
                        vector.tensor_scalar(out=b_xn1[tc % 2][:, :], in0=b_x[tc % NXB][:, :],
                                             scalar1=b_r1[:, tc:tc + 1], scalar2=b_mr1[:, tc:tc + 1],
                                             op0=ALU.mult, op1=ALU.subtract).then_inc(s_ap, 1)

                vector.wait_ge(s_th, 2 * NPJ)
                vector.wait_ge(s_ssq, TQ)
                vector.tensor_tensor_scan(out=b_hs[:, :], data0=b_ab[:, 0:TSEQ],
                                          data1=b_ab[:, TSEQ:2 * TSEQ],
                                          initial=b_zero[:, :], op0=ALU.mult, op1=ALU.add).then_inc(s_hs, 1)
                vector.wait_ge(s_hs, 1)
                vector.tensor_scalar(out=b_ab[0:S, 0:TOWN], in0=b_hs[:, 0:TOWN],
                                     scalar1=b_sel0[0:S, :], scalar2=0.0,
                                     op0=ALU.mult, op1=ALU.subtract).then_inc(s_f1v, 1)
                vector.wait_ge(s_f1v, TQ // FSTB + 1)
                vector.scalar_tensor_tensor(out=b_hsb[:, :], in0=b_hs[:, TSEQ - TOWN:TSEQ],
                                            scalar=b_sel1[0:S, :], in1=b_ab[0:S, 0:TOWN],
                                            op0=ALU.mult, op1=ALU.add).then_inc(s_hs, 1)
                vector.tensor_scalar(out=b_tc1[:, :], in0=b_r1[:, 0:TC], scalar1=b_sel0[:, :],
                                     scalar2=0.0, op0=ALU.mult, op1=ALU.subtract).then_inc(s_f1v, 1)
                vector.wait_ge(s_f1v, TQ // FSTB + 2)
                vector.scalar_tensor_tensor(out=b_r1o[:, :], in0=b_r1[:, TQ - TC:TQ],
                                            scalar=b_sel1[:, :], in1=b_tc1[:, :],
                                            op0=ALU.mult, op1=ALU.add).then_inc(s_f1v, 1)
                vector.wait_ge(s_f1v, TQ // FSTB + 3)
                vector.tensor_scalar(out=b_tc1[:, :], in0=b_mr1[:, 0:TC], scalar1=b_sel0[:, :],
                                     scalar2=0.0, op0=ALU.mult, op1=ALU.subtract).then_inc(s_f1v, 1)
                vector.wait_ge(s_f1v, TQ // FSTB + 4)
                vector.scalar_tensor_tensor(out=b_mr1o[:, :], in0=b_mr1[:, TQ - TC:TQ],
                                            scalar=b_sel1[:, :], in1=b_tc1[:, :],
                                            op0=ALU.mult, op1=ALU.add).then_inc(s_hs, 1)

            @block.scalar
            def _(scalar):
                def tanh_blk(k):
                    scalar.wait_ge(s_abp, k + 1)
                    scalar.activation(out=b_ab[:, k * PJB:(k + 1) * PJB],
                                      in_=pall[0:S, (k % 2) * 512:(k % 2) * 512 + PJB],
                                      func=AF.Tanh, bias=b_abbias[:, 0:1]).then_inc(s_th, 1)
                    scalar.activation(out=b_ab[:, TSEQ + k * PJB:TSEQ + (k + 1) * PJB],
                                      in_=pall[0:S, 1024 + (k % 2) * 512:1024 + (k % 2) * 512 + PJB],
                                      func=AF.Tanh, bias=b_abbias[:, 1:2]).then_inc(s_th, 1)

                nxt_k = [0]
                for bi in range(TQ // FSTB):
                    scalar.wait_ge(s_fa, bi + 1)
                    fin_act(scalar, b_tq2, b_r1, slice(bi * FSTB, (bi + 1) * FSTB)).then_inc(s_fb, 1)
                    while (nxt_k[0] < NPJ
                           and RPQ * (nxt_k[0] // NPJQ + 1) <= FSTB * bi):
                        tanh_blk(nxt_k[0])
                        nxt_k[0] += 1
                while nxt_k[0] < NPJ:
                    tanh_blk(nxt_k[0])
                    nxt_k[0] += 1

            @block.tensor
            def _(tensor):
                tensor.wait_ge(s_pp, 16 * 10)
                for j in range(D // JW):
                    mm = tensor.matmul(pall[:, j * JW:(j + 1) * JW], b_onesF[:, :],
                                       b_sdrF[:, j * JW:(j + 1) * JW], start=True, stop=True)
                mm.then_inc(s_sdp, 1)
                tensor.wait_ge(s_ppc, 9)
                for q in range(NQ):
                    tensor.wait_ge(s_xt, 16 * DC * (q + 1))
                    for pb in range(NPJQ):
                        k = q * NPJQ + pb
                        if k >= 2:
                            tensor.wait_ge(s_th, 2 * (k - 1))
                        po_a = (k % 2) * 512
                        po_b = 1024 + (k % 2) * 512
                        xbuf = (q % 2) * DC
                        for dc in range(DC):
                            tensor.matmul(pall[0:S, po_a:po_a + PJB],
                                          b_wabT[:, dc * S2:dc * S2 + S],
                                          b_xt[xbuf + dc][:, pb * PJB:(pb + 1) * PJB],
                                          start=(dc == 0), stop=(dc == DC - 1))
                            mm = tensor.matmul(pall[0:S, po_b:po_b + PJB],
                                               b_wabT[:, dc * S2 + S:dc * S2 + S2],
                                               b_xt[xbuf + dc][:, pb * PJB:(pb + 1) * PJB],
                                               start=(dc == 0), stop=(dc == DC - 1))
                        mm.then_inc(s_abp, 1)

        B1 = contextlib.ExitStack()
        b1 = B1.enter_context
        with B1:
            b_xo = [b1(nc.sbuf_tensor(f"b_xo{i}", [128, D], F32)) for i in range(NOB)]
            b_txn = [b1(nc.sbuf_tensor(f"b_txn{i}", [128, D], F32)) for i in range(1)]
            b_y1 = [b1(nc.sbuf_tensor(f"b_y1{i}", [128, D], F32)) for i in range(3)]
            b_t2 = [b1(nc.sbuf_tensor(f"b_t2{i}", [128, D], F32)) for i in range(1)]
            b_x2 = [b1(nc.sbuf_tensor(f"b_x2{i}", [128, D], F32)) for i in range(3)]
            b_xn2 = [b1(nc.sbuf_tensor(f"b_xn2{i}", [128, D], BF16)) for i in range(2)]

            @block.sync
            def _(sync):
                sync.wait_ge(s_hs, 3)
                sync.wait_ge(s_ap, TQ)
                sync.wait_ge(s_ssq, TQ)
                sync.wait_ge(s_x1s[0], 16 * _cnt(TQ, 0, 2))
                sync.wait_ge(s_x1s[1], 16 * _cnt(TQ, 1, 2))
                sync.wait_ge(s_abp, NPJ)
                for oc in range(TC):
                    if oc >= NOB:
                        sync.wait_ge(s_x2, oc - NOB + 1)
                    sync.dma_start(out=b_xo[oc % NOB][:, :],
                                   in_=x_own[oc * 128:(oc + 1) * 128, :]).then_inc(s_xo[oc % NOB], 16)
                    if oc >= 2:
                        oc2 = oc - 2
                        sync.wait_ge(s_x2, oc2 + 1)
                        sync.dma_start(out=x2_h[oc2 * 128:(oc2 + 1) * 128, :],
                                       in_=b_x2[oc2 % 3][:, :]).then_inc(s_x2s[oc2 % 3], 16)
                        sync.wait_ge(s_xn2, oc2 + 1)
                        sync.dma_start(out=xn2b_h[oc2 * 128:(oc2 + 1) * 128, :],
                                       in_=b_xn2[oc2 % 2][:, :]).then_inc(s_n2s[oc2 % 2], 16)
                for oc2 in range(TC - 2, TC):
                    sync.wait_ge(s_x2, oc2 + 1)
                    sync.dma_start(out=x2_h[oc2 * 128:(oc2 + 1) * 128, :],
                                   in_=b_x2[oc2 % 3][:, :]).then_inc(s_x2s[oc2 % 3], 16)
                    sync.wait_ge(s_xn2, oc2 + 1)
                    sync.dma_start(out=xn2b_h[oc2 * 128:(oc2 + 1) * 128, :],
                                   in_=b_xn2[oc2 % 2][:, :]).then_inc(s_n2s[oc2 % 2], 16)

            @block.vector
            def _(vector):
                for bi in range(TC // OSTB):
                    lo = bi * OSTB; hi = lo + OSTB; sl = slice(lo, hi)
                    for oc in range(lo, hi):
                        vector.wait_ge(s_xo[oc % NOB], 16 * (oc // NOB + 1))
                        vector.wait_ge(s_wps, oc + 1)
                        if oc >= 3:
                            vector.wait_ge(s_sq2, oc - 2)
                        vector.tensor_scalar(out=b_txn[0][:, :], in0=b_xo[oc % NOB][:, :],
                                             scalar1=b_r1o[:, oc:oc + 1], scalar2=b_mr1o[:, oc:oc + 1],
                                             op0=ALU.mult, op1=ALU.subtract)
                        vector.scalar_tensor_tensor(out=b_y1[oc % 3][:, :], in0=b_txn[0][:, :],
                                                    scalar=1.0, in1=pall[:, 0:D],
                                                    op0=ALU.mult, op1=ALU.add,
                                                    accum_out=b_sum2[:, oc:oc + 1]).then_inc(s_y1, 1)
                    vector.wait_ge(s_sq2, hi)
                    vector.wait_ge(s_y1, hi)
                    fin_A(vector, b_sum2, b_ssq2, b_m2, b_tc1, b_tc2, sl).then_inc(s_fa2, 1)
                    vector.wait_ge(s_fb2, 2 * bi + 1)
                    dfence(vector, fin_B(vector, b_tc2, b_m2, b_r2, b_mr2, sl))
                    for oc in range(lo, hi):
                        if oc >= 3:
                            vector.wait_ge(s_x2s[(oc - 3) % 3], 16 * ((oc - 3) // 3 + 1))
                            vector.wait_ge(s_sq3, oc - 2)
                        vector.tensor_scalar(out=b_t2[0][:, :], in0=b_y1[oc % 3][:, :],
                                             scalar1=b_r2[:, oc:oc + 1], scalar2=b_mr2[:, oc:oc + 1],
                                             op0=ALU.mult, op1=ALU.subtract)
                        vector.scalar_tensor_tensor(out=b_x2[oc % 3][:, :], in0=b_t2[0][:, :],
                                                    scalar=1.0, in1=b_xo[oc % NOB][:, :],
                                                    op0=ALU.mult, op1=ALU.add,
                                                    accum_out=b_sum3[:, oc:oc + 1]).then_inc(s_x2, 1)
                    vector.wait_ge(s_sq3, hi)
                    vector.wait_ge(s_x2, hi)
                    fin_A(vector, b_sum3, b_ssq3, b_m3, b_tc1, b_tc2, sl).then_inc(s_fa2, 1)
                    vector.wait_ge(s_fb2, 2 * bi + 2)
                    dfence(vector, fin_B(vector, b_tc2, b_m3, b_r3, b_mr3, sl))
                    for oc in range(lo, hi):
                        if oc >= 2:
                            vector.wait_ge(s_n2s[oc % 2], 16 * ((oc - 2) // 2 + 1))
                        vector.tensor_scalar(out=b_xn2[oc % 2][:, :], in0=b_x2[oc % 3][:, :],
                                             scalar1=b_r3[:, oc:oc + 1], scalar2=b_mr3[:, oc:oc + 1],
                                             op0=ALU.mult, op1=ALU.subtract).then_inc(s_xn2, 1)

            @block.scalar
            def _(scalar):
                for bi in range(TC // OSTB):
                    lo = bi * OSTB; hi = lo + OSTB
                    for oc in range(lo, hi):
                        scalar.wait_ge(s_y1, oc + 1)
                        scalar.activation(out=b_sqs[:, :], in_=b_y1[oc % 3][:, :], func=AF.Square,
                                          accum_out=b_ssq2[:, oc:oc + 1]).then_inc(s_sq2, 1)
                    scalar.wait_ge(s_fa2, 2 * bi + 1)
                    fin_act(scalar, b_tc2, b_r2, slice(lo, hi)).then_inc(s_fb2, 1)
                    for oc in range(lo, hi):
                        scalar.wait_ge(s_x2, oc + 1)
                        scalar.activation(out=b_sqs[:, :], in_=b_x2[oc % 3][:, :], func=AF.Square,
                                          accum_out=b_ssq3[:, oc:oc + 1]).then_inc(s_sq3, 1)
                    scalar.wait_ge(s_fa2, 2 * bi + 2)
                    fin_act(scalar, b_tc2, b_r3, slice(lo, hi)).then_inc(s_fb2, 1)

            @block.tensor
            def _(tensor):
                for oc in range(TC):
                    tensor.wait_ge(s_hs, 2)
                    if oc >= 1:
                        tensor.wait_ge(s_y1, oc)
                    for j in range(NJW):
                        tensor.matmul(pall[:, j * JW:(j + 1) * JW],
                                      b_hsb[:, oc * 128:(oc + 1) * 128],
                                      b_woT[:, j * JW:(j + 1) * JW],
                                      start=True, stop=False)
                        mm = tensor.matmul(pall[:, j * JW:(j + 1) * JW],
                                           b_ones_b[:, :], b_bout_b[:, j * JW:(j + 1) * JW],
                                           start=False, stop=True)
                    mm.then_inc(s_wps, 1)

        B2 = contextlib.ExitStack()
        b2 = B2.enter_context
        with B2:
            b_xn2T = [b2(nc.sbuf_tensor(f"b_xn2T{i}", [128, TSL], BF16)) for i in range(DC)]
            b_wg = [b2(nc.sbuf_tensor(f"b_wg{i}", [128, HB], BF16)) for i in range(2 * DC)]
            b_wu = [b2(nc.sbuf_tensor(f"b_wu{i}", [128, HB], BF16)) for i in range(DC)]
            b_gt_ = [b2(nc.sbuf_tensor(f"b_gt{i}", [128, TSL], BF16)) for i in range(2)]
            b_ut = [b2(nc.sbuf_tensor(f"b_ut{i}", [128, TSL], BF16)) for i in range(2)]
            b_pw = [b2(nc.sbuf_tensor(f"b_pw{i}", [128, TSL], BF16)) for i in range(2)]
            b_pst = [b2(nc.sbuf_tensor(f"b_pst{i}", [128, TSL], BF16)) for i in range(NPS)]
            b_wd = [b2(nc.sbuf_tensor(f"b_wd{i}", [128, DSL], BF16)) for i in range(NWD)]
            b_f = [b2(nc.sbuf_tensor(f"b_f{i}", [128, D], BF16)) for i in range(TCS)]
            b_x2r = [b2(nc.sbuf_tensor(f"b_x2r{i}", [128, D], F32)) for i in range(2)]

            @block.sync
            def _(sync):
                for j in range(3):
                    sync.wait_ge(s_x2s[j], 16 * _cnt(TC, j, 3))
                for j in range(2):
                    sync.wait_ge(s_n2s[j], 16 * _cnt(TC, j, 2))
                sync.wait_ge(s_sq2, TC)
                sync.wait_ge(s_sq3, TC)
                sync.wait_ge(s_xn2, TC)
                def emit_n2t(sw):
                    if sw > 0:
                        sync.wait_ge(s_gmm, 2 * HC * sw)
                    for dc in range(DC):
                        sync.dma_start(out=b_xn2T[dc][:, :],
                                       in_=xn2b_h[sw * TSL:(sw + 1) * TSL, dc * 128:(dc + 1) * 128],
                                       transpose=True).then_inc(s_n2t, 16)

                emit_n2t(0)
                for sw in range(NSW):
                    def emit_prod_store(hc):
                        ghc = sw * HC + hc
                        sync.wait_ge(s_pr, ghc + 1)
                        sync.dma_start(out=prod_h[hc * 128:(hc + 1) * 128, :],
                                       in_=b_pw[ghc % 2][:, :]).then_inc(s_prs[ghc % 2], 16)

                    for hc in range(HC):
                        emit_prod_store(hc)
                    for ds in range(NDS):
                        if ds == min(1, NDS - 1) and sw + 1 < NSW:
                            emit_n2t(sw + 1)
                        for hc in range(HC):
                            k = sw * NDS * HC + ds * HC + hc
                            ghc = sw * HC + hc
                            sync.wait_ge(s_prs[ghc % 2], 16 * (ghc // 2 + 1))
                            if k >= NPS:
                                sync.wait_ge(s_dmm, k - NPS + 1)
                            sync.dma_start(out=b_pst[k % NPS][:, :],
                                           in_=prod_h[hc * 128:(hc + 1) * 128, :]).then_inc(s_psl[k % NPS], 16)
                    for tc in range(TCS):
                        gtc = sw * TCS + tc
                        if gtc >= 2:
                            sync.wait_ge(s_ys[gtc % 2], 16 * ((gtc - 2) // 2 + 1))
                        sync.dma_start(out=b_x2r[gtc % 2][:, :],
                                       in_=x2_h[gtc * 128:(gtc + 1) * 128, :]).then_inc(s_x2r[gtc % 2], 16)
                        sync.wait_ge(s_yo, gtc + 1)
                        sync.dma_start(out=y_out[gtc * 128:(gtc + 1) * 128, :],
                                       in_=b_x2r[gtc % 2][:, :]).then_inc(s_ys[gtc % 2], 16)
                sync.wait_ge(s_ys[0], 16 * _cnt(TC, 0, 2))
                sync.wait_ge(s_ys[1], 16 * _cnt(TC, 1, 2))
                if dbg_st is not None:
                    for i, t in enumerate([b_sum1, b_ssq1, b_m1, b_r1, b_mr1, b_tq2]):
                        sync.dma_start(out=dbg_st[i * 128:(i + 1) * 128, :],
                                       in_=t[:, :]).then_inc(s_pp, 16)
                    sync.wait_ge(s_pp, 16 * 16)

            @block.tensor
            def _(tensor):
                for sw in range(NSW):
                    tensor.wait_ge(s_n2t, 16 * DC * (sw + 1))
                    for hc in range(HC):
                        hb, hcc = divmod(hc, HCB)
                        tensor.wait_ge(s_gdq, DC * (sw * NHB + hb + 1))
                        if hc >= 2:
                            tensor.wait_ge(s_gev, 2 * HC * sw + 2 * (hc - 2) + 1)
                        elif sw > 0:
                            tensor.wait_ge(s_dev, sw * NDS * TCS)
                        po = (4 + 2 * (hc % 2)) * 512
                        gbuf = ((sw * NHB + hb) % 2) * DC
                        for dc in range(DC):
                            for j in range(NJ5):
                                mm = tensor.matmul(pall[:, po + j * T5:po + (j + 1) * T5],
                                                   b_wg[gbuf + dc][:, hcc * 128:(hcc + 1) * 128],
                                                   b_xn2T[dc][:, j * T5:(j + 1) * T5],
                                                   start=(dc == 0), stop=(dc == DC - 1))
                        mm.then_inc(s_gmm, 1)
                        tensor.wait_ge(s_udq, DC * (sw * NHB + hb + 1))
                        if hc >= 2:
                            tensor.wait_ge(s_gev, 2 * HC * sw + 2 * (hc - 2) + 2)
                        elif sw == 0:
                            tensor.wait_ge(s_y1, TC)
                        else:
                            tensor.wait_ge(s_dev, sw * NDS * TCS)
                        po = (2 * (hc % 2)) * 512
                        for dc in range(DC):
                            for j in range(NJ5):
                                mm = tensor.matmul(pall[:, po + j * T5:po + (j + 1) * T5],
                                                   b_wu[dc][:, hcc * 128:(hcc + 1) * 128],
                                                   b_xn2T[dc][:, j * T5:(j + 1) * T5],
                                                   start=(dc == 0), stop=(dc == DC - 1))
                        mm.then_inc(s_gmm, 1)
                    for ds in range(NDS):
                        if ds == 0:
                            tensor.wait_ge(s_gev, 2 * HC * (sw + 1))
                        else:
                            tensor.wait_ge(s_dev, (sw * NDS + ds) * TCS)
                        for hc in range(HC):
                            k = sw * NDS * HC + ds * HC + hc
                            tensor.wait_ge(s_psl[k % NPS], 16 * (k // NPS + 1))
                            tensor.wait_ge(s_dwq, k + 1)
                            for tc in range(TCS):
                                mm = tensor.matmul(pall[:, tc * 512:tc * 512 + DSL],
                                                   b_pst[k % NPS][:, tc * 128:(tc + 1) * 128],
                                                   b_wd[k % NWD][:, :],
                                                   start=(hc == 0), stop=(hc == HC - 1))
                            mm.then_inc(s_dmm, 1)

            @block.scalar
            def _(scalar):
                emit_udq(scalar, DC)
                for sw in range(NSW):
                    for hc in range(HC):
                        ghc = sw * HC + hc
                        gb = sw * NHB + hc // HCB
                        hcc = hc % HCB
                        scalar.wait_ge(s_gmm, 2 * HC * sw + 2 * hc + 1)
                        if ghc >= 2:
                            scalar.wait_ge(s_pr, ghc - 1)
                        po = (4 + 2 * (hc % 2)) * 512
                        scalar.activation(out=b_gt_[hc % 2][:, :], in_=pall[:, po:po + TSL],
                                          func=AF.Silu,
                                          scale=b_sg[:, hc:hc + 1]).then_inc(s_gev, 1)
                        scalar.wait_ge(s_gmm, 2 * HC * sw + 2 * hc + 2)
                        po = (2 * (hc % 2)) * 512
                        scalar.activation(out=b_ut[hc % 2][:, :], in_=pall[:, po:po + TSL],
                                          func=AF.Identity,
                                          scale=b_su[:, hc:hc + 1]).then_inc(s_gev, 1)
                        if hcc == HCB - 1:
                            emit_udq(scalar, DC * (gb + 2))
                    for ds in range(NDS):
                        for tc in range(TCS):
                            scalar.wait_ge(s_dmm, sw * NDS * HC + ds * HC + HC)
                            if sw > 0 and ds == 0:
                                scalar.wait_ge(s_yo, sw * TCS)
                            scalar.activation(out=b_f[tc][:, ds * DSL:(ds + 1) * DSL],
                                              in_=pall[:, tc * 512:tc * 512 + DSL],
                                              func=AF.Copy).then_inc(s_dev, 1)
                    for tc in range(TCS):
                        gtc = sw * TCS + tc
                        scalar.activation(out=b_sqs[:, :], in_=b_f[tc][:, :], func=AF.Copy,
                                          accum_out=b_sum4[:, gtc:gtc + 1]).then_inc(s_sq4, 1)
                        scalar.activation(out=b_sqs[:, :], in_=b_f[tc][:, :], func=AF.Square,
                                          accum_out=b_ssq4[:, gtc:gtc + 1]).then_inc(s_sq4, 1)
                    scalar.wait_ge(s_fa4, sw + 1)
                    fin_act(scalar, b_tc2, b_r4, slice(sw * TCS, (sw + 1) * TCS)).then_inc(s_fb4, 1)

            @block.vector
            def _(vector):
                cw_i = [0]

                def emit_wdq(lim):
                    while cw_i[0] < min(lim, NSW * NDS * HC):
                        k = cw_i[0]
                        swk, r = divmod(k, NDS * HC)
                        ds = r // HC
                        vector.wait_ge(s_rd[k % RD], 16 * (k // RD + 1))
                        if k >= NWD:
                            vector.wait_ge(s_dmm, k - NWD + 1)
                        vector.scalar_tensor_tensor(out=b_wd[k % NWD][:, :],
                                                    in0=b_wdraw[k % RD][:, :],
                                                    scalar=128.0,
                                                    in1=b_sdbc[:, ds * DSL:(ds + 1) * DSL],
                                                    op0=ALU.subtract,
                                                    op1=ALU.mult).then_inc(s_dwq, 1)
                        cw_i[0] += 1

                emit_gdq(vector, 2 * DC)
                for sw in range(NSW):
                    for hc in range(HC):
                        ghc = sw * HC + hc
                        gb = sw * NHB + hc // HCB
                        hcc = hc % HCB
                        vector.wait_ge(s_gev, 2 * HC * sw + 2 * (hc + 1))
                        if ghc >= 2:
                            vector.wait_ge(s_prs[ghc % 2], 16 * ((ghc - 2) // 2 + 1))
                        vector.tensor_tensor(out=b_pw[ghc % 2][:, :], in0=b_gt_[hc % 2][:, :],
                                             in1=b_ut[hc % 2][:, :], op=ALU.mult).then_inc(s_pr, 1)
                        if hcc == HCB - 1:
                            emit_gdq(vector, DC * (gb + 3))
                    emit_wdq((sw + 1) * NDS * HC)
                    sl = slice(sw * TCS, (sw + 1) * TCS)
                    vector.wait_ge(s_sq4, 2 * (sw + 1) * TCS)
                    fin_A(vector, b_sum4, b_ssq4, b_m4, b_tc1, b_tc2, sl).then_inc(s_fa4, 1)
                    vector.wait_ge(s_fb4, sw + 1)
                    dfence(vector, fin_B(vector, b_tc2, b_m4, b_r4, b_mr4, sl))
                    for tc in range(TCS):
                        gtc = sw * TCS + tc
                        vector.wait_ge(s_x2r[gtc % 2], 16 * (gtc // 2 + 1))
                        vector.tensor_scalar(out=b_x2r[gtc % 2][:, :], in0=b_x2r[gtc % 2][:, :],
                                             scalar1=1.0, scalar2=b_mr4[:, gtc:gtc + 1],
                                             op0=ALU.mult, op1=ALU.subtract)
                        vector.scalar_tensor_tensor(out=b_x2r[gtc % 2][:, :], in0=b_f[tc][:, :],
                                                    scalar=b_r4[:, gtc:gtc + 1],
                                                    in1=b_x2r[gtc % 2][:, :],
                                                    op0=ALU.mult, op1=ALU.add).then_inc(s_yo, 1)

    return nc


def kernel(L0_attn_x, L0_attn_Wa, L0_attn_ba, L0_attn_Wb, L0_attn_bb,
           L0_attn_Wout, L0_attn_bout, L0_ffn_wq_gate, L0_ffn_s_gate,
           L0_ffn_wq_up, L0_ffn_s_up, L0_ffn_wq_down, L0_ffn_s_down,
           _cfg=None, _n_cores=8):
    cfg = dict(FULL_CFG if _cfg is None else _cfg)
    D = cfg["D"]; TSEQ = cfg["TSEQ"]; TOWN = cfg["TOWN"]
    HC = cfg["H"] // 128; DC = D // 128
    x = np.ascontiguousarray(np.asarray(L0_attn_x, dtype=np.float32)).reshape(-1, TSEQ, D)
    B = x.shape[0]
    n_cores = _n_cores

    wab_t = np.ascontiguousarray(np.concatenate([np.asarray(L0_attn_Wa, np.float32),
                                                 np.asarray(L0_attn_Wb, np.float32)], axis=0).T)
    ab_bias = np.ascontiguousarray(np.stack([np.asarray(L0_attn_ba, np.float32),
                                             np.asarray(L0_attn_bb, np.float32)], axis=1))
    wout_t = np.ascontiguousarray(np.asarray(L0_attn_Wout, np.float32).T)
    bout_row = np.ascontiguousarray(np.asarray(L0_attn_bout, np.float32)[None, :])
    wqg_t = np.ascontiguousarray(np.asarray(L0_ffn_wq_gate, np.int32).T)
    wqu_t = np.ascontiguousarray(np.asarray(L0_ffn_wq_up, np.int32).T)
    wqd_t = np.ascontiguousarray(np.asarray(L0_ffn_wq_down, np.int32).T)
    sg_t = np.ascontiguousarray(np.asarray(L0_ffn_s_gate, np.float32).reshape(HC, 128).T)
    su_t = np.ascontiguousarray(np.asarray(L0_ffn_s_up, np.float32).reshape(HC, 128).T)
    sd_row = np.ascontiguousarray(np.asarray(L0_ffn_s_down, np.float32)[None, :])
    ones_r = np.ones((1, 128), np.float32)

    nc = build_nc(cfg)
    in_maps = []
    for c in range(n_cores):
        b, half = divmod(c, 2)
        in_maps.append(dict(
            x_seq=np.ascontiguousarray(x[b]),
            x_own=np.ascontiguousarray(x[b, half * TOWN:(half + 1) * TOWN]),
            sel0=np.full((128, 1), 1.0 - half, np.float32),
            sel1=np.full((128, 1), float(half), np.float32),
            wab_t=wab_t, ab_bias=ab_bias, wout_t=wout_t, bout_row=bout_row,
            wqg_t=wqg_t, wqu_t=wqu_t, wqd_t=wqd_t, sg_t=sg_t, su_t=su_t, sd_row=sd_row,
            ones_r=ones_r,
        ))
    res = run_bass_kernel_spmd(nc, in_maps, core_ids=list(range(n_cores)))
    out = np.concatenate([res.results[c]["y_out"] for c in range(n_cores)], axis=0)
    return out.reshape(B, TSEQ, D)


# version 15
# speedup vs baseline: 1.2773x; 1.0104x over previous
import contextlib
import numpy as np

import concourse.bass as bass
import concourse.mybir as mybir
from concourse.bass_utils import run_bass_kernel_spmd

F32 = mybir.dt.float32
BF16 = mybir.dt.bfloat16
I32 = mybir.dt.int32
AF = mybir.ActivationFunctionType
ALU = mybir.AluOpType
AX = mybir.AxisListType

FULL_CFG = dict(D=2048, S=64, TSEQ=4096, TOWN=2048, H=5632, NSW=2, EPS=1e-5)


def _cnt(K, j, NB):
    return (K - j + NB - 1) // NB


def build_nc(cfg, dbg=False):
    D = cfg["D"]; S = cfg["S"]; TSEQ = cfg["TSEQ"]; TOWN = cfg["TOWN"]
    H = cfg["H"]; NSW = cfg["NSW"]; EPS = cfg["EPS"]
    DC = D // 128
    TQ = TSEQ // 128
    TC = TOWN // 128
    HC = H // 128
    TSL = TOWN // NSW
    TCS = TSL // 128
    S2 = 2 * S
    QTB = min(512, TSEQ // 2)
    NQ = TSEQ // QTB
    RPQ = QTB // 128
    PJB = min(512, QTB)
    NPJQ = QTB // PJB
    NPJ = NQ * NPJQ
    HB = min(512, H)
    NHB = H // HB
    HCB = HB // 128
    DSL = min(512, D)
    NDS = D // DSL
    T5 = min(512, TSL)
    NJ5 = TSL // T5
    JW = min(512, D)
    NJW = D // JW
    FSTB = 4 if TQ % 4 == 0 else 1
    OSTB = 2 if TC % 2 == 0 else 1
    NXB = 5
    NOB = 3
    NWD = 8
    NPS = 4
    IVD = 1.0 / D

    nc = bass.Bass("TRN2", target_bir_lowering=False, detect_race_conditions=False)

    x_seq = nc.dram_tensor("x_seq", [TSEQ, D], F32, kind="ExternalInput").ap()
    x_own = nc.dram_tensor("x_own", [TOWN, D], F32, kind="ExternalInput").ap()
    sel0 = nc.dram_tensor("sel0", [128, 1], F32, kind="ExternalInput").ap()
    sel1 = nc.dram_tensor("sel1", [128, 1], F32, kind="ExternalInput").ap()
    wab_t = nc.dram_tensor("wab_t", [D, S2], F32, kind="ExternalInput").ap()
    ab_bias = nc.dram_tensor("ab_bias", [S, 2], F32, kind="ExternalInput").ap()
    wout_t = nc.dram_tensor("wout_t", [S, D], F32, kind="ExternalInput").ap()
    bout_row = nc.dram_tensor("bout_row", [1, D], F32, kind="ExternalInput").ap()
    wqg_t = nc.dram_tensor("wqg_t", [D, H], I32, kind="ExternalInput").ap()
    wqu_t = nc.dram_tensor("wqu_t", [D, H], I32, kind="ExternalInput").ap()
    wqd_t = nc.dram_tensor("wqd_t", [H, D], I32, kind="ExternalInput").ap()
    sg_t = nc.dram_tensor("sg_t", [128, HC], F32, kind="ExternalInput").ap()
    su_t = nc.dram_tensor("su_t", [128, HC], F32, kind="ExternalInput").ap()
    sd_row = nc.dram_tensor("sd_row", [1, D], F32, kind="ExternalInput").ap()
    ones_r = nc.dram_tensor("ones_r", [1, 128], F32, kind="ExternalInput").ap()

    y_out = nc.dram_tensor("y_out", [TOWN, D], F32, kind="ExternalOutput").ap()

    KI = "ExternalOutput" if dbg else "Internal"
    xn1b_h = nc.dram_tensor("xn1b_h", [TSEQ, D], BF16, kind=KI).ap()
    xn2b_h = nc.dram_tensor("xn2b_h", [TOWN, D], BF16, kind=KI).ap()
    x2_h = nc.dram_tensor("x2_h", [TOWN, D], F32, kind=KI).ap()
    prod_h = nc.dram_tensor("prod_h", [H, TSL], BF16, kind=KI).ap()
    dbg_st = nc.dram_tensor("dbg_st", [6 * 128, TQ], F32, kind=KI).ap() if dbg else None

    G = contextlib.ExitStack()
    g = G.enter_context
    with G:
        b_wabT = g(nc.sbuf_tensor("b_wabT", [128, DC * S2], BF16))
        b_woT = g(nc.sbuf_tensor("b_woT", [S, D], BF16))
        b_abbias = g(nc.sbuf_tensor("b_abbias", [S, 2], F32))
        b_bout_b = g(nc.sbuf_tensor("b_bout_b", [1, D], BF16))
        b_ones_b = g(nc.sbuf_tensor("b_ones_b", [1, 128], BF16))
        b_sel0 = g(nc.sbuf_tensor("b_sel0", [128, 1], F32))
        b_sel1 = g(nc.sbuf_tensor("b_sel1", [128, 1], F32))
        b_sg = g(nc.sbuf_tensor("b_sg", [128, HC], F32))
        b_su = g(nc.sbuf_tensor("b_su", [128, HC], F32))
        b_neg128 = g(nc.sbuf_tensor("b_neg128", [128, 1], F32))
        b_sdbc = g(nc.sbuf_tensor("b_sdbc", [128, D], BF16))
        b_onesF = g(nc.sbuf_tensor("b_onesF", [1, 128], F32))
        b_zero = g(nc.sbuf_tensor("b_zero", [S, 1], F32))
        b_sum1 = g(nc.sbuf_tensor("b_sum1", [128, TQ], F32))
        b_ssq1 = g(nc.sbuf_tensor("b_ssq1", [128, TQ], F32))
        b_m1 = g(nc.sbuf_tensor("b_m1", [128, TQ], F32))
        b_r1 = g(nc.sbuf_tensor("b_r1", [128, TQ], F32))
        b_mr1 = g(nc.sbuf_tensor("b_mr1", [128, TQ], F32))
        b_tq1 = g(nc.sbuf_tensor("b_tq1", [128, TQ], F32))
        b_tq2 = g(nc.sbuf_tensor("b_tq2", [128, TQ], F32))
        b_r1o = g(nc.sbuf_tensor("b_r1o", [128, TC], F32))
        b_mr1o = g(nc.sbuf_tensor("b_mr1o", [128, TC], F32))
        b_sum2 = g(nc.sbuf_tensor("b_sum2", [128, TC], F32))
        b_ssq2 = g(nc.sbuf_tensor("b_ssq2", [128, TC], F32))
        b_m2 = g(nc.sbuf_tensor("b_m2", [128, TC], F32))
        b_r2 = g(nc.sbuf_tensor("b_r2", [128, TC], F32))
        b_mr2 = g(nc.sbuf_tensor("b_mr2", [128, TC], F32))
        b_sum3 = g(nc.sbuf_tensor("b_sum3", [128, TC], F32))
        b_ssq3 = g(nc.sbuf_tensor("b_ssq3", [128, TC], F32))
        b_m3 = g(nc.sbuf_tensor("b_m3", [128, TC], F32))
        b_r3 = g(nc.sbuf_tensor("b_r3", [128, TC], F32))
        b_mr3 = g(nc.sbuf_tensor("b_mr3", [128, TC], F32))
        b_sum4 = g(nc.sbuf_tensor("b_sum4", [128, TC], F32))
        b_ssq4 = g(nc.sbuf_tensor("b_ssq4", [128, TC], F32))
        b_m4 = g(nc.sbuf_tensor("b_m4", [128, TC], F32))
        b_r4 = g(nc.sbuf_tensor("b_r4", [128, TC], F32))
        b_mr4 = g(nc.sbuf_tensor("b_mr4", [128, TC], F32))
        b_tc1 = g(nc.sbuf_tensor("b_tc1", [128, TC], F32))
        b_tc2 = g(nc.sbuf_tensor("b_tc2", [128, TC], F32))
        b_sqs = g(nc.sbuf_tensor("b_sqs", [128, D], BF16))
        b_one = g(nc.sbuf_tensor("b_one", [128, max(TQ, TC)], F32))
        b_nhalf = g(nc.sbuf_tensor("b_nhalf", [128, 1], F32))
        b_hsb = g(nc.sbuf_tensor("b_hsb", [S, TOWN], BF16))
        RG = 8; RU = 8; RD = 8
        b_wraw_g = [g(nc.sbuf_tensor(f"b_wraw_g{i}", [128, HB], BF16)) for i in range(RG)]
        b_wraw_u = [g(nc.sbuf_tensor(f"b_wraw_u{i}", [128, HB], BF16)) for i in range(RU)]
        b_wdraw = [g(nc.sbuf_tensor(f"b_wdraw{i}", [128, DSL], BF16)) for i in range(RD)]
        pall = g(nc.psum_tensor("pall", [128, 8 * 512], F32))

        def S1_(name):
            return G.enter_context(nc.semaphore(name))

        def SN_(name, n):
            return [G.enter_context(nc.semaphore(f"{name}{i}")) for i in range(n)]

        s_ppc = S1_("s_ppc"); s_red = S1_("s_red"); s_ssq = S1_("s_ssq")
        s_fa = S1_("s_fa"); s_fb = S1_("s_fb"); s_ap = S1_("s_ap")
        s_abp = S1_("s_abp"); s_th = S1_("s_th"); s_hs = S1_("s_hs")
        s_wps = S1_("s_wps"); s_y1 = S1_("s_y1")
        s_sq2 = S1_("s_sq2"); s_x2 = S1_("s_x2")
        s_sq3 = S1_("s_sq3"); s_xn2 = S1_("s_xn2")
        s_fa2 = S1_("s_fa2"); s_fb2 = S1_("s_fb2")
        s_gdq = S1_("s_gdq"); s_udq = S1_("s_udq"); s_dwq = S1_("s_dwq")
        s_gmm = S1_("s_gmm"); s_gev = S1_("s_gev"); s_pr = S1_("s_pr")
        s_dmm = S1_("s_dmm"); s_dev = S1_("s_dev")
        s_sq4 = S1_("s_sq4"); s_fa4 = S1_("s_fa4"); s_fb4 = S1_("s_fb4")
        s_yo = S1_("s_yo"); s_sdp = S1_("s_sdp")
        s_f1v = S1_("s_f1v"); s_r3v = S1_("s_r3v")
        s_dvf = S1_("s_dvf"); s_acf = S1_("s_acf")
        dvf_n = [0]
        acf_n = [0]

        def dfence(vec, instr):
            instr.then_inc(s_dvf, 1)
            dvf_n[0] += 1
            vec.wait_ge(s_dvf, dvf_n[0])

        def afence(scal, instr):
            instr.then_inc(s_acf, 1)
            acf_n[0] += 1
            scal.wait_ge(s_acf, acf_n[0])
        s_pp = S1_("s_pp")
        s_xld = SN_("s_xld", NXB)
        s_x1s = SN_("s_x1s", 2)
        s_xt = S1_("s_xt")
        s_xo = SN_("s_xo", NOB)
        s_x2s = SN_("s_x2s", 3)
        s_n2s = SN_("s_n2s", 2)
        s_rg = SN_("s_rg", RG)
        s_ru = SN_("s_ru", RU)
        s_rd = SN_("s_rd", RD)
        s_n2t = S1_("s_n2t")
        s_prs = SN_("s_prs", 2)
        s_psl = SN_("s_psl", NPS)
        s_x2r = SN_("s_x2r", 2)
        s_ys = SN_("s_ys", 2)

        block = g(nc.Block())

        def fin_A(vec, sums, ssqs, m_t, tmpa, tmpb, sl):
            dfence(vec, vec.tensor_scalar(out=m_t[:, sl], in0=sums[:, sl], scalar1=IVD,
                                          scalar2=0.0, op0=ALU.mult, op1=ALU.subtract))
            dfence(vec, vec.tensor_tensor(out=tmpa[:, sl], in0=m_t[:, sl], in1=m_t[:, sl],
                                          op=ALU.mult))
            dfence(vec, vec.tensor_scalar(out=tmpa[:, sl], in0=tmpa[:, sl], scalar1=1.0,
                                          scalar2=EPS, op0=ALU.mult, op1=ALU.subtract))
            return vec.scalar_tensor_tensor(out=tmpb[:, sl], in0=ssqs[:, sl], scalar=IVD,
                                            in1=tmpa[:, sl], op0=ALU.mult, op1=ALU.subtract)

        def fin_act(scalar_, tmpb, r_t, sl):
            afence(scalar_, scalar_.activation(out=tmpb[:, sl], in_=tmpb[:, sl], func=AF.Ln))
            return scalar_.activation(out=r_t[:, sl], in_=tmpb[:, sl], func=AF.Exp,
                                      scale=b_nhalf[:, :])

        def fin_B(vec, tmpb, m_t, r_t, mr_t, sl):
            return vec.tensor_tensor(out=mr_t[:, sl], in0=m_t[:, sl], in1=r_t[:, sl], op=ALU.mult)

        NGB = NSW * NHB
        cg_i = [0]; cu_i = [0]

        def emit_gdq(vec, lim):
            while cg_i[0] < min(lim, NGB * DC):
                ig = cg_i[0]
                gb, dc = divmod(ig, DC)
                vec.wait_ge(s_rg[ig % RG], 16 * (ig // RG + 1))
                if dc == 0 and gb >= 2:
                    vec.wait_ge(s_gmm, 2 * HCB * (gb - 1) - 1)
                vec.tensor_scalar(out=b_wg[(gb % 2) * DC + dc][:, :],
                                  in0=b_wraw_g[ig % RG][:, :],
                                  scalar1=128.0, scalar2=1.0,
                                  op0=ALU.subtract, op1=ALU.mult).then_inc(s_gdq, 1)
                cg_i[0] += 1

        def emit_udq(scalar_, lim):
            while cu_i[0] < min(lim, NGB * DC):
                iu = cu_i[0]
                gb, dc = divmod(iu, DC)
                scalar_.wait_ge(s_ru[iu % RU], 16 * (iu // RU + 1))
                if iu == 0:
                    scalar_.wait_ge(s_ppc, 9)
                if dc == 0 and gb >= 1:
                    scalar_.wait_ge(s_gmm, 2 * HCB * gb)
                scalar_.activation(out=b_wu[dc][:, :], in_=b_wraw_u[iu % RU][:, :],
                                   func=AF.Identity,
                                   bias=b_neg128[:, :]).then_inc(s_udq, 1)
                cu_i[0] += 1

        A = contextlib.ExitStack()
        a = A.enter_context
        with A:
            P0 = contextlib.ExitStack()
            p0 = P0.enter_context
            b_wabF = p0(nc.sbuf_tensor("b_wabF", [128, DC * S2], F32))
            b_woF = p0(nc.sbuf_tensor("b_woF", [S, D], F32))
            b_boutF = p0(nc.sbuf_tensor("b_boutF", [1, D], F32))
            b_sdrF = p0(nc.sbuf_tensor("b_sdrF", [1, D], F32))
            P0.close()
            b_x = [a(nc.sbuf_tensor(f"b_x{i}", [128, D], F32)) for i in range(NXB)]
            b_xn1 = [a(nc.sbuf_tensor(f"b_xn1{i}", [128, D], BF16)) for i in range(2)]
            b_xt = [a(nc.sbuf_tensor(f"b_xt{i}", [128, QTB], BF16)) for i in range(2 * DC)]
            b_ab = a(nc.sbuf_tensor("b_ab", [S, 2 * TSEQ], F32))
            b_hs = a(nc.sbuf_tensor("b_hs", [S, TSEQ], F32))

            @block.sync
            def _(sync):
                for dst, srcap in [(b_wabF[:, :].rearrange("p (c s) -> p c s", s=S2),
                                    wab_t.rearrange("(c p) s -> p c s", p=128)),
                                   (b_abbias[:, :], ab_bias),
                                   (b_woF[:, :], wout_t),
                                   (b_boutF[:, :], bout_row), (b_onesF[:, :], ones_r),
                                   (b_sel0[:, :], sel0), (b_sel1[:, :], sel1),
                                   (b_sg[:, :], sg_t), (b_su[:, :], su_t),
                                   (b_sdrF[:, :], sd_row)]:
                    sync.dma_start(out=dst, in_=srcap).then_inc(s_pp, 16)
                LAG = FSTB + 1
                sync.wait_ge(s_ppc, 9)
                sync.wait_ge(s_sdp, 1)

                def xt_round(q):
                    K = RPQ * (q + 1)
                    sync.wait_ge(s_x1s[0], 16 * _cnt(K, 0, 2))
                    sync.wait_ge(s_x1s[1], 16 * _cnt(K, 1, 2))
                    if q >= 2:
                        sync.wait_ge(s_abp, NPJQ * (q - 1))
                    for dc in range(DC):
                        sync.dma_start(out=b_xt[(q % 2) * DC + dc][:, :],
                                       in_=xn1b_h[q * QTB:(q + 1) * QTB, dc * 128:(dc + 1) * 128],
                                       transpose=True).then_inc(s_xt, 16)

                nxt_q = [0]
                for tc in range(TQ):
                    if tc >= NXB:
                        sync.wait_ge(s_ap, tc - NXB + 1)
                        sync.wait_ge(s_red, tc - NXB + 1)
                        sync.wait_ge(s_ssq, tc - NXB + 1)
                    sync.dma_start(out=b_x[tc % NXB][:, :],
                                   in_=x_seq[tc * 128:(tc + 1) * 128, :]).then_inc(s_xld[tc % NXB], 16)
                    if tc >= LAG:
                        st_c = tc - LAG
                        sync.wait_ge(s_ap, st_c + 1)
                        sync.dma_start(out=xn1b_h[st_c * 128:(st_c + 1) * 128, :],
                                       in_=b_xn1[st_c % 2][:, :]).then_inc(s_x1s[st_c % 2], 16)
                        if st_c + 1 >= RPQ * (nxt_q[0] + 1) and nxt_q[0] < NQ:
                            xt_round(nxt_q[0])
                            nxt_q[0] += 1
                for st_c in range(max(0, TQ - LAG), TQ):
                    sync.wait_ge(s_ap, st_c + 1)
                    sync.dma_start(out=xn1b_h[st_c * 128:(st_c + 1) * 128, :],
                                   in_=b_xn1[st_c % 2][:, :]).then_inc(s_x1s[st_c % 2], 16)
                    if st_c + 1 >= RPQ * (nxt_q[0] + 1) and nxt_q[0] < NQ:
                        xt_round(nxt_q[0])
                        nxt_q[0] += 1
                while nxt_q[0] < NQ:
                    xt_round(nxt_q[0])
                    nxt_q[0] += 1

            @block.gpsimd
            def _(gpsimd):
                lg = [0]; lu = [0]; lw = [0]

                def g_tile():
                    ig = lg[0]
                    gb, dc = divmod(ig, DC)
                    sw, hb = divmod(gb, NHB)
                    if ig >= RG:
                        gpsimd.wait_ge(s_gdq, ig - RG + 1)
                    gpsimd.dma_start(out=b_wraw_g[ig % RG][:, :],
                                     in_=wqg_t[dc * 128:(dc + 1) * 128,
                                               hb * HB:(hb + 1) * HB]).then_inc(s_rg[ig % RG], 16)
                    lg[0] += 1

                def u_tile():
                    iu = lu[0]
                    gb, dc = divmod(iu, DC)
                    sw, hb = divmod(gb, NHB)
                    if iu >= RU:
                        gpsimd.wait_ge(s_udq, iu - RU + 1)
                    gpsimd.dma_start(out=b_wraw_u[iu % RU][:, :],
                                     in_=wqu_t[dc * 128:(dc + 1) * 128,
                                               hb * HB:(hb + 1) * HB]).then_inc(s_ru[iu % RU], 16)
                    lu[0] += 1

                def w_tile():
                    k = lw[0]
                    sw, r = divmod(k, NDS * HC)
                    ds, hc = divmod(r, HC)
                    if k >= RD:
                        gpsimd.wait_ge(s_dwq, k - RD + 1)
                    gpsimd.dma_start(out=b_wdraw[k % RD][:, :],
                                     in_=wqd_t[hc * 128:(hc + 1) * 128,
                                               ds * DSL:(ds + 1) * DSL]).then_inc(s_rd[k % RD], 16)
                    lw[0] += 1

                NWDK = NSW * NDS * HC

                def g_blk():
                    if lg[0] < NGB * DC:
                        for _ in range(DC):
                            g_tile()

                def u_blk():
                    if lu[0] < NGB * DC:
                        for _ in range(DC):
                            u_tile()

                g_blk(); g_blk(); u_blk()
                for gb in range(NHB):
                    u_blk()
                    g_blk()
                while lw[0] < NDS * HC:
                    w_tile()
                for gb in range(NHB, 2 * NHB):
                    u_blk()
                    g_blk()
                while lw[0] < NWDK:
                    w_tile()


            @block.vector
            def _(vector):
                vector.wait_ge(s_pp, 16 * 10)
                vector.tensor_copy(out=b_wabT[:, :], in_=b_wabF[:, :]).then_inc(s_ppc, 1)
                vector.tensor_copy(out=b_woT[:, :], in_=b_woF[:, :]).then_inc(s_ppc, 1)
                vector.tensor_copy(out=b_bout_b[:, :], in_=b_boutF[:, :]).then_inc(s_ppc, 1)
                vector.tensor_copy(out=b_ones_b[:, :], in_=b_onesF[:, :]).then_inc(s_ppc, 1)
                vector.memset(b_zero[:, :], 0.0).then_inc(s_ppc, 1)
                vector.memset(b_one[:, :], 1.0).then_inc(s_ppc, 1)
                vector.memset(b_nhalf[:, :], -0.5).then_inc(s_ppc, 1)
                vector.memset(b_neg128[:, :], -128.0).then_inc(s_ppc, 1)
                vector.wait_ge(s_sdp, 1)
                vector.tensor_copy(out=b_sdbc[:, :], in_=pall[:, 0:D]).then_inc(s_ppc, 1)

                for bi in range(TQ // FSTB):
                    lo = bi * FSTB; hi = lo + FSTB; sl = slice(lo, hi)
                    for tc in range(lo, hi):
                        vector.wait_ge(s_xld[tc % NXB], 16 * (tc // NXB + 1))
                        vector.tensor_reduce(out=b_sum1[:, tc:tc + 1], in_=b_x[tc % NXB][:, :],
                                             axis=AX.X, op=ALU.add).then_inc(s_red, 1)
                        vector.scalar_tensor_tensor(out=b_sqs[:, :], in0=b_x[tc % NXB][:, :],
                                                    scalar=1.0, in1=b_x[tc % NXB][:, :],
                                                    op0=ALU.mult, op1=ALU.mult,
                                                    accum_out=b_ssq1[:, tc:tc + 1]).then_inc(s_ssq, 1)
                    vector.wait_ge(s_red, hi)
                    vector.wait_ge(s_ssq, hi)
                    fin_A(vector, b_sum1, b_ssq1, b_m1, b_tq1, b_tq2, sl).then_inc(s_fa, 1)
                    vector.wait_ge(s_fb, bi + 1)
                    fin_B(vector, b_tq2, b_m1, b_r1, b_mr1, sl).then_inc(s_f1v, 1)
                    vector.wait_ge(s_f1v, bi + 1)
                    for tc in range(lo, hi):
                        if tc >= 2:
                            vector.wait_ge(s_x1s[tc % 2], 16 * ((tc - 2) // 2 + 1))
                        vector.tensor_scalar(out=b_xn1[tc % 2][:, :], in0=b_x[tc % NXB][:, :],
                                             scalar1=b_r1[:, tc:tc + 1], scalar2=b_mr1[:, tc:tc + 1],
                                             op0=ALU.mult, op1=ALU.subtract).then_inc(s_ap, 1)

                vector.wait_ge(s_th, 2 * NPJ)
                vector.wait_ge(s_ssq, TQ)
                vector.tensor_tensor_scan(out=b_hs[:, :], data0=b_ab[:, 0:TSEQ],
                                          data1=b_ab[:, TSEQ:2 * TSEQ],
                                          initial=b_zero[:, :], op0=ALU.mult, op1=ALU.add).then_inc(s_hs, 1)
                vector.wait_ge(s_hs, 1)
                vector.tensor_scalar(out=b_ab[0:S, 0:TOWN], in0=b_hs[:, 0:TOWN],
                                     scalar1=b_sel0[0:S, :], scalar2=0.0,
                                     op0=ALU.mult, op1=ALU.subtract).then_inc(s_f1v, 1)
                vector.wait_ge(s_f1v, TQ // FSTB + 1)
                vector.scalar_tensor_tensor(out=b_hsb[:, :], in0=b_hs[:, TSEQ - TOWN:TSEQ],
                                            scalar=b_sel1[0:S, :], in1=b_ab[0:S, 0:TOWN],
                                            op0=ALU.mult, op1=ALU.add).then_inc(s_hs, 1)
                vector.tensor_scalar(out=b_tc1[:, :], in0=b_r1[:, 0:TC], scalar1=b_sel0[:, :],
                                     scalar2=0.0, op0=ALU.mult, op1=ALU.subtract).then_inc(s_f1v, 1)
                vector.wait_ge(s_f1v, TQ // FSTB + 2)
                vector.scalar_tensor_tensor(out=b_r1o[:, :], in0=b_r1[:, TQ - TC:TQ],
                                            scalar=b_sel1[:, :], in1=b_tc1[:, :],
                                            op0=ALU.mult, op1=ALU.add).then_inc(s_f1v, 1)
                vector.wait_ge(s_f1v, TQ // FSTB + 3)
                vector.tensor_scalar(out=b_tc1[:, :], in0=b_mr1[:, 0:TC], scalar1=b_sel0[:, :],
                                     scalar2=0.0, op0=ALU.mult, op1=ALU.subtract).then_inc(s_f1v, 1)
                vector.wait_ge(s_f1v, TQ // FSTB + 4)
                vector.scalar_tensor_tensor(out=b_mr1o[:, :], in0=b_mr1[:, TQ - TC:TQ],
                                            scalar=b_sel1[:, :], in1=b_tc1[:, :],
                                            op0=ALU.mult, op1=ALU.add).then_inc(s_hs, 1)

            @block.scalar
            def _(scalar):
                def tanh_blk(k):
                    scalar.wait_ge(s_abp, k + 1)
                    scalar.activation(out=b_ab[:, k * PJB:(k + 1) * PJB],
                                      in_=pall[0:S, (k % 2) * 512:(k % 2) * 512 + PJB],
                                      func=AF.Tanh, bias=b_abbias[:, 0:1]).then_inc(s_th, 1)
                    scalar.activation(out=b_ab[:, TSEQ + k * PJB:TSEQ + (k + 1) * PJB],
                                      in_=pall[0:S, 1024 + (k % 2) * 512:1024 + (k % 2) * 512 + PJB],
                                      func=AF.Tanh, bias=b_abbias[:, 1:2]).then_inc(s_th, 1)

                nxt_k = [0]
                for bi in range(TQ // FSTB):
                    scalar.wait_ge(s_fa, bi + 1)
                    fin_act(scalar, b_tq2, b_r1, slice(bi * FSTB, (bi + 1) * FSTB)).then_inc(s_fb, 1)
                    while (nxt_k[0] < NPJ
                           and RPQ * (nxt_k[0] // NPJQ + 1) <= FSTB * bi):
                        tanh_blk(nxt_k[0])
                        nxt_k[0] += 1
                while nxt_k[0] < NPJ:
                    tanh_blk(nxt_k[0])
                    nxt_k[0] += 1

            @block.tensor
            def _(tensor):
                tensor.wait_ge(s_pp, 16 * 10)
                for j in range(D // JW):
                    mm = tensor.matmul(pall[:, j * JW:(j + 1) * JW], b_onesF[:, :],
                                       b_sdrF[:, j * JW:(j + 1) * JW], start=True, stop=True)
                mm.then_inc(s_sdp, 1)
                tensor.wait_ge(s_ppc, 9)
                for q in range(NQ):
                    tensor.wait_ge(s_xt, 16 * DC * (q + 1))
                    for pb in range(NPJQ):
                        k = q * NPJQ + pb
                        if k >= 2:
                            tensor.wait_ge(s_th, 2 * (k - 1))
                        po_a = (k % 2) * 512
                        po_b = 1024 + (k % 2) * 512
                        xbuf = (q % 2) * DC
                        for dc in range(DC):
                            tensor.matmul(pall[0:S, po_a:po_a + PJB],
                                          b_wabT[:, dc * S2:dc * S2 + S],
                                          b_xt[xbuf + dc][:, pb * PJB:(pb + 1) * PJB],
                                          start=(dc == 0), stop=(dc == DC - 1))
                            mm = tensor.matmul(pall[0:S, po_b:po_b + PJB],
                                               b_wabT[:, dc * S2 + S:dc * S2 + S2],
                                               b_xt[xbuf + dc][:, pb * PJB:(pb + 1) * PJB],
                                               start=(dc == 0), stop=(dc == DC - 1))
                        mm.then_inc(s_abp, 1)

        B1 = contextlib.ExitStack()
        b1 = B1.enter_context
        with B1:
            b_xo = [b1(nc.sbuf_tensor(f"b_xo{i}", [128, D], F32)) for i in range(NOB)]
            b_txn = [b1(nc.sbuf_tensor(f"b_txn{i}", [128, D], F32)) for i in range(1)]
            b_y1 = [b1(nc.sbuf_tensor(f"b_y1{i}", [128, D], F32)) for i in range(3)]
            b_t2 = [b1(nc.sbuf_tensor(f"b_t2{i}", [128, D], F32)) for i in range(1)]
            b_x2 = [b1(nc.sbuf_tensor(f"b_x2{i}", [128, D], F32)) for i in range(3)]
            b_xn2 = [b1(nc.sbuf_tensor(f"b_xn2{i}", [128, D], BF16)) for i in range(2)]

            @block.sync
            def _(sync):
                sync.wait_ge(s_hs, 3)
                sync.wait_ge(s_ap, TQ)
                sync.wait_ge(s_ssq, TQ)
                sync.wait_ge(s_x1s[0], 16 * _cnt(TQ, 0, 2))
                sync.wait_ge(s_x1s[1], 16 * _cnt(TQ, 1, 2))
                sync.wait_ge(s_abp, NPJ)
                for oc in range(TC):
                    if oc >= NOB:
                        sync.wait_ge(s_x2, oc - NOB + 1)
                    sync.dma_start(out=b_xo[oc % NOB][:, :],
                                   in_=x_own[oc * 128:(oc + 1) * 128, :]).then_inc(s_xo[oc % NOB], 16)
                    if oc >= 2:
                        oc2 = oc - 2
                        sync.wait_ge(s_x2, oc2 + 1)
                        sync.dma_start(out=x2_h[oc2 * 128:(oc2 + 1) * 128, :],
                                       in_=b_x2[oc2 % 3][:, :]).then_inc(s_x2s[oc2 % 3], 16)
                        sync.wait_ge(s_xn2, oc2 + 1)
                        sync.dma_start(out=xn2b_h[oc2 * 128:(oc2 + 1) * 128, :],
                                       in_=b_xn2[oc2 % 2][:, :]).then_inc(s_n2s[oc2 % 2], 16)
                for oc2 in range(TC - 2, TC):
                    sync.wait_ge(s_x2, oc2 + 1)
                    sync.dma_start(out=x2_h[oc2 * 128:(oc2 + 1) * 128, :],
                                   in_=b_x2[oc2 % 3][:, :]).then_inc(s_x2s[oc2 % 3], 16)
                    sync.wait_ge(s_xn2, oc2 + 1)
                    sync.dma_start(out=xn2b_h[oc2 * 128:(oc2 + 1) * 128, :],
                                   in_=b_xn2[oc2 % 2][:, :]).then_inc(s_n2s[oc2 % 2], 16)

            @block.vector
            def _(vector):
                for bi in range(TC // OSTB):
                    lo = bi * OSTB; hi = lo + OSTB; sl = slice(lo, hi)
                    for oc in range(lo, hi):
                        vector.wait_ge(s_xo[oc % NOB], 16 * (oc // NOB + 1))
                        vector.wait_ge(s_wps, oc + 1)
                        if oc >= 3:
                            vector.wait_ge(s_sq2, oc - 2)
                        vector.tensor_scalar(out=b_txn[0][:, :], in0=b_xo[oc % NOB][:, :],
                                             scalar1=b_r1o[:, oc:oc + 1], scalar2=b_mr1o[:, oc:oc + 1],
                                             op0=ALU.mult, op1=ALU.subtract)
                        vector.scalar_tensor_tensor(out=b_y1[oc % 3][:, :], in0=b_txn[0][:, :],
                                                    scalar=1.0, in1=pall[:, 0:D],
                                                    op0=ALU.mult, op1=ALU.add,
                                                    accum_out=b_sum2[:, oc:oc + 1]).then_inc(s_y1, 1)
                    vector.wait_ge(s_sq2, hi)
                    vector.wait_ge(s_y1, hi)
                    fin_A(vector, b_sum2, b_ssq2, b_m2, b_tc1, b_tc2, sl).then_inc(s_fa2, 1)
                    vector.wait_ge(s_fb2, 2 * bi + 1)
                    dfence(vector, fin_B(vector, b_tc2, b_m2, b_r2, b_mr2, sl))
                    for oc in range(lo, hi):
                        if oc >= 3:
                            vector.wait_ge(s_x2s[(oc - 3) % 3], 16 * ((oc - 3) // 3 + 1))
                            vector.wait_ge(s_sq3, oc - 2)
                        vector.tensor_scalar(out=b_t2[0][:, :], in0=b_y1[oc % 3][:, :],
                                             scalar1=b_r2[:, oc:oc + 1], scalar2=b_mr2[:, oc:oc + 1],
                                             op0=ALU.mult, op1=ALU.subtract)
                        vector.scalar_tensor_tensor(out=b_x2[oc % 3][:, :], in0=b_t2[0][:, :],
                                                    scalar=1.0, in1=b_xo[oc % NOB][:, :],
                                                    op0=ALU.mult, op1=ALU.add,
                                                    accum_out=b_sum3[:, oc:oc + 1]).then_inc(s_x2, 1)
                    vector.wait_ge(s_sq3, hi)
                    vector.wait_ge(s_x2, hi)
                    fin_A(vector, b_sum3, b_ssq3, b_m3, b_tc1, b_tc2, sl).then_inc(s_fa2, 1)
                    vector.wait_ge(s_fb2, 2 * bi + 2)
                    dfence(vector, fin_B(vector, b_tc2, b_m3, b_r3, b_mr3, sl))
                    for oc in range(lo, hi):
                        if oc >= 2:
                            vector.wait_ge(s_n2s[oc % 2], 16 * ((oc - 2) // 2 + 1))
                        vector.tensor_scalar(out=b_xn2[oc % 2][:, :], in0=b_x2[oc % 3][:, :],
                                             scalar1=b_r3[:, oc:oc + 1], scalar2=b_mr3[:, oc:oc + 1],
                                             op0=ALU.mult, op1=ALU.subtract).then_inc(s_xn2, 1)

            @block.scalar
            def _(scalar):
                for bi in range(TC // OSTB):
                    lo = bi * OSTB; hi = lo + OSTB
                    for oc in range(lo, hi):
                        scalar.wait_ge(s_y1, oc + 1)
                        scalar.activation(out=b_sqs[:, :], in_=b_y1[oc % 3][:, :], func=AF.Square,
                                          accum_out=b_ssq2[:, oc:oc + 1]).then_inc(s_sq2, 1)
                    scalar.wait_ge(s_fa2, 2 * bi + 1)
                    fin_act(scalar, b_tc2, b_r2, slice(lo, hi)).then_inc(s_fb2, 1)
                    for oc in range(lo, hi):
                        scalar.wait_ge(s_x2, oc + 1)
                        scalar.activation(out=b_sqs[:, :], in_=b_x2[oc % 3][:, :], func=AF.Square,
                                          accum_out=b_ssq3[:, oc:oc + 1]).then_inc(s_sq3, 1)
                    scalar.wait_ge(s_fa2, 2 * bi + 2)
                    fin_act(scalar, b_tc2, b_r3, slice(lo, hi)).then_inc(s_fb2, 1)

            @block.tensor
            def _(tensor):
                for oc in range(TC):
                    tensor.wait_ge(s_hs, 2)
                    if oc >= 1:
                        tensor.wait_ge(s_y1, oc)
                    for j in range(NJW):
                        tensor.matmul(pall[:, j * JW:(j + 1) * JW],
                                      b_hsb[:, oc * 128:(oc + 1) * 128],
                                      b_woT[:, j * JW:(j + 1) * JW],
                                      start=True, stop=False)
                        mm = tensor.matmul(pall[:, j * JW:(j + 1) * JW],
                                           b_ones_b[:, :], b_bout_b[:, j * JW:(j + 1) * JW],
                                           start=False, stop=True)
                    mm.then_inc(s_wps, 1)

        B2 = contextlib.ExitStack()
        b2 = B2.enter_context
        with B2:
            b_xn2T = [b2(nc.sbuf_tensor(f"b_xn2T{i}", [128, TSL], BF16)) for i in range(DC)]
            b_wg = [b2(nc.sbuf_tensor(f"b_wg{i}", [128, HB], BF16)) for i in range(2 * DC)]
            b_wu = [b2(nc.sbuf_tensor(f"b_wu{i}", [128, HB], BF16)) for i in range(DC)]
            b_gt_ = [b2(nc.sbuf_tensor(f"b_gt{i}", [128, TSL], BF16)) for i in range(2)]
            b_ut = [b2(nc.sbuf_tensor(f"b_ut{i}", [128, TSL], BF16)) for i in range(2)]
            b_pw = [b2(nc.sbuf_tensor(f"b_pw{i}", [128, TSL], BF16)) for i in range(2)]
            b_pst = [b2(nc.sbuf_tensor(f"b_pst{i}", [128, TSL], BF16)) for i in range(NPS)]
            b_wd = [b2(nc.sbuf_tensor(f"b_wd{i}", [128, DSL], BF16)) for i in range(NWD)]
            b_f = [b2(nc.sbuf_tensor(f"b_f{i}", [128, D], BF16)) for i in range(TCS)]
            b_x2r = [b2(nc.sbuf_tensor(f"b_x2r{i}", [128, D], F32)) for i in range(2)]

            @block.sync
            def _(sync):
                for j in range(3):
                    sync.wait_ge(s_x2s[j], 16 * _cnt(TC, j, 3))
                for j in range(2):
                    sync.wait_ge(s_n2s[j], 16 * _cnt(TC, j, 2))
                sync.wait_ge(s_sq2, TC)
                sync.wait_ge(s_sq3, TC)
                sync.wait_ge(s_xn2, TC)
                def emit_n2t(sw):
                    if sw > 0:
                        sync.wait_ge(s_gmm, 2 * HC * sw)
                    for dc in range(DC):
                        sync.dma_start(out=b_xn2T[dc][:, :],
                                       in_=xn2b_h[sw * TSL:(sw + 1) * TSL, dc * 128:(dc + 1) * 128],
                                       transpose=True).then_inc(s_n2t, 16)

                emit_n2t(0)
                for sw in range(NSW):
                    def emit_prod_store(hc):
                        ghc = sw * HC + hc
                        sync.wait_ge(s_pr, ghc + 1)
                        sync.dma_start(out=prod_h[hc * 128:(hc + 1) * 128, :],
                                       in_=b_pw[ghc % 2][:, :]).then_inc(s_prs[ghc % 2], 16)

                    for hc in range(HC):
                        emit_prod_store(hc)
                    for ds in range(NDS):
                        if ds == min(1, NDS - 1) and sw + 1 < NSW:
                            emit_n2t(sw + 1)
                        for hc in range(HC):
                            k = sw * NDS * HC + ds * HC + hc
                            ghc = sw * HC + hc
                            sync.wait_ge(s_prs[ghc % 2], 16 * (ghc // 2 + 1))
                            if k >= NPS:
                                sync.wait_ge(s_dmm, k - NPS + 1)
                            sync.dma_start(out=b_pst[k % NPS][:, :],
                                           in_=prod_h[hc * 128:(hc + 1) * 128, :]).then_inc(s_psl[k % NPS], 16)
                    for tc in range(TCS):
                        gtc = sw * TCS + tc
                        if gtc >= 2:
                            sync.wait_ge(s_ys[gtc % 2], 16 * ((gtc - 2) // 2 + 1))
                        sync.dma_start(out=b_x2r[gtc % 2][:, :],
                                       in_=x2_h[gtc * 128:(gtc + 1) * 128, :]).then_inc(s_x2r[gtc % 2], 16)
                        sync.wait_ge(s_yo, gtc + 1)
                        sync.dma_start(out=y_out[gtc * 128:(gtc + 1) * 128, :],
                                       in_=b_x2r[gtc % 2][:, :]).then_inc(s_ys[gtc % 2], 16)
                sync.wait_ge(s_ys[0], 16 * _cnt(TC, 0, 2))
                sync.wait_ge(s_ys[1], 16 * _cnt(TC, 1, 2))
                if dbg_st is not None:
                    for i, t in enumerate([b_sum1, b_ssq1, b_m1, b_r1, b_mr1, b_tq2]):
                        sync.dma_start(out=dbg_st[i * 128:(i + 1) * 128, :],
                                       in_=t[:, :]).then_inc(s_pp, 16)
                    sync.wait_ge(s_pp, 16 * 16)

            @block.tensor
            def _(tensor):
                for sw in range(NSW):
                    tensor.wait_ge(s_n2t, 16 * DC * (sw + 1))
                    for hc in range(HC):
                        hb, hcc = divmod(hc, HCB)
                        tensor.wait_ge(s_gdq, DC * (sw * NHB + hb + 1))
                        if hc >= 2:
                            tensor.wait_ge(s_gev, 2 * HC * sw + 2 * (hc - 2) + 1)
                        elif sw > 0:
                            tensor.wait_ge(s_dev, sw * NDS * TCS)
                        po = (4 + 2 * (hc % 2)) * 512
                        gbuf = ((sw * NHB + hb) % 2) * DC
                        for dc in range(DC):
                            for j in range(NJ5):
                                mm = tensor.matmul(pall[:, po + j * T5:po + (j + 1) * T5],
                                                   b_wg[gbuf + dc][:, hcc * 128:(hcc + 1) * 128],
                                                   b_xn2T[dc][:, j * T5:(j + 1) * T5],
                                                   start=(dc == 0), stop=(dc == DC - 1))
                        mm.then_inc(s_gmm, 1)
                        tensor.wait_ge(s_udq, DC * (sw * NHB + hb + 1))
                        if hc >= 2:
                            tensor.wait_ge(s_gev, 2 * HC * sw + 2 * (hc - 2) + 2)
                        elif sw == 0:
                            tensor.wait_ge(s_y1, TC)
                        else:
                            tensor.wait_ge(s_dev, sw * NDS * TCS)
                        po = (2 * (hc % 2)) * 512
                        for dc in range(DC):
                            for j in range(NJ5):
                                mm = tensor.matmul(pall[:, po + j * T5:po + (j + 1) * T5],
                                                   b_wu[dc][:, hcc * 128:(hcc + 1) * 128],
                                                   b_xn2T[dc][:, j * T5:(j + 1) * T5],
                                                   start=(dc == 0), stop=(dc == DC - 1))
                        mm.then_inc(s_gmm, 1)
                    for ds in range(NDS):
                        if ds == 0:
                            tensor.wait_ge(s_gev, 2 * HC * (sw + 1))
                        else:
                            tensor.wait_ge(s_dev, (sw * NDS + ds) * TCS)
                        for hc in range(HC):
                            k = sw * NDS * HC + ds * HC + hc
                            tensor.wait_ge(s_psl[k % NPS], 16 * (k // NPS + 1))
                            tensor.wait_ge(s_dwq, k + 1)
                            for tc in range(TCS):
                                mm = tensor.matmul(pall[:, tc * 512:tc * 512 + DSL],
                                                   b_pst[k % NPS][:, tc * 128:(tc + 1) * 128],
                                                   b_wd[k % NWD][:, :],
                                                   start=(hc == 0), stop=(hc == HC - 1))
                            mm.then_inc(s_dmm, 1)

            @block.scalar
            def _(scalar):
                emit_udq(scalar, DC)
                for sw in range(NSW):
                    for hc in range(HC):
                        ghc = sw * HC + hc
                        gb = sw * NHB + hc // HCB
                        hcc = hc % HCB
                        scalar.wait_ge(s_gmm, 2 * HC * sw + 2 * hc + 1)
                        if ghc >= 2:
                            scalar.wait_ge(s_pr, ghc - 1)
                        po = (4 + 2 * (hc % 2)) * 512
                        scalar.activation(out=b_gt_[hc % 2][:, :], in_=pall[:, po:po + TSL],
                                          func=AF.Silu,
                                          scale=b_sg[:, hc:hc + 1]).then_inc(s_gev, 1)
                        scalar.wait_ge(s_gmm, 2 * HC * sw + 2 * hc + 2)
                        po = (2 * (hc % 2)) * 512
                        scalar.activation(out=b_ut[hc % 2][:, :], in_=pall[:, po:po + TSL],
                                          func=AF.Identity,
                                          scale=b_su[:, hc:hc + 1]).then_inc(s_gev, 1)
                        if hcc == HCB - 1:
                            emit_udq(scalar, DC * (gb + 2))
                    for ds in range(NDS):
                        for tc in range(TCS):
                            scalar.wait_ge(s_dmm, sw * NDS * HC + ds * HC + HC)
                            if sw > 0 and ds == 0:
                                scalar.wait_ge(s_yo, sw * TCS)
                            scalar.activation(out=b_f[tc][:, ds * DSL:(ds + 1) * DSL],
                                              in_=pall[:, tc * 512:tc * 512 + DSL],
                                              func=AF.Copy).then_inc(s_dev, 1)
                    for tc in range(TCS):
                        gtc = sw * TCS + tc
                        scalar.activation(out=b_sqs[:, :], in_=b_f[tc][:, :], func=AF.Copy,
                                          accum_out=b_sum4[:, gtc:gtc + 1]).then_inc(s_sq4, 1)
                        scalar.activation(out=b_sqs[:, :], in_=b_f[tc][:, :], func=AF.Square,
                                          accum_out=b_ssq4[:, gtc:gtc + 1]).then_inc(s_sq4, 1)
                    scalar.wait_ge(s_fa4, sw + 1)
                    fin_act(scalar, b_tc2, b_r4, slice(sw * TCS, (sw + 1) * TCS)).then_inc(s_fb4, 1)

            @block.vector
            def _(vector):
                cw_i = [0]

                def emit_wdq(lim):
                    while cw_i[0] < min(lim, NSW * NDS * HC):
                        k = cw_i[0]
                        swk, r = divmod(k, NDS * HC)
                        ds = r // HC
                        vector.wait_ge(s_rd[k % RD], 16 * (k // RD + 1))
                        if k >= NWD:
                            vector.wait_ge(s_dmm, k - NWD + 1)
                        vector.scalar_tensor_tensor(out=b_wd[k % NWD][:, :],
                                                    in0=b_wdraw[k % RD][:, :],
                                                    scalar=128.0,
                                                    in1=b_sdbc[:, ds * DSL:(ds + 1) * DSL],
                                                    op0=ALU.subtract,
                                                    op1=ALU.mult).then_inc(s_dwq, 1)
                        cw_i[0] += 1

                emit_gdq(vector, 2 * DC)
                for sw in range(NSW):
                    for hc in range(HC):
                        ghc = sw * HC + hc
                        gb = sw * NHB + hc // HCB
                        hcc = hc % HCB
                        vector.wait_ge(s_gev, 2 * HC * sw + 2 * (hc + 1))
                        if ghc >= 2:
                            vector.wait_ge(s_prs[ghc % 2], 16 * ((ghc - 2) // 2 + 1))
                        vector.tensor_tensor(out=b_pw[ghc % 2][:, :], in0=b_gt_[hc % 2][:, :],
                                             in1=b_ut[hc % 2][:, :], op=ALU.mult).then_inc(s_pr, 1)
                        if hcc == HCB - 1:
                            emit_gdq(vector, DC * (gb + 3))
                    emit_wdq((sw + 1) * NDS * HC)
                    sl = slice(sw * TCS, (sw + 1) * TCS)
                    vector.wait_ge(s_sq4, 2 * (sw + 1) * TCS)
                    fin_A(vector, b_sum4, b_ssq4, b_m4, b_tc1, b_tc2, sl).then_inc(s_fa4, 1)
                    vector.wait_ge(s_fb4, sw + 1)
                    dfence(vector, fin_B(vector, b_tc2, b_m4, b_r4, b_mr4, sl))
                    for tc in range(TCS):
                        gtc = sw * TCS + tc
                        vector.wait_ge(s_x2r[gtc % 2], 16 * (gtc // 2 + 1))
                        vector.tensor_scalar(out=b_x2r[gtc % 2][:, :], in0=b_x2r[gtc % 2][:, :],
                                             scalar1=1.0, scalar2=b_mr4[:, gtc:gtc + 1],
                                             op0=ALU.mult, op1=ALU.subtract)
                        vector.scalar_tensor_tensor(out=b_x2r[gtc % 2][:, :], in0=b_f[tc][:, :],
                                                    scalar=b_r4[:, gtc:gtc + 1],
                                                    in1=b_x2r[gtc % 2][:, :],
                                                    op0=ALU.mult, op1=ALU.add).then_inc(s_yo, 1)

    return nc


def kernel(L0_attn_x, L0_attn_Wa, L0_attn_ba, L0_attn_Wb, L0_attn_bb,
           L0_attn_Wout, L0_attn_bout, L0_ffn_wq_gate, L0_ffn_s_gate,
           L0_ffn_wq_up, L0_ffn_s_up, L0_ffn_wq_down, L0_ffn_s_down,
           _cfg=None, _n_cores=8):
    cfg = dict(FULL_CFG if _cfg is None else _cfg)
    D = cfg["D"]; TSEQ = cfg["TSEQ"]; TOWN = cfg["TOWN"]
    HC = cfg["H"] // 128; DC = D // 128
    x = np.ascontiguousarray(np.asarray(L0_attn_x, dtype=np.float32)).reshape(-1, TSEQ, D)
    B = x.shape[0]
    n_cores = _n_cores

    wab_t = np.ascontiguousarray(np.concatenate([np.asarray(L0_attn_Wa, np.float32),
                                                 np.asarray(L0_attn_Wb, np.float32)], axis=0).T)
    ab_bias = np.ascontiguousarray(np.stack([np.asarray(L0_attn_ba, np.float32),
                                             np.asarray(L0_attn_bb, np.float32)], axis=1))
    wout_t = np.ascontiguousarray(np.asarray(L0_attn_Wout, np.float32).T)
    bout_row = np.ascontiguousarray(np.asarray(L0_attn_bout, np.float32)[None, :])
    wqg_t = np.ascontiguousarray(np.asarray(L0_ffn_wq_gate, np.int32).T)
    wqu_t = np.ascontiguousarray(np.asarray(L0_ffn_wq_up, np.int32).T)
    wqd_t = np.ascontiguousarray(np.asarray(L0_ffn_wq_down, np.int32).T)
    sg_t = np.ascontiguousarray(np.asarray(L0_ffn_s_gate, np.float32).reshape(HC, 128).T)
    su_t = np.ascontiguousarray(np.asarray(L0_ffn_s_up, np.float32).reshape(HC, 128).T)
    sd_row = np.ascontiguousarray(np.asarray(L0_ffn_s_down, np.float32)[None, :])
    ones_r = np.ones((1, 128), np.float32)

    nc = build_nc(cfg)
    in_maps = []
    for c in range(n_cores):
        b, half = divmod(c, 2)
        in_maps.append(dict(
            x_seq=np.ascontiguousarray(x[b]),
            x_own=np.ascontiguousarray(x[b, half * TOWN:(half + 1) * TOWN]),
            sel0=np.full((128, 1), 1.0 - half, np.float32),
            sel1=np.full((128, 1), float(half), np.float32),
            wab_t=wab_t, ab_bias=ab_bias, wout_t=wout_t, bout_row=bout_row,
            wqg_t=wqg_t, wqu_t=wqu_t, wqd_t=wqd_t, sg_t=sg_t, su_t=su_t, sd_row=sd_row,
            ones_r=ones_r,
        ))
    res = run_bass_kernel_spmd(nc, in_maps, core_ids=list(range(n_cores)))
    out = np.concatenate([res.results[c]["y_out"] for c in range(n_cores)], axis=0)
    return out.reshape(B, TSEQ, D)


# version 16 (final)
# speedup vs baseline: 1.3009x; 1.0184x over previous
import contextlib
import numpy as np

import concourse.bass as bass
import concourse.mybir as mybir
from concourse.bass_utils import run_bass_kernel_spmd

F32 = mybir.dt.float32
BF16 = mybir.dt.bfloat16
I32 = mybir.dt.int32
AF = mybir.ActivationFunctionType
ALU = mybir.AluOpType
AX = mybir.AxisListType

FULL_CFG = dict(D=2048, S=64, TSEQ=4096, TOWN=2048, H=5632, NSW=2, EPS=1e-5)


def _cnt(K, j, NB):
    return (K - j + NB - 1) // NB


def build_nc(cfg, dbg=False):
    D = cfg["D"]; S = cfg["S"]; TSEQ = cfg["TSEQ"]; TOWN = cfg["TOWN"]
    H = cfg["H"]; NSW = cfg["NSW"]; EPS = cfg["EPS"]
    DC = D // 128
    TQ = TSEQ // 128
    TC = TOWN // 128
    HC = H // 128
    TSL = TOWN // NSW
    TCS = TSL // 128
    S2 = 2 * S
    QTB = min(512, TSEQ // 2)
    NQ = TSEQ // QTB
    RPQ = QTB // 128
    PJB = min(512, QTB)
    NPJQ = QTB // PJB
    NPJ = NQ * NPJQ
    HB = min(512, H)
    NHB = H // HB
    HCB = HB // 128
    DSL = min(512, D)
    NDS = D // DSL
    T5 = min(512, TSL)
    NJ5 = TSL // T5
    JW = min(512, D)
    NJW = D // JW
    FSTB = 4 if TQ % 4 == 0 else 1
    OSTB = 2 if TC % 2 == 0 else 1
    NXB = 5
    NOB = 3
    NWD = 4
    NPS = 6
    IVD = 1.0 / D

    nc = bass.Bass("TRN2", target_bir_lowering=False, detect_race_conditions=False)

    x_seq = nc.dram_tensor("x_seq", [TSEQ, D], F32, kind="ExternalInput").ap()
    x_own = nc.dram_tensor("x_own", [TOWN, D], F32, kind="ExternalInput").ap()
    sel0 = nc.dram_tensor("sel0", [128, 1], F32, kind="ExternalInput").ap()
    sel1 = nc.dram_tensor("sel1", [128, 1], F32, kind="ExternalInput").ap()
    wab_t = nc.dram_tensor("wab_t", [D, S2], F32, kind="ExternalInput").ap()
    ab_bias = nc.dram_tensor("ab_bias", [S, 2], F32, kind="ExternalInput").ap()
    wout_t = nc.dram_tensor("wout_t", [S, D], F32, kind="ExternalInput").ap()
    bout_row = nc.dram_tensor("bout_row", [1, D], F32, kind="ExternalInput").ap()
    wqg_t = nc.dram_tensor("wqg_t", [D, H], I32, kind="ExternalInput").ap()
    wqu_t = nc.dram_tensor("wqu_t", [D, H], I32, kind="ExternalInput").ap()
    wqd_t = nc.dram_tensor("wqd_t", [H, D], I32, kind="ExternalInput").ap()
    sg_t = nc.dram_tensor("sg_t", [128, HC], F32, kind="ExternalInput").ap()
    su_t = nc.dram_tensor("su_t", [128, HC], F32, kind="ExternalInput").ap()
    sd_row = nc.dram_tensor("sd_row", [1, D], F32, kind="ExternalInput").ap()
    ones_r = nc.dram_tensor("ones_r", [1, 128], F32, kind="ExternalInput").ap()

    y_out = nc.dram_tensor("y_out", [TOWN, D], F32, kind="ExternalOutput").ap()

    KI = "ExternalOutput" if dbg else "Internal"
    xn1b_h = nc.dram_tensor("xn1b_h", [TSEQ, D], BF16, kind=KI).ap()
    xn2b_h = nc.dram_tensor("xn2b_h", [TOWN, D], BF16, kind=KI).ap()
    x2_h = nc.dram_tensor("x2_h", [TOWN, D], F32, kind=KI).ap()
    prod_h = nc.dram_tensor("prod_h", [H, TSL], BF16, kind=KI).ap()
    dbg_st = nc.dram_tensor("dbg_st", [6 * 128, TQ], F32, kind=KI).ap() if dbg else None

    G = contextlib.ExitStack()
    g = G.enter_context
    with G:
        b_wabT = g(nc.sbuf_tensor("b_wabT", [128, DC * S2], BF16))
        b_woT = g(nc.sbuf_tensor("b_woT", [S, D], BF16))
        b_abbias = g(nc.sbuf_tensor("b_abbias", [S, 2], F32))
        b_bout_b = g(nc.sbuf_tensor("b_bout_b", [1, D], BF16))
        b_ones_b = g(nc.sbuf_tensor("b_ones_b", [1, 128], BF16))
        b_sel0 = g(nc.sbuf_tensor("b_sel0", [128, 1], F32))
        b_sel1 = g(nc.sbuf_tensor("b_sel1", [128, 1], F32))
        b_sg = g(nc.sbuf_tensor("b_sg", [128, HC], F32))
        b_su = g(nc.sbuf_tensor("b_su", [128, HC], F32))
        b_neg128 = g(nc.sbuf_tensor("b_neg128", [128, 1], F32))
        b_sdbc = g(nc.sbuf_tensor("b_sdbc", [128, D], BF16))
        b_onesF = g(nc.sbuf_tensor("b_onesF", [1, 128], F32))
        b_zero = g(nc.sbuf_tensor("b_zero", [S, 1], F32))
        b_sum1 = g(nc.sbuf_tensor("b_sum1", [128, TQ], F32))
        b_ssq1 = g(nc.sbuf_tensor("b_ssq1", [128, TQ], F32))
        b_m1 = g(nc.sbuf_tensor("b_m1", [128, TQ], F32))
        b_r1 = g(nc.sbuf_tensor("b_r1", [128, TQ], F32))
        b_mr1 = g(nc.sbuf_tensor("b_mr1", [128, TQ], F32))
        b_tq1 = g(nc.sbuf_tensor("b_tq1", [128, TQ], F32))
        b_tq2 = g(nc.sbuf_tensor("b_tq2", [128, TQ], F32))
        b_r1o = g(nc.sbuf_tensor("b_r1o", [128, TC], F32))
        b_mr1o = g(nc.sbuf_tensor("b_mr1o", [128, TC], F32))
        b_sum2 = g(nc.sbuf_tensor("b_sum2", [128, TC], F32))
        b_ssq2 = g(nc.sbuf_tensor("b_ssq2", [128, TC], F32))
        b_m2 = g(nc.sbuf_tensor("b_m2", [128, TC], F32))
        b_r2 = g(nc.sbuf_tensor("b_r2", [128, TC], F32))
        b_mr2 = g(nc.sbuf_tensor("b_mr2", [128, TC], F32))
        b_sum3 = g(nc.sbuf_tensor("b_sum3", [128, TC], F32))
        b_ssq3 = g(nc.sbuf_tensor("b_ssq3", [128, TC], F32))
        b_m3 = g(nc.sbuf_tensor("b_m3", [128, TC], F32))
        b_r3 = g(nc.sbuf_tensor("b_r3", [128, TC], F32))
        b_mr3 = g(nc.sbuf_tensor("b_mr3", [128, TC], F32))
        b_sum4 = g(nc.sbuf_tensor("b_sum4", [128, TC], F32))
        b_ssq4 = g(nc.sbuf_tensor("b_ssq4", [128, TC], F32))
        b_m4 = g(nc.sbuf_tensor("b_m4", [128, TC], F32))
        b_r4 = g(nc.sbuf_tensor("b_r4", [128, TC], F32))
        b_mr4 = g(nc.sbuf_tensor("b_mr4", [128, TC], F32))
        b_tc1 = g(nc.sbuf_tensor("b_tc1", [128, TC], F32))
        b_tc2 = g(nc.sbuf_tensor("b_tc2", [128, TC], F32))
        b_sqs = g(nc.sbuf_tensor("b_sqs", [128, D], BF16))
        b_one = g(nc.sbuf_tensor("b_one", [128, max(TQ, TC)], F32))
        b_nhalf = g(nc.sbuf_tensor("b_nhalf", [128, 1], F32))
        b_hsb = g(nc.sbuf_tensor("b_hsb", [S, TOWN], BF16))
        RG = 8; RU = 8; RD = 8
        b_wraw_g = [g(nc.sbuf_tensor(f"b_wraw_g{i}", [128, HB], BF16)) for i in range(RG)]
        b_wraw_u = [g(nc.sbuf_tensor(f"b_wraw_u{i}", [128, HB], BF16)) for i in range(RU)]
        b_wdraw = [g(nc.sbuf_tensor(f"b_wdraw{i}", [128, DSL], BF16)) for i in range(RD)]
        pall = g(nc.psum_tensor("pall", [128, 8 * 512], F32))

        def S1_(name):
            return G.enter_context(nc.semaphore(name))

        def SN_(name, n):
            return [G.enter_context(nc.semaphore(f"{name}{i}")) for i in range(n)]

        s_ppc = S1_("s_ppc"); s_red = S1_("s_red"); s_ssq = S1_("s_ssq")
        s_fa = S1_("s_fa"); s_fb = S1_("s_fb"); s_ap = S1_("s_ap")
        s_abp = S1_("s_abp"); s_th = S1_("s_th"); s_hs = S1_("s_hs")
        s_wps = S1_("s_wps"); s_y1 = S1_("s_y1")
        s_sq2 = S1_("s_sq2"); s_x2 = S1_("s_x2")
        s_sq3 = S1_("s_sq3"); s_xn2 = S1_("s_xn2")
        s_fa2 = S1_("s_fa2"); s_fb2 = S1_("s_fb2")
        s_gdq = S1_("s_gdq"); s_udq = S1_("s_udq"); s_dwq = S1_("s_dwq")
        s_gmm = S1_("s_gmm"); s_gev = S1_("s_gev"); s_pr = S1_("s_pr")
        s_dmm = S1_("s_dmm"); s_dev = S1_("s_dev")
        s_sq4 = S1_("s_sq4"); s_fa4 = S1_("s_fa4"); s_fb4 = S1_("s_fb4")
        s_yo = S1_("s_yo"); s_sdp = S1_("s_sdp")
        s_f1v = S1_("s_f1v"); s_r3v = S1_("s_r3v")
        s_dvf = S1_("s_dvf"); s_acf = S1_("s_acf")
        dvf_n = [0]
        acf_n = [0]

        def dfence(vec, instr):
            instr.then_inc(s_dvf, 1)
            dvf_n[0] += 1
            vec.wait_ge(s_dvf, dvf_n[0])

        def afence(scal, instr):
            instr.then_inc(s_acf, 1)
            acf_n[0] += 1
            scal.wait_ge(s_acf, acf_n[0])
        s_pp = S1_("s_pp")
        s_xld = SN_("s_xld", NXB)
        s_x1s = SN_("s_x1s", 2)
        s_xt = S1_("s_xt")
        s_xo = SN_("s_xo", NOB)
        s_x2s = SN_("s_x2s", 3)
        s_n2s = SN_("s_n2s", 2)
        s_rg = SN_("s_rg", RG)
        s_ru = SN_("s_ru", RU)
        s_rd = SN_("s_rd", RD)
        s_n2t = S1_("s_n2t")
        s_prs = SN_("s_prs", 2)
        s_psl = SN_("s_psl", NPS)
        s_x2r = SN_("s_x2r", 2)
        s_ys = SN_("s_ys", 2)

        block = g(nc.Block())

        def fin_A(vec, sums, ssqs, m_t, tmpa, tmpb, sl):
            dfence(vec, vec.tensor_scalar(out=m_t[:, sl], in0=sums[:, sl], scalar1=IVD,
                                          scalar2=0.0, op0=ALU.mult, op1=ALU.subtract))
            dfence(vec, vec.tensor_tensor(out=tmpa[:, sl], in0=m_t[:, sl], in1=m_t[:, sl],
                                          op=ALU.mult))
            dfence(vec, vec.tensor_scalar(out=tmpa[:, sl], in0=tmpa[:, sl], scalar1=1.0,
                                          scalar2=EPS, op0=ALU.mult, op1=ALU.subtract))
            return vec.scalar_tensor_tensor(out=tmpb[:, sl], in0=ssqs[:, sl], scalar=IVD,
                                            in1=tmpa[:, sl], op0=ALU.mult, op1=ALU.subtract)

        def fin_act(scalar_, tmpb, r_t, sl):
            afence(scalar_, scalar_.activation(out=tmpb[:, sl], in_=tmpb[:, sl], func=AF.Ln))
            return scalar_.activation(out=r_t[:, sl], in_=tmpb[:, sl], func=AF.Exp,
                                      scale=b_nhalf[:, :])

        def fin_B(vec, tmpb, m_t, r_t, mr_t, sl):
            return vec.tensor_tensor(out=mr_t[:, sl], in0=m_t[:, sl], in1=r_t[:, sl], op=ALU.mult)

        NGB = NSW * NHB
        cg_i = [0]; cu_i = [0]

        def emit_gdq(vec, lim):
            while cg_i[0] < min(lim, NGB * DC):
                ig = cg_i[0]
                gb, dc = divmod(ig, DC)
                vec.wait_ge(s_rg[ig % RG], 16 * (ig // RG + 1))
                if dc == 0 and gb >= 2:
                    vec.wait_ge(s_gmm, 2 * HCB * (gb - 1) - 1)
                vec.tensor_scalar(out=b_wg[(gb % 2) * DC + dc][:, :],
                                  in0=b_wraw_g[ig % RG][:, :],
                                  scalar1=128.0, scalar2=1.0,
                                  op0=ALU.subtract, op1=ALU.mult).then_inc(s_gdq, 1)
                cg_i[0] += 1

        def emit_udq(scalar_, lim):
            while cu_i[0] < min(lim, NGB * DC):
                iu = cu_i[0]
                gb, dc = divmod(iu, DC)
                scalar_.wait_ge(s_ru[iu % RU], 16 * (iu // RU + 1))
                if iu == 0:
                    scalar_.wait_ge(s_ppc, 9)
                if dc == 0 and gb >= 1:
                    scalar_.wait_ge(s_gmm, 2 * HCB * gb)
                scalar_.activation(out=b_wu[dc][:, :], in_=b_wraw_u[iu % RU][:, :],
                                   func=AF.Identity,
                                   bias=b_neg128[:, :]).then_inc(s_udq, 1)
                cu_i[0] += 1

        A = contextlib.ExitStack()
        a = A.enter_context
        with A:
            P0 = contextlib.ExitStack()
            p0 = P0.enter_context
            b_wabF = p0(nc.sbuf_tensor("b_wabF", [128, DC * S2], F32))
            b_woF = p0(nc.sbuf_tensor("b_woF", [S, D], F32))
            b_boutF = p0(nc.sbuf_tensor("b_boutF", [1, D], F32))
            b_sdrF = p0(nc.sbuf_tensor("b_sdrF", [1, D], F32))
            P0.close()
            b_x = [a(nc.sbuf_tensor(f"b_x{i}", [128, D], F32)) for i in range(NXB)]
            b_xn1 = [a(nc.sbuf_tensor(f"b_xn1{i}", [128, D], BF16)) for i in range(2)]
            b_xt = [a(nc.sbuf_tensor(f"b_xt{i}", [128, QTB], BF16)) for i in range(2 * DC)]
            b_ab = a(nc.sbuf_tensor("b_ab", [S, 2 * TSEQ], F32))
            b_hs = a(nc.sbuf_tensor("b_hs", [S, TSEQ], F32))

            @block.sync
            def _(sync):
                for dst, srcap in [(b_wabF[:, :].rearrange("p (c s) -> p c s", s=S2),
                                    wab_t.rearrange("(c p) s -> p c s", p=128)),
                                   (b_abbias[:, :], ab_bias),
                                   (b_woF[:, :], wout_t),
                                   (b_boutF[:, :], bout_row), (b_onesF[:, :], ones_r),
                                   (b_sel0[:, :], sel0), (b_sel1[:, :], sel1),
                                   (b_sg[:, :], sg_t), (b_su[:, :], su_t),
                                   (b_sdrF[:, :], sd_row)]:
                    sync.dma_start(out=dst, in_=srcap).then_inc(s_pp, 16)
                LAG = FSTB + 1
                sync.wait_ge(s_ppc, 9)
                sync.wait_ge(s_sdp, 1)

                def xt_round(q):
                    K = RPQ * (q + 1)
                    sync.wait_ge(s_x1s[0], 16 * _cnt(K, 0, 2))
                    sync.wait_ge(s_x1s[1], 16 * _cnt(K, 1, 2))
                    if q >= 2:
                        sync.wait_ge(s_abp, NPJQ * (q - 1))
                    for dc in range(DC):
                        sync.dma_start(out=b_xt[(q % 2) * DC + dc][:, :],
                                       in_=xn1b_h[q * QTB:(q + 1) * QTB, dc * 128:(dc + 1) * 128],
                                       transpose=True).then_inc(s_xt, 16)

                nxt_q = [0]
                for tc in range(TQ):
                    if tc >= NXB:
                        sync.wait_ge(s_ap, tc - NXB + 1)
                        sync.wait_ge(s_red, tc - NXB + 1)
                        sync.wait_ge(s_ssq, tc - NXB + 1)
                    sync.dma_start(out=b_x[tc % NXB][:, :],
                                   in_=x_seq[tc * 128:(tc + 1) * 128, :]).then_inc(s_xld[tc % NXB], 16)
                    if tc >= LAG:
                        st_c = tc - LAG
                        sync.wait_ge(s_ap, st_c + 1)
                        sync.dma_start(out=xn1b_h[st_c * 128:(st_c + 1) * 128, :],
                                       in_=b_xn1[st_c % 2][:, :]).then_inc(s_x1s[st_c % 2], 16)
                        if st_c + 1 >= RPQ * (nxt_q[0] + 1) and nxt_q[0] < NQ:
                            xt_round(nxt_q[0])
                            nxt_q[0] += 1
                for st_c in range(max(0, TQ - LAG), TQ):
                    sync.wait_ge(s_ap, st_c + 1)
                    sync.dma_start(out=xn1b_h[st_c * 128:(st_c + 1) * 128, :],
                                   in_=b_xn1[st_c % 2][:, :]).then_inc(s_x1s[st_c % 2], 16)
                    if st_c + 1 >= RPQ * (nxt_q[0] + 1) and nxt_q[0] < NQ:
                        xt_round(nxt_q[0])
                        nxt_q[0] += 1
                while nxt_q[0] < NQ:
                    xt_round(nxt_q[0])
                    nxt_q[0] += 1

            @block.gpsimd
            def _(gpsimd):
                lg = [0]; lu = [0]; lw = [0]

                def g_tile():
                    ig = lg[0]
                    gb, dc = divmod(ig, DC)
                    sw, hb = divmod(gb, NHB)
                    if ig >= RG:
                        gpsimd.wait_ge(s_gdq, ig - RG + 1)
                    gpsimd.dma_start(out=b_wraw_g[ig % RG][:, :],
                                     in_=wqg_t[dc * 128:(dc + 1) * 128,
                                               hb * HB:(hb + 1) * HB]).then_inc(s_rg[ig % RG], 16)
                    lg[0] += 1

                def u_tile():
                    iu = lu[0]
                    gb, dc = divmod(iu, DC)
                    sw, hb = divmod(gb, NHB)
                    if iu >= RU:
                        gpsimd.wait_ge(s_udq, iu - RU + 1)
                    gpsimd.dma_start(out=b_wraw_u[iu % RU][:, :],
                                     in_=wqu_t[dc * 128:(dc + 1) * 128,
                                               hb * HB:(hb + 1) * HB]).then_inc(s_ru[iu % RU], 16)
                    lu[0] += 1

                def w_tile():
                    k = lw[0]
                    sw, r = divmod(k, NDS * HC)
                    ds, hc = divmod(r, HC)
                    if k >= RD:
                        gpsimd.wait_ge(s_dwq, k - RD + 1)
                    gpsimd.dma_start(out=b_wdraw[k % RD][:, :],
                                     in_=wqd_t[hc * 128:(hc + 1) * 128,
                                               ds * DSL:(ds + 1) * DSL]).then_inc(s_rd[k % RD], 16)
                    lw[0] += 1

                NWDK = NSW * NDS * HC

                def g_blk():
                    if lg[0] < NGB * DC:
                        for _ in range(DC):
                            g_tile()

                def u_blk():
                    if lu[0] < NGB * DC:
                        for _ in range(DC):
                            u_tile()

                g_blk(); g_blk(); u_blk()
                for gb in range(NHB):
                    u_blk()
                    g_blk()
                while lw[0] < NDS * HC:
                    w_tile()
                for gb in range(NHB, 2 * NHB):
                    u_blk()
                    g_blk()
                while lw[0] < NWDK:
                    w_tile()


            @block.vector
            def _(vector):
                vector.wait_ge(s_pp, 16 * 10)
                vector.tensor_copy(out=b_wabT[:, :], in_=b_wabF[:, :]).then_inc(s_ppc, 1)
                vector.tensor_copy(out=b_woT[:, :], in_=b_woF[:, :]).then_inc(s_ppc, 1)
                vector.tensor_copy(out=b_bout_b[:, :], in_=b_boutF[:, :]).then_inc(s_ppc, 1)
                vector.tensor_copy(out=b_ones_b[:, :], in_=b_onesF[:, :]).then_inc(s_ppc, 1)
                vector.memset(b_zero[:, :], 0.0).then_inc(s_ppc, 1)
                vector.memset(b_one[:, :], 1.0).then_inc(s_ppc, 1)
                vector.memset(b_nhalf[:, :], -0.5).then_inc(s_ppc, 1)
                vector.memset(b_neg128[:, :], -128.0).then_inc(s_ppc, 1)
                vector.wait_ge(s_sdp, 1)
                vector.tensor_copy(out=b_sdbc[:, :], in_=pall[:, 0:D]).then_inc(s_ppc, 1)

                for bi in range(TQ // FSTB):
                    lo = bi * FSTB; hi = lo + FSTB; sl = slice(lo, hi)
                    for tc in range(lo, hi):
                        vector.wait_ge(s_xld[tc % NXB], 16 * (tc // NXB + 1))
                        vector.tensor_reduce(out=b_sum1[:, tc:tc + 1], in_=b_x[tc % NXB][:, :],
                                             axis=AX.X, op=ALU.add).then_inc(s_red, 1)
                        vector.scalar_tensor_tensor(out=b_sqs[:, :], in0=b_x[tc % NXB][:, :],
                                                    scalar=1.0, in1=b_x[tc % NXB][:, :],
                                                    op0=ALU.mult, op1=ALU.mult,
                                                    accum_out=b_ssq1[:, tc:tc + 1]).then_inc(s_ssq, 1)
                    vector.wait_ge(s_red, hi)
                    vector.wait_ge(s_ssq, hi)
                    fin_A(vector, b_sum1, b_ssq1, b_m1, b_tq1, b_tq2, sl).then_inc(s_fa, 1)
                    vector.wait_ge(s_fb, bi + 1)
                    fin_B(vector, b_tq2, b_m1, b_r1, b_mr1, sl).then_inc(s_f1v, 1)
                    vector.wait_ge(s_f1v, bi + 1)
                    for tc in range(lo, hi):
                        if tc >= 2:
                            vector.wait_ge(s_x1s[tc % 2], 16 * ((tc - 2) // 2 + 1))
                        vector.tensor_scalar(out=b_xn1[tc % 2][:, :], in0=b_x[tc % NXB][:, :],
                                             scalar1=b_r1[:, tc:tc + 1], scalar2=b_mr1[:, tc:tc + 1],
                                             op0=ALU.mult, op1=ALU.subtract).then_inc(s_ap, 1)

                vector.wait_ge(s_th, 2 * NPJ)
                vector.wait_ge(s_ssq, TQ)
                vector.tensor_tensor_scan(out=b_hs[:, :], data0=b_ab[:, 0:TSEQ],
                                          data1=b_ab[:, TSEQ:2 * TSEQ],
                                          initial=b_zero[:, :], op0=ALU.mult, op1=ALU.add).then_inc(s_hs, 1)
                vector.wait_ge(s_hs, 1)
                vector.tensor_scalar(out=b_ab[0:S, 0:TOWN], in0=b_hs[:, 0:TOWN],
                                     scalar1=b_sel0[0:S, :], scalar2=0.0,
                                     op0=ALU.mult, op1=ALU.subtract).then_inc(s_f1v, 1)
                vector.wait_ge(s_f1v, TQ // FSTB + 1)
                vector.scalar_tensor_tensor(out=b_hsb[:, :], in0=b_hs[:, TSEQ - TOWN:TSEQ],
                                            scalar=b_sel1[0:S, :], in1=b_ab[0:S, 0:TOWN],
                                            op0=ALU.mult, op1=ALU.add).then_inc(s_hs, 1)
                vector.tensor_scalar(out=b_tc1[:, :], in0=b_r1[:, 0:TC], scalar1=b_sel0[:, :],
                                     scalar2=0.0, op0=ALU.mult, op1=ALU.subtract).then_inc(s_f1v, 1)
                vector.wait_ge(s_f1v, TQ // FSTB + 2)
                vector.scalar_tensor_tensor(out=b_r1o[:, :], in0=b_r1[:, TQ - TC:TQ],
                                            scalar=b_sel1[:, :], in1=b_tc1[:, :],
                                            op0=ALU.mult, op1=ALU.add).then_inc(s_f1v, 1)
                vector.wait_ge(s_f1v, TQ // FSTB + 3)
                vector.tensor_scalar(out=b_tc1[:, :], in0=b_mr1[:, 0:TC], scalar1=b_sel0[:, :],
                                     scalar2=0.0, op0=ALU.mult, op1=ALU.subtract).then_inc(s_f1v, 1)
                vector.wait_ge(s_f1v, TQ // FSTB + 4)
                vector.scalar_tensor_tensor(out=b_mr1o[:, :], in0=b_mr1[:, TQ - TC:TQ],
                                            scalar=b_sel1[:, :], in1=b_tc1[:, :],
                                            op0=ALU.mult, op1=ALU.add).then_inc(s_hs, 1)

            @block.scalar
            def _(scalar):
                def tanh_blk(k):
                    scalar.wait_ge(s_abp, k + 1)
                    scalar.activation(out=b_ab[:, k * PJB:(k + 1) * PJB],
                                      in_=pall[0:S, (k % 2) * 512:(k % 2) * 512 + PJB],
                                      func=AF.Tanh, bias=b_abbias[:, 0:1]).then_inc(s_th, 1)
                    scalar.activation(out=b_ab[:, TSEQ + k * PJB:TSEQ + (k + 1) * PJB],
                                      in_=pall[0:S, 1024 + (k % 2) * 512:1024 + (k % 2) * 512 + PJB],
                                      func=AF.Tanh, bias=b_abbias[:, 1:2]).then_inc(s_th, 1)

                nxt_k = [0]
                for bi in range(TQ // FSTB):
                    scalar.wait_ge(s_fa, bi + 1)
                    fin_act(scalar, b_tq2, b_r1, slice(bi * FSTB, (bi + 1) * FSTB)).then_inc(s_fb, 1)
                    while (nxt_k[0] < NPJ
                           and RPQ * (nxt_k[0] // NPJQ + 1) <= FSTB * bi):
                        tanh_blk(nxt_k[0])
                        nxt_k[0] += 1
                while nxt_k[0] < NPJ:
                    tanh_blk(nxt_k[0])
                    nxt_k[0] += 1

            @block.tensor
            def _(tensor):
                tensor.wait_ge(s_pp, 16 * 10)
                for j in range(D // JW):
                    mm = tensor.matmul(pall[:, j * JW:(j + 1) * JW], b_onesF[:, :],
                                       b_sdrF[:, j * JW:(j + 1) * JW], start=True, stop=True)
                mm.then_inc(s_sdp, 1)
                tensor.wait_ge(s_ppc, 9)
                for q in range(NQ):
                    tensor.wait_ge(s_xt, 16 * DC * (q + 1))
                    for pb in range(NPJQ):
                        k = q * NPJQ + pb
                        if k >= 2:
                            tensor.wait_ge(s_th, 2 * (k - 1))
                        po_a = (k % 2) * 512
                        po_b = 1024 + (k % 2) * 512
                        xbuf = (q % 2) * DC
                        for dc in range(DC):
                            tensor.matmul(pall[0:S, po_a:po_a + PJB],
                                          b_wabT[:, dc * S2:dc * S2 + S],
                                          b_xt[xbuf + dc][:, pb * PJB:(pb + 1) * PJB],
                                          start=(dc == 0), stop=(dc == DC - 1))
                            mm = tensor.matmul(pall[0:S, po_b:po_b + PJB],
                                               b_wabT[:, dc * S2 + S:dc * S2 + S2],
                                               b_xt[xbuf + dc][:, pb * PJB:(pb + 1) * PJB],
                                               start=(dc == 0), stop=(dc == DC - 1))
                        mm.then_inc(s_abp, 1)

        B1 = contextlib.ExitStack()
        b1 = B1.enter_context
        with B1:
            b_xo = [b1(nc.sbuf_tensor(f"b_xo{i}", [128, D], F32)) for i in range(NOB)]
            b_txn = [b1(nc.sbuf_tensor(f"b_txn{i}", [128, D], F32)) for i in range(1)]
            b_y1 = [b1(nc.sbuf_tensor(f"b_y1{i}", [128, D], F32)) for i in range(3)]
            b_t2 = [b1(nc.sbuf_tensor(f"b_t2{i}", [128, D], F32)) for i in range(1)]
            b_x2 = [b1(nc.sbuf_tensor(f"b_x2{i}", [128, D], F32)) for i in range(3)]
            b_xn2 = [b1(nc.sbuf_tensor(f"b_xn2{i}", [128, D], BF16)) for i in range(2)]

            @block.sync
            def _(sync):
                sync.wait_ge(s_hs, 3)
                sync.wait_ge(s_ap, TQ)
                sync.wait_ge(s_ssq, TQ)
                sync.wait_ge(s_x1s[0], 16 * _cnt(TQ, 0, 2))
                sync.wait_ge(s_x1s[1], 16 * _cnt(TQ, 1, 2))
                sync.wait_ge(s_abp, NPJ)
                for oc in range(TC):
                    if oc >= NOB:
                        sync.wait_ge(s_x2, oc - NOB + 1)
                    sync.dma_start(out=b_xo[oc % NOB][:, :],
                                   in_=x_own[oc * 128:(oc + 1) * 128, :]).then_inc(s_xo[oc % NOB], 16)
                    if oc >= 2:
                        oc2 = oc - 2
                        sync.wait_ge(s_x2, oc2 + 1)
                        sync.dma_start(out=x2_h[oc2 * 128:(oc2 + 1) * 128, :],
                                       in_=b_x2[oc2 % 3][:, :]).then_inc(s_x2s[oc2 % 3], 16)
                        sync.wait_ge(s_xn2, oc2 + 1)
                        sync.dma_start(out=xn2b_h[oc2 * 128:(oc2 + 1) * 128, :],
                                       in_=b_xn2[oc2 % 2][:, :]).then_inc(s_n2s[oc2 % 2], 16)
                for oc2 in range(TC - 2, TC):
                    sync.wait_ge(s_x2, oc2 + 1)
                    sync.dma_start(out=x2_h[oc2 * 128:(oc2 + 1) * 128, :],
                                   in_=b_x2[oc2 % 3][:, :]).then_inc(s_x2s[oc2 % 3], 16)
                    sync.wait_ge(s_xn2, oc2 + 1)
                    sync.dma_start(out=xn2b_h[oc2 * 128:(oc2 + 1) * 128, :],
                                   in_=b_xn2[oc2 % 2][:, :]).then_inc(s_n2s[oc2 % 2], 16)

            @block.vector
            def _(vector):
                for bi in range(TC // OSTB):
                    lo = bi * OSTB; hi = lo + OSTB; sl = slice(lo, hi)
                    for oc in range(lo, hi):
                        vector.wait_ge(s_xo[oc % NOB], 16 * (oc // NOB + 1))
                        vector.wait_ge(s_wps, oc + 1)
                        if oc >= 3:
                            vector.wait_ge(s_sq2, oc - 2)
                        vector.tensor_scalar(out=b_txn[0][:, :], in0=b_xo[oc % NOB][:, :],
                                             scalar1=b_r1o[:, oc:oc + 1], scalar2=b_mr1o[:, oc:oc + 1],
                                             op0=ALU.mult, op1=ALU.subtract)
                        vector.scalar_tensor_tensor(out=b_y1[oc % 3][:, :], in0=b_txn[0][:, :],
                                                    scalar=1.0, in1=pall[:, 0:D],
                                                    op0=ALU.mult, op1=ALU.add,
                                                    accum_out=b_sum2[:, oc:oc + 1]).then_inc(s_y1, 1)
                    vector.wait_ge(s_sq2, hi)
                    vector.wait_ge(s_y1, hi)
                    fin_A(vector, b_sum2, b_ssq2, b_m2, b_tc1, b_tc2, sl).then_inc(s_fa2, 1)
                    vector.wait_ge(s_fb2, 2 * bi + 1)
                    dfence(vector, fin_B(vector, b_tc2, b_m2, b_r2, b_mr2, sl))
                    for oc in range(lo, hi):
                        if oc >= 3:
                            vector.wait_ge(s_x2s[(oc - 3) % 3], 16 * ((oc - 3) // 3 + 1))
                            vector.wait_ge(s_sq3, oc - 2)
                        vector.tensor_scalar(out=b_t2[0][:, :], in0=b_y1[oc % 3][:, :],
                                             scalar1=b_r2[:, oc:oc + 1], scalar2=b_mr2[:, oc:oc + 1],
                                             op0=ALU.mult, op1=ALU.subtract)
                        vector.scalar_tensor_tensor(out=b_x2[oc % 3][:, :], in0=b_t2[0][:, :],
                                                    scalar=1.0, in1=b_xo[oc % NOB][:, :],
                                                    op0=ALU.mult, op1=ALU.add,
                                                    accum_out=b_sum3[:, oc:oc + 1]).then_inc(s_x2, 1)
                    vector.wait_ge(s_sq3, hi)
                    vector.wait_ge(s_x2, hi)
                    fin_A(vector, b_sum3, b_ssq3, b_m3, b_tc1, b_tc2, sl).then_inc(s_fa2, 1)
                    vector.wait_ge(s_fb2, 2 * bi + 2)
                    dfence(vector, fin_B(vector, b_tc2, b_m3, b_r3, b_mr3, sl))
                    for oc in range(lo, hi):
                        if oc >= 2:
                            vector.wait_ge(s_n2s[oc % 2], 16 * ((oc - 2) // 2 + 1))
                        vector.tensor_scalar(out=b_xn2[oc % 2][:, :], in0=b_x2[oc % 3][:, :],
                                             scalar1=b_r3[:, oc:oc + 1], scalar2=b_mr3[:, oc:oc + 1],
                                             op0=ALU.mult, op1=ALU.subtract).then_inc(s_xn2, 1)

            @block.scalar
            def _(scalar):
                for bi in range(TC // OSTB):
                    lo = bi * OSTB; hi = lo + OSTB
                    for oc in range(lo, hi):
                        scalar.wait_ge(s_y1, oc + 1)
                        scalar.activation(out=b_sqs[:, :], in_=b_y1[oc % 3][:, :], func=AF.Square,
                                          accum_out=b_ssq2[:, oc:oc + 1]).then_inc(s_sq2, 1)
                    scalar.wait_ge(s_fa2, 2 * bi + 1)
                    fin_act(scalar, b_tc2, b_r2, slice(lo, hi)).then_inc(s_fb2, 1)
                    for oc in range(lo, hi):
                        scalar.wait_ge(s_x2, oc + 1)
                        scalar.activation(out=b_sqs[:, :], in_=b_x2[oc % 3][:, :], func=AF.Square,
                                          accum_out=b_ssq3[:, oc:oc + 1]).then_inc(s_sq3, 1)
                    scalar.wait_ge(s_fa2, 2 * bi + 2)
                    fin_act(scalar, b_tc2, b_r3, slice(lo, hi)).then_inc(s_fb2, 1)

            @block.tensor
            def _(tensor):
                for oc in range(TC):
                    tensor.wait_ge(s_hs, 2)
                    if oc >= 1:
                        tensor.wait_ge(s_y1, oc)
                    for j in range(NJW):
                        tensor.matmul(pall[:, j * JW:(j + 1) * JW],
                                      b_hsb[:, oc * 128:(oc + 1) * 128],
                                      b_woT[:, j * JW:(j + 1) * JW],
                                      start=True, stop=False)
                        mm = tensor.matmul(pall[:, j * JW:(j + 1) * JW],
                                           b_ones_b[:, :], b_bout_b[:, j * JW:(j + 1) * JW],
                                           start=False, stop=True)
                    mm.then_inc(s_wps, 1)

        B2 = contextlib.ExitStack()
        b2 = B2.enter_context
        with B2:
            b_xn2T = [b2(nc.sbuf_tensor(f"b_xn2T{i}", [128, TSL], BF16)) for i in range(DC)]
            b_wg = [b2(nc.sbuf_tensor(f"b_wg{i}", [128, HB], BF16)) for i in range(2 * DC)]
            b_wu = [b2(nc.sbuf_tensor(f"b_wu{i}", [128, HB], BF16)) for i in range(DC)]
            b_gt_ = [b2(nc.sbuf_tensor(f"b_gt{i}", [128, TSL], BF16)) for i in range(2)]
            b_ut = [b2(nc.sbuf_tensor(f"b_ut{i}", [128, TSL], BF16)) for i in range(2)]
            b_pw = [b2(nc.sbuf_tensor(f"b_pw{i}", [128, TSL], BF16)) for i in range(2)]
            b_pst = [b2(nc.sbuf_tensor(f"b_pst{i}", [128, TSL], BF16)) for i in range(NPS)]
            b_wd = [b2(nc.sbuf_tensor(f"b_wd{i}", [128, DSL], BF16)) for i in range(NWD)]
            b_f = [b2(nc.sbuf_tensor(f"b_f{i}", [128, D], BF16)) for i in range(TCS)]
            b_x2r = [b2(nc.sbuf_tensor(f"b_x2r{i}", [128, D], F32)) for i in range(2)]

            @block.sync
            def _(sync):
                for j in range(3):
                    sync.wait_ge(s_x2s[j], 16 * _cnt(TC, j, 3))
                for j in range(2):
                    sync.wait_ge(s_n2s[j], 16 * _cnt(TC, j, 2))
                sync.wait_ge(s_sq2, TC)
                sync.wait_ge(s_sq3, TC)
                sync.wait_ge(s_xn2, TC)
                def emit_n2t(sw):
                    if sw > 0:
                        sync.wait_ge(s_gmm, 2 * HC * sw)
                    for dc in range(DC):
                        sync.dma_start(out=b_xn2T[dc][:, :],
                                       in_=xn2b_h[sw * TSL:(sw + 1) * TSL, dc * 128:(dc + 1) * 128],
                                       transpose=True).then_inc(s_n2t, 16)

                emit_n2t(0)
                for sw in range(NSW):
                    def emit_prod_store(hc):
                        ghc = sw * HC + hc
                        sync.wait_ge(s_pr, ghc + 1)
                        sync.dma_start(out=prod_h[hc * 128:(hc + 1) * 128, :],
                                       in_=b_pw[ghc % 2][:, :]).then_inc(s_prs[ghc % 2], 16)

                    for hc in range(HC):
                        emit_prod_store(hc)
                    for ds in range(NDS):
                        if ds == min(1, NDS - 1) and sw + 1 < NSW:
                            emit_n2t(sw + 1)
                        for hc in range(HC):
                            k = sw * NDS * HC + ds * HC + hc
                            ghc = sw * HC + hc
                            sync.wait_ge(s_prs[ghc % 2], 16 * (ghc // 2 + 1))
                            if k >= NPS:
                                sync.wait_ge(s_dmm, k - NPS + 1)
                            sync.dma_start(out=b_pst[k % NPS][:, :],
                                           in_=prod_h[hc * 128:(hc + 1) * 128, :]).then_inc(s_psl[k % NPS], 16)
                    for tc in range(TCS):
                        gtc = sw * TCS + tc
                        if gtc >= 2:
                            sync.wait_ge(s_ys[gtc % 2], 16 * ((gtc - 2) // 2 + 1))
                        sync.dma_start(out=b_x2r[gtc % 2][:, :],
                                       in_=x2_h[gtc * 128:(gtc + 1) * 128, :]).then_inc(s_x2r[gtc % 2], 16)
                        sync.wait_ge(s_yo, gtc + 1)
                        sync.dma_start(out=y_out[gtc * 128:(gtc + 1) * 128, :],
                                       in_=b_x2r[gtc % 2][:, :]).then_inc(s_ys[gtc % 2], 16)
                sync.wait_ge(s_ys[0], 16 * _cnt(TC, 0, 2))
                sync.wait_ge(s_ys[1], 16 * _cnt(TC, 1, 2))
                if dbg_st is not None:
                    for i, t in enumerate([b_sum1, b_ssq1, b_m1, b_r1, b_mr1, b_tq2]):
                        sync.dma_start(out=dbg_st[i * 128:(i + 1) * 128, :],
                                       in_=t[:, :]).then_inc(s_pp, 16)
                    sync.wait_ge(s_pp, 16 * 16)

            @block.tensor
            def _(tensor):
                for sw in range(NSW):
                    tensor.wait_ge(s_n2t, 16 * DC * (sw + 1))
                    for hc in range(HC):
                        hb, hcc = divmod(hc, HCB)
                        tensor.wait_ge(s_gdq, DC * (sw * NHB + hb + 1))
                        if hc >= 2:
                            tensor.wait_ge(s_gev, 2 * HC * sw + 2 * (hc - 2) + 1)
                        elif sw > 0:
                            tensor.wait_ge(s_dev, sw * NDS * TCS)
                        po = (4 + 2 * (hc % 2)) * 512
                        gbuf = ((sw * NHB + hb) % 2) * DC
                        for dc in range(DC):
                            for j in range(NJ5):
                                mm = tensor.matmul(pall[:, po + j * T5:po + (j + 1) * T5],
                                                   b_wg[gbuf + dc][:, hcc * 128:(hcc + 1) * 128],
                                                   b_xn2T[dc][:, j * T5:(j + 1) * T5],
                                                   start=(dc == 0), stop=(dc == DC - 1))
                        mm.then_inc(s_gmm, 1)
                        tensor.wait_ge(s_udq, DC * (sw * NHB + hb + 1))
                        if hc >= 2:
                            tensor.wait_ge(s_gev, 2 * HC * sw + 2 * (hc - 2) + 2)
                        elif sw == 0:
                            tensor.wait_ge(s_y1, TC)
                        else:
                            tensor.wait_ge(s_dev, sw * NDS * TCS)
                        po = (2 * (hc % 2)) * 512
                        for dc in range(DC):
                            for j in range(NJ5):
                                mm = tensor.matmul(pall[:, po + j * T5:po + (j + 1) * T5],
                                                   b_wu[dc][:, hcc * 128:(hcc + 1) * 128],
                                                   b_xn2T[dc][:, j * T5:(j + 1) * T5],
                                                   start=(dc == 0), stop=(dc == DC - 1))
                        mm.then_inc(s_gmm, 1)
                    for ds in range(NDS):
                        if ds == 0:
                            tensor.wait_ge(s_gev, 2 * HC * (sw + 1))
                        else:
                            tensor.wait_ge(s_dev, (sw * NDS + ds) * TCS)
                        for hc in range(HC):
                            k = sw * NDS * HC + ds * HC + hc
                            tensor.wait_ge(s_psl[k % NPS], 16 * (k // NPS + 1))
                            tensor.wait_ge(s_dwq, k + 1)
                            for tc in range(TCS):
                                mm = tensor.matmul(pall[:, tc * 512:tc * 512 + DSL],
                                                   b_pst[k % NPS][:, tc * 128:(tc + 1) * 128],
                                                   b_wd[k % NWD][:, :],
                                                   start=(hc == 0), stop=(hc == HC - 1))
                            mm.then_inc(s_dmm, 1)

            @block.scalar
            def _(scalar):
                emit_udq(scalar, DC)
                for sw in range(NSW):
                    for hc in range(HC):
                        ghc = sw * HC + hc
                        gb = sw * NHB + hc // HCB
                        hcc = hc % HCB
                        scalar.wait_ge(s_gmm, 2 * HC * sw + 2 * hc + 1)
                        if ghc >= 2:
                            scalar.wait_ge(s_pr, ghc - 1)
                        po = (4 + 2 * (hc % 2)) * 512
                        scalar.activation(out=b_gt_[hc % 2][:, :], in_=pall[:, po:po + TSL],
                                          func=AF.Silu,
                                          scale=b_sg[:, hc:hc + 1]).then_inc(s_gev, 1)
                        scalar.wait_ge(s_gmm, 2 * HC * sw + 2 * hc + 2)
                        po = (2 * (hc % 2)) * 512
                        scalar.activation(out=b_ut[hc % 2][:, :], in_=pall[:, po:po + TSL],
                                          func=AF.Identity,
                                          scale=b_su[:, hc:hc + 1]).then_inc(s_gev, 1)
                        if hcc == HCB - 1:
                            emit_udq(scalar, DC * (gb + 2))
                    for ds in range(NDS):
                        for tc in range(TCS):
                            scalar.wait_ge(s_dmm, sw * NDS * HC + ds * HC + HC)
                            if sw > 0 and ds == 0:
                                scalar.wait_ge(s_yo, sw * TCS)
                            scalar.activation(out=b_f[tc][:, ds * DSL:(ds + 1) * DSL],
                                              in_=pall[:, tc * 512:tc * 512 + DSL],
                                              func=AF.Copy).then_inc(s_dev, 1)
                    for tc in range(TCS):
                        gtc = sw * TCS + tc
                        scalar.activation(out=b_sqs[:, :], in_=b_f[tc][:, :], func=AF.Copy,
                                          accum_out=b_sum4[:, gtc:gtc + 1]).then_inc(s_sq4, 1)
                        scalar.activation(out=b_sqs[:, :], in_=b_f[tc][:, :], func=AF.Square,
                                          accum_out=b_ssq4[:, gtc:gtc + 1]).then_inc(s_sq4, 1)
                    scalar.wait_ge(s_fa4, sw + 1)
                    fin_act(scalar, b_tc2, b_r4, slice(sw * TCS, (sw + 1) * TCS)).then_inc(s_fb4, 1)

            @block.vector
            def _(vector):
                cw_i = [0]

                def emit_wdq(lim):
                    while cw_i[0] < min(lim, NSW * NDS * HC):
                        k = cw_i[0]
                        swk, r = divmod(k, NDS * HC)
                        ds = r // HC
                        vector.wait_ge(s_rd[k % RD], 16 * (k // RD + 1))
                        if k >= NWD:
                            vector.wait_ge(s_dmm, k - NWD + 1)
                        vector.scalar_tensor_tensor(out=b_wd[k % NWD][:, :],
                                                    in0=b_wdraw[k % RD][:, :],
                                                    scalar=128.0,
                                                    in1=b_sdbc[:, ds * DSL:(ds + 1) * DSL],
                                                    op0=ALU.subtract,
                                                    op1=ALU.mult).then_inc(s_dwq, 1)
                        cw_i[0] += 1

                emit_gdq(vector, 2 * DC)
                for sw in range(NSW):
                    for hc in range(HC):
                        ghc = sw * HC + hc
                        gb = sw * NHB + hc // HCB
                        hcc = hc % HCB
                        vector.wait_ge(s_gev, 2 * HC * sw + 2 * (hc + 1))
                        if ghc >= 2:
                            vector.wait_ge(s_prs[ghc % 2], 16 * ((ghc - 2) // 2 + 1))
                        vector.tensor_tensor(out=b_pw[ghc % 2][:, :], in0=b_gt_[hc % 2][:, :],
                                             in1=b_ut[hc % 2][:, :], op=ALU.mult).then_inc(s_pr, 1)
                        if hcc == HCB - 1:
                            emit_gdq(vector, DC * (gb + 3))
                    emit_wdq((sw + 1) * NDS * HC)
                    sl = slice(sw * TCS, (sw + 1) * TCS)
                    vector.wait_ge(s_sq4, 2 * (sw + 1) * TCS)
                    fin_A(vector, b_sum4, b_ssq4, b_m4, b_tc1, b_tc2, sl).then_inc(s_fa4, 1)
                    vector.wait_ge(s_fb4, sw + 1)
                    dfence(vector, fin_B(vector, b_tc2, b_m4, b_r4, b_mr4, sl))
                    for tc in range(TCS):
                        gtc = sw * TCS + tc
                        vector.wait_ge(s_x2r[gtc % 2], 16 * (gtc // 2 + 1))
                        vector.tensor_scalar(out=b_x2r[gtc % 2][:, :], in0=b_x2r[gtc % 2][:, :],
                                             scalar1=1.0, scalar2=b_mr4[:, gtc:gtc + 1],
                                             op0=ALU.mult, op1=ALU.subtract)
                        vector.scalar_tensor_tensor(out=b_x2r[gtc % 2][:, :], in0=b_f[tc][:, :],
                                                    scalar=b_r4[:, gtc:gtc + 1],
                                                    in1=b_x2r[gtc % 2][:, :],
                                                    op0=ALU.mult, op1=ALU.add).then_inc(s_yo, 1)

    return nc


def kernel(L0_attn_x, L0_attn_Wa, L0_attn_ba, L0_attn_Wb, L0_attn_bb,
           L0_attn_Wout, L0_attn_bout, L0_ffn_wq_gate, L0_ffn_s_gate,
           L0_ffn_wq_up, L0_ffn_s_up, L0_ffn_wq_down, L0_ffn_s_down,
           _cfg=None, _n_cores=8):
    cfg = dict(FULL_CFG if _cfg is None else _cfg)
    D = cfg["D"]; TSEQ = cfg["TSEQ"]; TOWN = cfg["TOWN"]
    HC = cfg["H"] // 128; DC = D // 128
    x = np.ascontiguousarray(np.asarray(L0_attn_x, dtype=np.float32)).reshape(-1, TSEQ, D)
    B = x.shape[0]
    n_cores = _n_cores

    wab_t = np.ascontiguousarray(np.concatenate([np.asarray(L0_attn_Wa, np.float32),
                                                 np.asarray(L0_attn_Wb, np.float32)], axis=0).T)
    ab_bias = np.ascontiguousarray(np.stack([np.asarray(L0_attn_ba, np.float32),
                                             np.asarray(L0_attn_bb, np.float32)], axis=1))
    wout_t = np.ascontiguousarray(np.asarray(L0_attn_Wout, np.float32).T)
    bout_row = np.ascontiguousarray(np.asarray(L0_attn_bout, np.float32)[None, :])
    wqg_t = np.ascontiguousarray(np.asarray(L0_ffn_wq_gate, np.int32).T)
    wqu_t = np.ascontiguousarray(np.asarray(L0_ffn_wq_up, np.int32).T)
    wqd_t = np.ascontiguousarray(np.asarray(L0_ffn_wq_down, np.int32).T)
    sg_t = np.ascontiguousarray(np.asarray(L0_ffn_s_gate, np.float32).reshape(HC, 128).T)
    su_t = np.ascontiguousarray(np.asarray(L0_ffn_s_up, np.float32).reshape(HC, 128).T)
    sd_row = np.ascontiguousarray(np.asarray(L0_ffn_s_down, np.float32)[None, :])
    ones_r = np.ones((1, 128), np.float32)

    nc = build_nc(cfg)
    in_maps = []
    for c in range(n_cores):
        b, half = divmod(c, 2)
        in_maps.append(dict(
            x_seq=np.ascontiguousarray(x[b]),
            x_own=np.ascontiguousarray(x[b, half * TOWN:(half + 1) * TOWN]),
            sel0=np.full((128, 1), 1.0 - half, np.float32),
            sel1=np.full((128, 1), float(half), np.float32),
            wab_t=wab_t, ab_bias=ab_bias, wout_t=wout_t, bout_row=bout_row,
            wqg_t=wqg_t, wqu_t=wqu_t, wqd_t=wqd_t, sg_t=sg_t, su_t=su_t, sd_row=sd_row,
            ones_r=ones_r,
        ))
    res = run_bass_kernel_spmd(nc, in_maps, core_ids=list(range(n_cores)))
    out = np.concatenate([res.results[c]["y_out"] for c in range(n_cores)], axis=0)
    return out.reshape(B, TSEQ, D)
